# version 02
# baseline (speedup 1.0000x reference)
import numpy as np
import ml_dtypes
import concourse.bass as bass
import concourse.mybir as mybir
from concourse.bass_utils import run_bass_kernel_spmd

F32 = mybir.dt.float32
BF16 = mybir.dt.bfloat16
U32 = mybir.dt.uint32
AF = mybir.ActivationFunctionType
ALU = mybir.AluOpType
AX = mybir.AxisListType
BF = ml_dtypes.bfloat16

HD = 64
HALF = 32
S = 512


class CFG:
    def __init__(self, T=2048, D=1024, L=256, HID=4096, E=8, HPG=4, B=2):
        self.T, self.D, self.L, self.HID, self.E, self.HPG, self.B = T, D, L, HID, E, HPG, B
        self.HEADS = D // HD
        self.NG = self.HEADS // HPG
        self.DC = D // 128
        self.TC = T // 128
        self.NS = T // S
        self.LC = L // 128
        self.HJ = HID // 128
        self.UJG = min(8, self.HJ)
        self.NJG = self.HJ // self.UJG
        self.NH = max(1, T // 1024)
        self.HSZ = T // self.NH
        self.CPG = HPG * HD // 128
        assert D % 128 == 0 and T % S == 0 and L % 128 == 0
        assert HID % (128 * self.UJG) == 0
        assert self.HEADS % HPG == 0 and HPG % 2 == 0 and self.CPG >= 1
        assert HPG * HD <= 512


REAL = CFG()


def _swap_cols(w, heads):
    out = np.empty_like(w)
    for h in range(heads):
        b = h * HD
        out[:, b:b + HALF] = w[:, b + HALF:b + HD]
        out[:, b + HALF:b + HD] = w[:, b:b + HALF]
    return out


def host_consts(cfg):
    T = cfg.T
    pos = np.arange(T, dtype=np.float64)[:, None]
    inv = 1.0 / 10000.0 ** (np.arange(HALF, dtype=np.float64) / HALF)
    ang = pos * inv[None, :]
    cosv = np.cos(ang).astype(np.float32).T
    sinv = np.sin(ang).astype(np.float32).T
    cos2 = np.concatenate([cosv, cosv], 0)
    sin2 = np.concatenate([-sinv, sinv], 0)
    masks = np.zeros((4, 128, S), BF)
    qf = np.arange(S)[None, :]
    kp = np.arange(128)[:, None]
    for off in range(4):
        masks[off] = ((kp + off * 128) <= qf).astype(BF)
    iota8 = np.broadcast_to(np.arange(cfg.E, dtype=np.float32), (128, cfg.E)).copy()
    ident = np.eye(128, dtype=np.float32)
    ones128 = np.ones((128, 128), BF)
    rs_sel = np.zeros((8, cfg.E * 128), BF)
    for e in range(cfg.E):
        rs_sel[e, e * 128:(e + 1) * 128] = 1
    epsc = np.full((128, 1), 1e-5, np.float32)
    return dict(cos2t=cos2.astype(BF), sin2st=sin2.astype(BF), masks=masks,
                iota8=iota8, identf=ident, ones128=ones128, rs_sel=rs_sel, epsc=epsc)


def host_prep(cfg, iv):
    E, L, HID = cfg.E, cfg.L, cfg.HID
    d = dict(host_consts(cfg))
    d["wq"] = np.ascontiguousarray(iv["L0_attn_W_q"], dtype=np.float32)
    d["wqsw"] = np.ascontiguousarray(_swap_cols(d["wq"], cfg.HEADS))
    d["wdkv"] = np.ascontiguousarray(iv["L0_attn_W_dkv"], dtype=np.float32)
    d["wuk"] = np.ascontiguousarray(iv["L0_attn_W_uk"], dtype=np.float32)
    d["wuksw"] = np.ascontiguousarray(_swap_cols(d["wuk"], cfg.HEADS))
    d["wuv"] = np.ascontiguousarray(iv["L0_attn_W_uv"], dtype=np.float32)
    d["wo"] = np.ascontiguousarray(iv["L0_attn_W_o"], dtype=np.float32)
    lng = np.broadcast_to(np.asarray(iv["L0_attn_ln_gamma"], np.float32), (128, L))
    lnb = np.broadcast_to(np.asarray(iv["L0_attn_ln_beta"], np.float32), (128, L))
    d["lngb"] = np.ascontiguousarray(np.stack([lng, lnb]))
    d["rw"] = np.ascontiguousarray(iv["L0_ffn_router_w"], dtype=np.float32)
    d["rbt"] = np.ascontiguousarray(
        np.broadcast_to(np.asarray(iv["L0_ffn_router_b"], np.float32), (128, E)))
    d["w1"] = np.ascontiguousarray(iv["L0_ffn_w1"], dtype=np.float32)
    d["w2"] = np.ascontiguousarray(iv["L0_ffn_w2"], dtype=np.float32)
    b1 = np.asarray(iv["L0_ffn_b1"], np.float32)
    d["b1a"] = np.ascontiguousarray(b1[:, :HID].reshape(E, cfg.HJ, 128).transpose(0, 2, 1))
    d["b1g"] = np.ascontiguousarray(b1[:, HID:].reshape(E, cfg.HJ, 128).transpose(0, 2, 1))
    d["b2"] = np.ascontiguousarray(iv["L0_ffn_b2"], dtype=np.float32)
    return d


class Sem:
    __slots__ = ("h", "n", "name")
    def __init__(self, h, name):
        self.h, self.n, self.name = h, 0, name


class P:
    def __init__(self):
        self.seen = {}
        self.slot = {}
        self.bufc = {}

    def inc(self, instr, sem, n=1):
        instr.then_inc(sem.h, n)
        sem.n += n
        return sem.n

    def wait(self, eng, ename, sem, val):
        if val <= 0:
            return
        key = (ename, sem.name)
        if self.seen.get(key, 0) >= val:
            return
        eng.wait_ge(sem.h, val)
        self.seen[key] = val

    def _w(self, eng, ename, d):
        for sem, tok in d.values():
            self.wait(eng, ename, sem, tok)

    def ps_acquire(self, teng, slots):
        for s_ in slots:
            self._w(teng, "t", self.slot.get(s_, {}))

    def ps_release(self, slots, sem, tok):
        for s_ in slots:
            self.slot.setdefault(s_, {})[sem.name] = (sem, tok)

    def buf_wait(self, eng, ename, name):
        self._w(eng, ename, self.bufc.get(name, {}))

    def buf_use(self, name, sem, tok):
        self.bufc.setdefault(name, {})[sem.name] = (sem, tok)


def build_program(cfg):
    from contextlib import ExitStack
    nc = bass.Bass("TRN2", target_bir_lowering=False)
    T, D, L, HID, E, B = cfg.T, cfg.D, cfg.L, cfg.HID, cfg.E, cfg.B
    DC, TC, NS, LC, HJ = cfg.DC, cfg.TC, cfg.NS, cfg.LC, cfg.HJ
    NG, HPG, CPG, UJG, NJG = cfg.NG, cfg.HPG, cfg.CPG, cfg.UJG, cfg.NJG
    NH, HSZ = cfg.NH, cfg.HSZ
    HC = HPG // 2
    VW = HPG * HD

    def din(name, shape, dty=F32):
        return nc.dram_tensor(name, list(shape), dty, kind="ExternalInput").ap()

    x_h = din("x", (B, T, D))
    wq_h = din("wq", (D, D)); wqsw_h = din("wqsw", (D, D))
    wdkv_h = din("wdkv", (D, L))
    wuk_h = din("wuk", (L, D)); wuksw_h = din("wuksw", (L, D))
    wuv_h = din("wuv", (L, D)); wo_h = din("wo", (D, D))
    lngb_h = din("lngb", (2, 128, L))
    rw_h = din("rw", (D, E)); rbt_h = din("rbt", (128, E))
    w1_h = din("w1", (E, D, 2 * HID)); w2_h = din("w2", (E, HID, D))
    b1a_h = din("b1a", (E, 128, HJ)); b1g_h = din("b1g", (E, 128, HJ))
    b2_h = din("b2", (E, D))
    cos_h = din("cos2t", (64, T), BF16); sin_h = din("sin2st", (64, T), BF16)
    masks_h = din("masks", (4, 128, S), BF16)
    iota_h = din("iota8", (128, E)); identf_h = din("identf", (128, 128))
    ones_h = din("ones128", (128, 128), BF16)
    rssel_h = din("rs_sel", (8, E * 128), BF16)
    epsc_h = din("epsc", (128, 1))
    out_h = nc.dram_tensor("out", [B, T, D], F32, kind="ExternalOutput").ap()

    with ExitStack() as _ctx:
        def _sb(*a):
            return _ctx.enter_context(nc.sbuf_tensor(*a))
        def _ps(*a):
            return _ctx.enter_context(nc.psum_tensor(*a))
        def _sem(n):
            return _ctx.enter_context(nc.semaphore(n))
        xnT = _sb("g_xnT", [128, DC, T], BF16)
        acc = _sb("g_acc", [128, DC, T], F32)
        ohT = _sb("g_ohT", [8, T], BF16)
        iota8 = _sb("g_iota8", [128, E], F32)
        identf = _sb("g_identf", [128, 128], F32)
        ones128 = _sb("g_ones128", [128, 128], BF16)
        rssel = _sb("g_rssel", [8, E * 128], BF16)
        rbt = _sb("g_rbt", [128, E], F32)
        lngb = _sb("g_lngb", [128, 2, L], F32)
        b1a = _sb("g_b1a", [128, E, HJ], F32)
        b1g = _sb("g_b1g", [128, E, HJ], F32)
        rwbf = _sb("g_rwbf", [128, DC, E], BF16)
        b2bf = _sb("g_b2bf", [8, D], BF16)
        st_a = _sb("g_st_a", [128, DC, 128], F32)
        st_2 = _sb("g_st_2", [128, D], F32)
        bf_a = _sb("g_bf_a", [128, 2, DC, 128], BF16)
        bf_g = _sb("g_bf_g", [128, 2, DC, 128], BF16)
        xcb = _sb("g_xcb", [128, 2, D], F32)
        stat = _sb("g_stat", [128, 12], F32)
        lgb = _sb("g_lgb", [128, E], F32)
        mx8 = _sb("g_mx8", [128, 8], F32)
        idx8 = _sb("g_idx8", [128, 8], U32)
        ohf = _sb("g_ohf", [128, E], F32)
        tiny = _sb("g_tiny", [1, 16], F32)
        tinyb = _sb("g_tinyb", [1, 2], BF16)
        negc = _sb("g_negc", [128, 1], F32)
        epsc = _sb("g_epsc", [128, 1], F32)
        psA = _ps("ps_A", [128, 2048], F32)
        psB = _ps("ps_B", [128, 2048], F32)
        h_ld = _sem("sldc"); h_pe = _sem("spe")
        h_dv = _sem("sdv"); h_ac = _sem("sac"); h_gp = _sem("sgp")
        h_la = _sem("slda"); h_l2 = _sem("sld2")
        h_lx0 = _sem("sldx0"); h_lx1 = _sem("sldx1")
        h_ld0 = _sem("sldd0"); h_ld1 = _sem("sldd1")
        h_st0 = _sem("sst0"); h_st1 = _sem("sst1")
        block = _ctx.enter_context(nc.Block())

        p = P()
        sld = Sem(h_ld, "ldc"); spe = Sem(h_pe, "pe")
        sdv = Sem(h_dv, "dv"); sac = Sem(h_ac, "ac"); sgp = Sem(h_gp, "gp")
        slda = Sem(h_la, "lda"); sld2 = Sem(h_l2, "ld2")
        sldx = [Sem(h_lx0, "ldx0"), Sem(h_lx1, "ldx1")]
        sldd = [Sem(h_ld0, "ldd0"), Sem(h_ld1, "ldd1")]
        sstb = [Sem(h_st0, "st0"), Sem(h_st1, "st1")]
        sq, t, v, sc, g = nc.sync, nc.tensor, nc.vector, nc.scalar, nc.gpsimd
        A = [psA[:, i * 512:(i + 1) * 512] for i in range(4)]
        Bp = [psB[:, i * 512:(i + 1) * 512] for i in range(4)]
        ALL_SEMS = (sld, spe, sdv, sac, sgp, slda, sld2, sldx[0], sldx[1],
                    sldd[0], sldd[1], sstb[0], sstb[1])
        ENGS = ((sq, "sq"), (t, "t"), (v, "v"), (sc, "sc"), (g, "g"))

        def ld(dst, src, sem=None):
            sem = sem or sld
            return p.inc(sq.dma_start(out=dst, in_=src), sem, 16)

        def mmacc(out, pairs, slots, extra=()):
            p.ps_acquire(t, slots)
            for (sem, tok) in extra:
                p.wait(t, "t", sem, tok)
            n = len(pairs)
            last = None
            for i_, (l_, r_) in enumerate(pairs):
                last = t.matmul(out, l_, r_, start=(i_ == 0), stop=(i_ == n - 1))
            return last

        def barrier():
            for eng, en in ENGS:
                for sem in ALL_SEMS:
                    p.wait(eng, en, sem, sem.n)

        def stage_cast(dst, src_h, stage, stage_name, ceng, cename, csem, dst_name=None):
            lsem = {"st_a": slda, "st_2": sld2}[stage_name]
            p.buf_wait(sq, "sq", stage_name)
            tok = ld(stage, src_h, lsem)
            p.wait(ceng, cename, lsem, tok)
            if dst_name:
                p.buf_wait(ceng, cename, dst_name)
            if ceng is sc:
                tc = p.inc(sc.copy(dst, stage), csem, 1)
            elif ceng is v:
                tc = p.inc(v.tensor_copy(dst, stage), csem, 1)
            else:
                tc = p.inc(g.tensor_copy(dst, stage), csem, 1)
            p.buf_use(stage_name, csem, tc)
            return tc

        ld(iota8[:], iota_h); ld(identf[:], identf_h)
        ld(ones128[:], ones_h); ld(rssel[:], rssel_h)
        ld(rbt[:], rbt_h)
        ld(epsc[:], epsc_h)
        ld(lngb[:], lngb_h.rearrange("o p q -> p o q"))
        ld(b1a[:], b1a_h.rearrange("e p j -> p e j"))
        ld(b1g[:], b1g_h.rearrange("e p j -> p e j"))
        stage_cast(rwbf[:], rw_h.rearrange("(c p) e -> p c e", p=128),
                   st_a[:, :, 0:E], "st_a", g, "g", sgp)
        stage_cast(b2bf[:], b2_h, st_2[0:8, :], "st_2", g, "g", sgp)
        barrier()

        for b in range(B):
            with ExitStack() as actx:
                dumpA = actx.enter_context(nc.sbuf_tensor("a%d_dump" % b, [128, D], F32))
                for i in range(TC):
                    bi = i % 2
                    xn_ = "xcb%d" % bi
                    p.buf_wait(sq, "sq", xn_)
                    tokx = ld(xcb[:, bi], x_h[b, i * 128:(i + 1) * 128, :], sldx[bi])
                    p.wait(v, "v", sldx[bi], tokx)
                    v.reduce_sum(stat[:, 0:1], xcb[:, bi], axis=AX.X)
                    v.drain()
                    v.tensor_scalar(stat[:, 1:2], stat[:, 0:1], 1.0 / D, None, op0=ALU.mult)
                    v.drain()
                    tv = p.inc(v.tensor_scalar(xcb[:, bi], xcb[:, bi], stat[:, 1:2], None,
                                               op0=ALU.subtract), sdv, 1)
                    p.wait(sc, "sc", sdv, tv)
                    p.buf_wait(sc, "sc", "dumpA")
                    sc.activation(dumpA[:], xcb[:, bi], AF.Square, accum_out=stat[:, 2:3])
                    sc.drain()
                    sc.activation(stat[:, 3:4], stat[:, 2:3], AF.Ln, bias=epsc[:], scale=1.0 / D)
                    sc.drain()
                    ta = p.inc(sc.activation(stat[:, 4:5], stat[:, 3:4], AF.Exp, scale=-0.5), sac, 1)
                    p.buf_use("dumpA", sac, ta)
                    p.wait(v, "v", sac, ta)
                    v.drain()
                    tv2 = p.inc(v.tensor_scalar(xcb[:, bi], xcb[:, bi], stat[:, 4:5], None,
                                                op0=ALU.mult), sdv, 1)
                    p.ps_acquire(t, ["A0", "A1", "A2", "A3"])
                    p.wait(t, "t", sdv, tv2)
                    ti = None
                    for c in range(DC):
                        ti = t.transpose(psA[:, c * 256:c * 256 + 128],
                                         xcb[:, bi, c * 128:(c + 1) * 128], identf[:])
                    tpe = p.inc(ti, spe, 1)
                    p.wait(v, "v", spe, tpe)
                    last = None
                    for c in range(DC):
                        last = v.tensor_copy(xnT[:, c, i * 128:(i + 1) * 128],
                                             psA[:, c * 256:c * 256 + 128])
                    tv3 = p.inc(last, sdv, 1)
                    p.buf_use(xn_, sdv, tv3)
                    p.ps_release(["A0", "A1", "A2", "A3"], sdv, tv3)
                    last = mmacc(psB[:, 0:E],
                                 [(xnT[:, c, i * 128:(i + 1) * 128], rwbf[:, c, :]) for c in range(DC)],
                                 ["B0"], extra=[(sdv, tv3)])
                    tpe2 = p.inc(last, spe, 1)
                    p.wait(v, "v", spe, tpe2)
                    v.tensor_tensor(lgb[:], psB[:, 0:E], rbt[:], op=ALU.add)
                    v.drain()
                    v.max(mx8[:], lgb[:])
                    v.drain()
                    v.max_index(idx8[:], mx8[:], lgb[:])
                    v.drain()
                    v.tensor_copy(stat[:, 5:6], idx8[:, 0:1])
                    v.drain()
                    tv4 = p.inc(v.tensor_scalar(ohf[:], iota8[:], stat[:, 5:6], None,
                                                op0=ALU.is_equal), sdv, 1)
                    p.ps_release(["B0"], sdv, tv4)
                    p.ps_acquire(t, ["B1"])
                    p.wait(t, "t", sdv, tv4)
                    ti = t.transpose(psB[0:E, 512:640], ohf[:], identf[:])
                    tpe3 = p.inc(ti, spe, 1)
                    p.wait(v, "v", spe, tpe3)
                    tv5 = p.inc(v.tensor_copy(ohT[:, i * 128:(i + 1) * 128], psB[0:E, 512:640]), sdv, 1)
                    p.ps_release(["B1"], sdv, tv5)
            oh_v = sdv.n
            barrier()

            with ExitStack() as bctx:
                def _bb(nm, shape, dty):
                    return bctx.enter_context(nc.sbuf_tensor("b%d_%s" % (b, nm), shape, dty))
                wdkvbf = _bb("wdkv", [128, DC, L], BF16)
                wukbf = _bb("wuk", [128, LC, D], BF16)
                wukswbf = _bb("wuksw", [128, LC, D], BF16)
                wuvbf = _bb("wuv", [128, LC, D], BF16)
                latnT = _bb("latnT", [128, LC, T], BF16)
                kroT = _bb("kroT", [128, HC, T], BF16)
                qroT = _bb("qroT", [128, HC, T], BF16)
                v65 = _bb("v65", [128, TC, HPG, 65], BF16)
                catT = _bb("catT", [128, CPG, S], BF16)
                ropet = _bb("ropet", [128, 2, S], F32)
                attnT = _bb("attnT", [128, 2, S], BF16)
                recipb = _bb("recipb", [128, S], BF16)
                recipf = _bb("recipf", [128, S], F32)
                bc_sb = _bb("bc_sb", [64, S], BF16)
                masksb = _bb("masksb", [128, 4, S], BF16)
                cos2t = _bb("cos2t", [64, T], BF16)
                sin2st = _bb("sin2st", [64, T], BF16)
                ld(masksb[:], masks_h.rearrange("o p q -> p o q"))
                ld(cos2t[:], cos_h)
                ld(sin2st[:], sin_h)
                for lc2 in range(L // 128):
                    stage_cast(wdkvbf[:, :, lc2 * 128:(lc2 + 1) * 128],
                               wdkv_h[:, lc2 * 128:(lc2 + 1) * 128]
                               .rearrange("(c p) l -> p c l", p=128),
                               st_a[:], "st_a", g, "g", sgp)
                for (hsrc, dst) in ((wuk_h, wukbf), (wuksw_h, wukswbf), (wuv_h, wuvbf)):
                    for lc in range(LC):
                        stage_cast(dst[:, lc, :], hsrc[lc * 128:(lc + 1) * 128, :],
                                   st_2[:], "st_2", g, "g", sgp)
                b_ld = sld.n
                v.memset(v65[:, :, :, 64:65], 1.0)
                p.inc(v.drain(), sdv, 1)

                for i in range(TC):
                    last = mmacc(psB[:, 1024:1280],
                                 [(xnT[:, c, i * 128:(i + 1) * 128], wdkvbf[:, c, :]) for c in range(DC)],
                                 ["B2"], extra=[(sgp, sgp.n)])
                    tpe = p.inc(last, spe, 1)
                    p.wait(v, "v", spe, tpe)
                    p.buf_wait(v, "v", "xcb0")
                    v.reduce_sum(stat[:, 6:7], psB[:, 1024:1280], axis=AX.X)
                    v.drain()
                    v.tensor_scalar(stat[:, 7:8], stat[:, 6:7], 1.0 / L, None, op0=ALU.mult)
                    v.drain()
                    tv = p.inc(v.tensor_scalar(xcb[:, 0, 0:L], psB[:, 1024:1280], stat[:, 7:8], None,
                                               op0=ALU.subtract), sdv, 1)
                    p.ps_release(["B2"], sdv, tv)
                    p.wait(sc, "sc", sdv, tv)
                    sc.activation(ropet[:, 0, 0:L], xcb[:, 0, 0:L], AF.Square, accum_out=stat[:, 8:9])
                    sc.drain()
                    sc.activation(stat[:, 9:10], stat[:, 8:9], AF.Ln, bias=epsc[:], scale=1.0 / L)
                    sc.drain()
                    ta = p.inc(sc.activation(stat[:, 10:11], stat[:, 9:10], AF.Exp, scale=-0.5), sac, 1)
                    p.wait(v, "v", sac, ta)
                    v.drain()
                    v.tensor_scalar(xcb[:, 1, 0:L], xcb[:, 0, 0:L], stat[:, 10:11], None, op0=ALU.mult)
                    v.drain()
                    v.tensor_tensor(xcb[:, 0, 0:L], xcb[:, 1, 0:L], lngb[:, 0, :], op=ALU.mult)
                    v.drain()
                    tv2 = p.inc(v.tensor_tensor(xcb[:, 1, 0:L], xcb[:, 0, 0:L], lngb[:, 1, :],
                                                op=ALU.add), sdv, 1)
                    p.buf_use("xcb0", sdv, tv2)
                    p.buf_use("xcb1", sdv, tv2)
                    p.ps_acquire(t, ["B3"])
                    p.wait(t, "t", sdv, tv2)
                    ti = None
                    for lc in range(LC):
                        ti = t.transpose(psB[:, 1536 + lc * 128:1536 + (lc + 1) * 128],
                                         xcb[:, 1, lc * 128:(lc + 1) * 128], identf[:])
                    tpe2 = p.inc(ti, spe, 1)
                    p.wait(v, "v", spe, tpe2)
                    last = None
                    for lc in range(LC):
                        last = v.tensor_copy(latnT[:, lc, i * 128:(i + 1) * 128],
                                             psB[:, 1536 + lc * 128:1536 + (lc + 1) * 128])
                    tv3 = p.inc(last, sdv, 1)
                    p.ps_release(["B3"], sdv, tv3)
                latn_v = sdv.n

                for grp in range(NG):
                    h0 = grp * HPG
                    for cc in range(HC):
                        c0 = (h0 + 2 * cc) * HD
                        bufq = cc % 2
                        tca = stage_cast(bf_a[:, bufq],
                                         wq_h[:, c0:c0 + 128].rearrange("(c p) m -> p c m", p=128),
                                         st_a[:], "st_a",
                                         g, "g", sgp, dst_name="bf_a%d" % bufq)
                        tcg = stage_cast(bf_g[:, bufq],
                                         wqsw_h[:, c0:c0 + 128].rearrange("(c p) m -> p c m", p=128),
                                         st_2[:].rearrange("p (c m) -> p c m", m=128), "st_2",
                                         g, "g", sgp, dst_name="bf_g%d" % bufq)
                        for par in range(2):
                            hh = h0 + 2 * cc + par
                            for s_ in range(NS):
                                sl_k = "A%d" % par; sl_s = "A%d" % (2 + par)
                                mmacc(A[par][0:64, :],
                                      [(wukbf[:, lc, hh * HD:(hh + 1) * HD],
                                        latnT[:, lc, s_ * S:(s_ + 1) * S]) for lc in range(LC)],
                                      [sl_k], extra=[(sdv, latn_v)])
                                last = mmacc(A[2 + par][0:64, :],
                                             [(wukswbf[:, lc, hh * HD:(hh + 1) * HD],
                                               latnT[:, lc, s_ * S:(s_ + 1) * S]) for lc in range(LC)],
                                             [sl_s])
                                tpe = p.inc(last, spe, 1)
                                p.wait(v, "v", spe, tpe)
                                p.wait(v, "v", sld, b_ld)
                                p.buf_wait(v, "v", "kroT")
                                v.drain()
                                v.tensor_tensor(ropet[0:64, 0, :], A[par][0:64, :],
                                                cos2t[:, s_ * S:(s_ + 1) * S], op=ALU.mult)
                                v.tensor_tensor(ropet[0:64, 1, :], A[2 + par][0:64, :],
                                                sin2st[:, s_ * S:(s_ + 1) * S], op=ALU.mult)
                                v.drain()
                                tv = p.inc(v.tensor_tensor(
                                    kroT[par * 64:par * 64 + 64, cc, s_ * S:(s_ + 1) * S],
                                    ropet[0:64, 0, :], ropet[0:64, 1, :], op=ALU.add), sdv, 1)
                                p.ps_release([sl_k, sl_s], sdv, tv)
                                mmacc(A[par][0:64, :],
                                      [(bf_a[:, bufq, c_, par * 64:par * 64 + 64],
                                        xnT[:, c_, s_ * S:(s_ + 1) * S]) for c_ in range(DC)],
                                      [sl_k], extra=[(sgp, max(tca, tcg))])
                                last = mmacc(A[2 + par][0:64, :],
                                             [(bf_g[:, bufq, c_, par * 64:par * 64 + 64],
                                               xnT[:, c_, s_ * S:(s_ + 1) * S]) for c_ in range(DC)],
                                             [sl_s])
                                tpe = p.inc(last, spe, 1)
                                p.buf_use("bf_a%d" % bufq, spe, tpe)
                                p.buf_use("bf_g%d" % bufq, spe, tpe)
                                p.wait(v, "v", spe, tpe)
                                p.buf_wait(v, "v", "qroT")
                                v.drain()
                                v.tensor_tensor(ropet[0:64, 0, :], A[par][0:64, :],
                                                cos2t[:, s_ * S:(s_ + 1) * S], op=ALU.mult)
                                v.tensor_tensor(ropet[0:64, 1, :], A[2 + par][0:64, :],
                                                sin2st[:, s_ * S:(s_ + 1) * S], op=ALU.mult)
                                v.drain()
                                tv = p.inc(v.tensor_tensor(
                                    qroT[par * 64:par * 64 + 64, cc, s_ * S:(s_ + 1) * S],
                                    ropet[0:64, 0, :], ropet[0:64, 1, :], op=ALU.add), sdv, 1)
                                p.ps_release([sl_k, sl_s], sdv, tv)
                    kq_v = sdv.n

                    for i in range(TC):
                        p.buf_wait(v, "v", "v65")
                        last = mmacc(A[0][:, 0:VW],
                                     [(latnT[:, lc, i * 128:(i + 1) * 128],
                                       wuvbf[:, lc, h0 * HD:h0 * HD + VW]) for lc in range(LC)],
                                     ["A0"])
                        tpe = p.inc(last, spe, 1)
                        p.wait(v, "v", spe, tpe)
                        tv = p.inc(v.tensor_copy(v65[:, i, :, 0:64],
                                                 A[0][:, 0:VW].rearrange("p (h d) -> p h d", d=64)),
                                   sdv, 1)
                        p.ps_release(["A0"], sdv, tv)
                    v65_v = sdv.n

                    first = True
                    for s_ in range(NS):
                        for par in range(2):
                            for cc in range(HC):
                                p.buf_wait(sc, "sc", "attn1")
                                p.wait(sc, "sc", sdv, kq_v)
                                tq = p.inc(sc.activation(attnT[par * 64:par * 64 + 64, 1, :],
                                                         kroT[par * 64:par * 64 + 64, cc,
                                                              s_ * S:(s_ + 1) * S],
                                                         AF.Square), sac, 1)
                                last = mmacc(Bp[par][0:1, :],
                                             [(ones128[par * 64:par * 64 + 64, 0:1],
                                               attnT[par * 64:par * 64 + 64, 1, :])],
                                             ["B%d" % par], extra=[(sac, tq)])
                                tpe = p.inc(last, spe, 1)
                                p.buf_use("attn1", spe, tpe)
                                p.wait(v, "v", spe, tpe)
                                v.reduce_max(tiny[0:1, 1:2], Bp[par][0:1, :], axis=AX.X)
                                v.drain()
                                if first:
                                    v.tensor_copy(tiny[0:1, 0:1], tiny[0:1, 1:2])
                                    first = False
                                else:
                                    v.tensor_tensor(tiny[0:1, 0:1], tiny[0:1, 0:1], tiny[0:1, 1:2],
                                                    op=ALU.max)
                                tvm = p.inc(v.drain(), sdv, 1)
                                p.ps_release(["B%d" % par], sdv, tvm)
                    mk_v = sdv.n

                    for s_ in range(NS):
                        first = True
                        for par in range(2):
                            for cc in range(HC):
                                p.buf_wait(sc, "sc", "attn1")
                                tq = p.inc(sc.activation(attnT[par * 64:par * 64 + 64, 1, :],
                                                         qroT[par * 64:par * 64 + 64, cc,
                                                              s_ * S:(s_ + 1) * S],
                                                         AF.Square), sac, 1)
                                last = mmacc(Bp[par][0:1, :],
                                             [(ones128[par * 64:par * 64 + 64, 0:1],
                                               attnT[par * 64:par * 64 + 64, 1, :])],
                                             ["B%d" % par], extra=[(sac, tq)])
                                tpe = p.inc(last, spe, 1)
                                p.buf_use("attn1", spe, tpe)
                                p.wait(v, "v", spe, tpe)
                                v.reduce_max(tiny[0:1, 2:3], Bp[par][0:1, :], axis=AX.X)
                                v.drain()
                                if first:
                                    v.tensor_copy(tiny[0:1, 3:4], tiny[0:1, 2:3])
                                    first = False
                                else:
                                    v.tensor_tensor(tiny[0:1, 3:4], tiny[0:1, 3:4], tiny[0:1, 2:3],
                                                    op=ALU.max)
                                tvq = p.inc(v.drain(), sdv, 1)
                                p.ps_release(["B%d" % par], sdv, tvq)
                        p.wait(v, "v", sdv, mk_v)
                        tv = p.inc(v.tensor_tensor(tiny[0:1, 4:5], tiny[0:1, 3:4], tiny[0:1, 0:1],
                                                   op=ALU.mult), sdv, 1)
                        p.wait(sc, "sc", sdv, tv)
                        sc.activation(tiny[0:1, 5:6], tiny[0:1, 4:5], AF.Ln)
                        sc.drain()
                        ta = p.inc(sc.activation(tiny[0:1, 6:7], tiny[0:1, 5:6], AF.Exp, scale=0.5),
                                   sac, 1)
                        p.wait(v, "v", sac, ta)
                        v.tensor_scalar(tiny[0:1, 7:8], tiny[0:1, 6:7], -0.125, -2.0,
                                        op0=ALU.mult, op1=ALU.add)
                        v.drain()
                        tv = p.inc(v.tensor_copy(tinyb[0:1, 0:1], tiny[0:1, 7:8]), sdv, 1)
                        p.buf_wait(v, "v", "negc")
                        last = mmacc(psB[:, 1024:1025], [(ones128[0:1, :], tinyb[0:1, 0:1])],
                                     ["B2"], extra=[(sdv, tv)])
                        tpe = p.inc(last, spe, 1)
                        p.wait(v, "v", spe, tpe)
                        tnegc = p.inc(v.tensor_copy(negc[:, 0:1], psB[:, 1024:1025]), sdv, 1)
                        p.ps_release(["B2"], sdv, tnegc)

                        nkt = 4 * s_ + 4
                        for lh in range(HPG):
                            cc = lh // 2; par = lh % 2
                            ob = lh % 2
                            o65 = Bp[ob][0:65, :]
                            p.ps_acquire(t, ["B%d" % ob])
                            for kt in range(nkt):
                                sb_ = kt % 2
                                stp = A[sb_]
                                abuf = attnT[:, sb_, :]
                                p.ps_acquire(t, ["A%d" % sb_])
                                p.wait(t, "t", sdv, kq_v)
                                tmm = t.matmul(stp,
                                               kroT[par * 64:par * 64 + 64, cc, kt * 128:(kt + 1) * 128],
                                               qroT[par * 64:par * 64 + 64, cc, s_ * S:(s_ + 1) * S],
                                               start=True, stop=True)
                                tpe = p.inc(tmm, spe, 1)
                                p.buf_wait(sc, "sc", "attn%d" % sb_)
                                p.wait(sc, "sc", sdv, tnegc)
                                p.wait(sc, "sc", spe, tpe)
                                tex = p.inc(sc.activation(abuf, stp, AF.Exp,
                                                          bias=negc[:, 0:1], scale=0.125), sac, 1)
                                p.buf_use("negc", sac, tex)
                                p.ps_release(["A%d" % sb_], sac, tex)
                                off = kt - 4 * s_
                                if off >= 0:
                                    p.wait(v, "v", sac, tex)
                                    tex2 = p.inc(v.tensor_tensor(abuf, abuf, masksb[:, off, :],
                                                                 op=ALU.mult), sdv, 1)
                                    p.wait(t, "t", sdv, tex2)
                                else:
                                    p.wait(t, "t", sac, tex)
                                p.wait(t, "t", sdv, v65_v)
                                last = t.matmul(o65, v65[:, kt, lh, :], abuf,
                                                start=(kt == 0), stop=(kt == nkt - 1))
                                tpe2 = p.inc(last, spe, 1)
                                p.buf_use("attn%d" % sb_, spe, tpe2)
                                p.buf_use("v65", spe, tpe2)
                            p.wait(v, "v", spe, spe.n)
                            p.buf_wait(v, "v", "recipb")
                            v.reciprocal(recipf[64:65, :], o65[64:65, :])
                            v.drain()
                            v.tensor_copy(recipb[64:65, :], recipf[64:65, :])
                            trc = p.inc(v.drain(), sdv, 1)
                            last = mmacc(psB[0:64, 1024:1536],
                                         [(ones128[64:65, 0:64], recipb[64:65, :])],
                                         ["B2"], extra=[(sdv, trc)])
                            tpe3 = p.inc(last, spe, 1)
                            p.buf_use("recipb", spe, tpe3)
                            p.wait(v, "v", spe, tpe3)
                            v.tensor_copy(bc_sb[:], psB[0:64, 1024:1536])
                            v.drain()
                            p.buf_wait(v, "v", "catT")
                            tv = p.inc(v.tensor_tensor(catT[par * 64:par * 64 + 64, cc, :],
                                                       o65[0:64, :], bc_sb[:], op=ALU.mult), sdv, 1)
                            p.ps_release(["B2"], sdv, tv)
                            p.ps_release(["B%d" % ob], sdv, tv)
                        cat_v = sdv.n

                        wo_tgt = [("bf_a0", bf_a[:, 0]), ("bf_g0", bf_g[:, 0])]
                        for ccw in range(CPG):
                            r0 = h0 * HD + ccw * 128
                            nm, tgt = wo_tgt[ccw % 2]
                            stage_cast(tgt.rearrange("p c m -> p (c m)"), wo_h[r0:r0 + 128, :],
                                       st_2[:], "st_2", g, "g", sgp, dst_name=nm)
                        wo_g = sgp.n
                        for dch in range(DC):
                            wsl = "A%d" % (2 + dch % 2)
                            wop = A[2 + dch % 2]
                            last = mmacc(wop,
                                         [(wo_tgt[ccw % 2][1][:, dch, :], catT[:, ccw, :])
                                          for ccw in range(CPG)],
                                         [wsl], extra=[(sgp, wo_g), (sdv, cat_v)])
                            tpe = p.inc(last, spe, 1)
                            p.buf_use("bf_a0", spe, tpe)
                            p.buf_use("bf_g0", spe, tpe)
                            p.buf_use("catT", spe, tpe)
                            p.wait(v, "v", spe, tpe)
                            dst = acc[:, dch, s_ * S:(s_ + 1) * S]
                            if grp == 0:
                                tv = p.inc(v.tensor_copy(dst, wop), sdv, 1)
                            else:
                                tv = p.inc(v.tensor_tensor(dst, dst, wop, op=ALU.add), sdv, 1)
                            p.ps_release([wsl], sdv, tv)
            barrier()

            with ExitStack() as cctx:
                def _cb(nm, shape, dty):
                    return cctx.enter_context(nc.sbuf_tensor("c%d_%s" % (b, nm), shape, dty))
                ubuf = _cb("ubuf", [128, UJG, T], BF16)
                w2g = _cb("w2g", [128, UJG, D], BF16)
                maskb = _cb("maskb", [128, T], BF16)
                smb = _cb("smb", [128, 2, S], BF16)
                for e in range(E):
                    for s_ in range(NS):
                        ys = s_ % 4
                        last = mmacc(Bp[ys], [(rssel[:, e * 128:(e + 1) * 128],
                                               ohT[:, s_ * S:(s_ + 1) * S])],
                                     ["B%d" % ys], extra=[(sdv, oh_v)])
                        tpe = p.inc(last, spe, 1)
                        p.wait(v, "v", spe, tpe)
                        p.buf_wait(v, "v", "maskb")
                        tv = p.inc(v.tensor_copy(maskb[:, s_ * S:(s_ + 1) * S], Bp[ys]), sdv, 1)
                        p.ps_release(["B%d" % ys], sdv, tv)
                    mb_v = sdv.n

                    for jg in range(NJG):
                        for j8 in range(UJG):
                            j = jg * UJG + j8
                            buf = j % 2
                            tca = stage_cast(bf_a[:, buf],
                                             w1_h[e, :, j * 128:(j + 1) * 128]
                                             .rearrange("(c p) m -> p c m", p=128),
                                             st_a[:], "st_a",
                                             sc, "sc", sac, dst_name="bf_a%d" % buf)
                            xn_ = "xcb%d" % buf
                            p.buf_wait(sq, "sq", xn_)
                            tg_ = ld(xcb[:, buf].rearrange("p (c m) -> p c m", m=128),
                                     w1_h[e, :, HID + j * 128:HID + (j + 1) * 128]
                                     .rearrange("(c p) m -> p c m", p=128), sldx[buf])
                            p.buf_wait(v, "v", "bf_g%d" % buf)
                            p.wait(v, "v", sldx[buf], tg_)
                            tcg = p.inc(v.tensor_copy(bf_g[:, buf].rearrange("p c m -> p (c m)"),
                                                      xcb[:, buf]), sdv, 1)
                            p.buf_use(xn_, sdv, tcg)
                            tc2 = stage_cast(w2g[:, j8, :], w2_h[e, j * 128:(j + 1) * 128, :],
                                             st_2[:], "st_2", g, "g", sgp, dst_name="w2g%d" % j8)
                            for q_ in range(NS):
                                us = q_ % 2
                                apn = psA[:, us * 1024:us * 1024 + 512]
                                gpn = psA[:, us * 1024 + 512:us * 1024 + 1024]
                                sla = "A%d" % (us * 2); slg = "A%d" % (us * 2 + 1)
                                mmacc(apn, [(bf_a[:, buf, c, :], xnT[:, c, q_ * S:(q_ + 1) * S])
                                            for c in range(DC)], [sla], extra=[(sac, tca)])
                                last = mmacc(gpn, [(bf_g[:, buf, c, :], xnT[:, c, q_ * S:(q_ + 1) * S])
                                                   for c in range(DC)], [slg], extra=[(sdv, tcg)])
                                tpe = p.inc(last, spe, 1)
                                p.buf_use("bf_a%d" % buf, spe, tpe)
                                p.buf_use("bf_g%d" % buf, spe, tpe)
                                p.buf_wait(sc, "sc", "smb%d" % us)
                                p.wait(sc, "sc", spe, tpe)
                                tsg = p.inc(sc.activation(smb[:, us, :], gpn, AF.Sigmoid,
                                                          bias=b1g[:, e, j:j + 1]), sac, 1)
                                p.wait(v, "v", sac, tsg)
                                p.wait(v, "v", sdv, mb_v)
                                v.tensor_tensor(smb[:, us, :], smb[:, us, :],
                                                maskb[:, q_ * S:(q_ + 1) * S], op=ALU.mult)
                                v.drain()
                                p.buf_wait(v, "v", "ubuf")
                                tv = p.inc(v.scalar_tensor_tensor(ubuf[:, j8, q_ * S:(q_ + 1) * S],
                                                                  apn, b1a[:, e, j:j + 1],
                                                                  smb[:, us, :],
                                                                  op0=ALU.add, op1=ALU.mult), sdv, 1)
                                p.buf_use("smb%d" % us, sdv, tv)
                                p.buf_use("maskb", sdv, tv)
                                p.ps_release([sla, slg], sdv, tv)
                        u_v = sdv.n
                        w2_ready = sgp.n
                        for dch in range(DC):
                            for hh in range(NH):
                                ys = (dch * NH + hh) % 2
                                yp = psB[:, ys * 1024:ys * 1024 + 1024]
                                ysl = ["B%d" % (ys * 2), "B%d" % (ys * 2 + 1)]
                                p.ps_acquire(t, ysl)
                                p.wait(t, "t", sdv, u_v)
                                p.wait(t, "t", sgp, w2_ready)
                                last = None
                                for nb in range(HSZ // 512):
                                    col = hh * HSZ + nb * 512
                                    for j8 in range(UJG):
                                        last = t.matmul(yp[:, nb * 512:(nb + 1) * 512],
                                                        w2g[:, j8, dch * 128:(dch + 1) * 128],
                                                        ubuf[:, j8, col:col + 512],
                                                        start=(j8 == 0), stop=(j8 == UJG - 1))
                                tpe = p.inc(last, spe, 1)
                                for j8 in range(UJG):
                                    p.buf_use("w2g%d" % j8, spe, tpe)
                                p.buf_use("ubuf", spe, tpe)
                                p.wait(v, "v", spe, tpe)
                                dst = acc[:, dch, hh * HSZ:(hh + 1) * HSZ]
                                tv = p.inc(v.tensor_tensor(dst, dst, yp, op=ALU.add), sdv, 1)
                                p.ps_release(ysl, sdv, tv)
                for dch in range(DC):
                    for s_ in range(NS):
                        ys = (dch * NS + s_) % 4
                        last = mmacc(Bp[ys], [(b2bf[:, dch * 128:(dch + 1) * 128],
                                               ohT[:, s_ * S:(s_ + 1) * S])], ["B%d" % ys])
                        tpe = p.inc(last, spe, 1)
                        p.wait(v, "v", spe, tpe)
                        dst = acc[:, dch, s_ * S:(s_ + 1) * S]
                        tv = p.inc(v.tensor_tensor(dst, dst, Bp[ys], op=ALU.add), sdv, 1)
                        p.ps_release(["B%d" % ys], sdv, tv)
            acc_done = sdv.n
            barrier()

            with ExitStack() as dctx:
                xtD = dctx.enter_context(nc.sbuf_tensor("d%d_xt" % b, [128, 2, D], F32))
                for i in range(TC):
                    bi = i % 2
                    p.buf_wait(sq, "sq", "xcb%d" % bi)
                    p.buf_wait(sq, "sq", "xtD%d" % bi)
                    tokx = ld(xtD[:, bi], x_h[b, i * 128:(i + 1) * 128, :], sldd[bi])
                    p.ps_acquire(t, ["A0", "A1", "A2", "A3"])
                    p.wait(t, "t", sdv, acc_done)
                    ti = None
                    for c in range(DC):
                        ti = t.transpose(psA[:, c * 256:c * 256 + 128],
                                         acc[:, c, i * 128:(i + 1) * 128], identf[:])
                    tpe = p.inc(ti, spe, 1)
                    p.wait(v, "v", spe, tpe)
                    p.wait(v, "v", sldd[bi], tokx)
                    last = None
                    for c in range(DC):
                        last = v.tensor_tensor(xcb[:, bi, c * 128:(c + 1) * 128],
                                               psA[:, c * 256:c * 256 + 128],
                                               xtD[:, bi, c * 128:(c + 1) * 128], op=ALU.add)
                    tv = p.inc(last, sdv, 1)
                    p.buf_use("xtD%d" % bi, sdv, tv)
                    p.ps_release(["A0", "A1", "A2", "A3"], sdv, tv)
                    p.wait(sc, "sc", sdv, tv)
                    tst = p.inc(sc.dma_start(out=out_h[b, i * 128:(i + 1) * 128, :],
                                             in_=xcb[:, bi]), sstb[bi], 16)
                    p.buf_use("xcb%d" % bi, sstb[bi], tst)
            barrier()

        sc.wait_ge(sstb[0].h, sstb[0].n)
        sc.wait_ge(sstb[1].h, sstb[1].n)
        sq.wait_ge(sstb[0].h, sstb[0].n)
        sq.wait_ge(sstb[1].h, sstb[1].n)
    return nc



def kernel(**inputs):
    cfg = REAL
    shared = host_prep(cfg, inputs)
    x = np.ascontiguousarray(np.asarray(inputs["L0_attn_x"], dtype=np.float32))
    n_cores = 8
    per = x.shape[0] // n_cores
    nc = build_program(cfg)
    in_maps = []
    for c in range(n_cores):
        m = dict(shared)
        m["x"] = x[c * per:(c + 1) * per]
        in_maps.append(m)
    res = run_bass_kernel_spmd(nc, in_maps, core_ids=list(range(n_cores)))
    return np.concatenate([r["out"] for r in res.results], axis=0)


# version 04
# speedup vs baseline: 1.1774x; 1.1774x over previous
import numpy as np
import ml_dtypes
import concourse.bass as bass
import concourse.mybir as mybir
from concourse.bass_utils import run_bass_kernel_spmd

F32 = mybir.dt.float32
BF16 = mybir.dt.bfloat16
U32 = mybir.dt.uint32
AF = mybir.ActivationFunctionType
ALU = mybir.AluOpType
AX = mybir.AxisListType
BF = ml_dtypes.bfloat16

HD = 64
HALF = 32
S = 512


class CFG:
    def __init__(self, T=2048, D=1024, L=256, HID=4096, E=8, HPG=4, B=2):
        self.T, self.D, self.L, self.HID, self.E, self.HPG, self.B = T, D, L, HID, E, HPG, B
        self.HEADS = D // HD
        self.NG = self.HEADS // HPG
        self.DC = D // 128
        self.TC = T // 128
        self.NS = T // S
        self.LC = L // 128
        self.HJ = HID // 128
        self.UJG = min(8, self.HJ)
        self.NJG = self.HJ // self.UJG
        self.NH = max(1, T // 1024)
        self.HSZ = T // self.NH
        self.CPG = HPG * HD // 128
        assert D % 128 == 0 and T % S == 0 and L % 128 == 0
        assert HID % (128 * self.UJG) == 0
        assert self.HEADS % HPG == 0 and HPG % 2 == 0 and self.CPG >= 1
        assert HPG * HD <= 512


REAL = CFG()


def _swap_cols(w, heads):
    out = np.empty_like(w)
    for h in range(heads):
        b = h * HD
        out[:, b:b + HALF] = w[:, b + HALF:b + HD]
        out[:, b + HALF:b + HD] = w[:, b:b + HALF]
    return out


def host_consts(cfg):
    T = cfg.T
    pos = np.arange(T, dtype=np.float64)[:, None]
    inv = 1.0 / 10000.0 ** (np.arange(HALF, dtype=np.float64) / HALF)
    ang = pos * inv[None, :]
    cosv = np.cos(ang).astype(np.float32).T
    sinv = np.sin(ang).astype(np.float32).T
    cos2 = np.concatenate([cosv, cosv], 0)
    sin2 = np.concatenate([-sinv, sinv], 0)
    masks = np.zeros((4, 128, S), BF)
    qf = np.arange(S)[None, :]
    kp = np.arange(128)[:, None]
    for off in range(4):
        masks[off] = ((kp + off * 128) <= qf).astype(BF)
    iota8 = np.broadcast_to(np.arange(cfg.E, dtype=np.float32), (128, cfg.E)).copy()
    ident = np.eye(128, dtype=np.float32)
    ones128 = np.ones((128, 128), BF)
    rs_sel = np.zeros((8, cfg.E * 128), BF)
    for e in range(cfg.E):
        rs_sel[e, e * 128:(e + 1) * 128] = 1
    epsc = np.full((128, 1), 1e-5, np.float32)
    return dict(cos2t=cos2.astype(BF), sin2st=sin2.astype(BF), masks=masks,
                iota8=iota8, identf=ident, ones128=ones128, rs_sel=rs_sel, epsc=epsc)


def host_prep(cfg, iv):
    E, L, HID = cfg.E, cfg.L, cfg.HID
    d = dict(host_consts(cfg))
    d["wq"] = np.ascontiguousarray(iv["L0_attn_W_q"], dtype=np.float32)
    d["wqsw"] = np.ascontiguousarray(_swap_cols(d["wq"], cfg.HEADS))
    d["wdkv"] = np.ascontiguousarray(iv["L0_attn_W_dkv"], dtype=np.float32)
    d["wuk"] = np.ascontiguousarray(iv["L0_attn_W_uk"], dtype=np.float32)
    d["wuksw"] = np.ascontiguousarray(_swap_cols(d["wuk"], cfg.HEADS))
    d["wuv"] = np.ascontiguousarray(iv["L0_attn_W_uv"], dtype=np.float32)
    d["wo"] = np.ascontiguousarray(iv["L0_attn_W_o"], dtype=np.float32)
    lng = np.broadcast_to(np.asarray(iv["L0_attn_ln_gamma"], np.float32), (128, L))
    lnb = np.broadcast_to(np.asarray(iv["L0_attn_ln_beta"], np.float32), (128, L))
    d["lngb"] = np.ascontiguousarray(np.stack([lng, lnb]))
    d["rw"] = np.ascontiguousarray(iv["L0_ffn_router_w"], dtype=np.float32)
    d["rbt"] = np.ascontiguousarray(
        np.broadcast_to(np.asarray(iv["L0_ffn_router_b"], np.float32), (128, E)))
    d["w1"] = np.ascontiguousarray(iv["L0_ffn_w1"], dtype=np.float32)
    d["w2"] = np.ascontiguousarray(iv["L0_ffn_w2"], dtype=np.float32)
    b1 = np.asarray(iv["L0_ffn_b1"], np.float32)
    d["b1a"] = np.ascontiguousarray(b1[:, :HID].reshape(E, cfg.HJ, 128).transpose(0, 2, 1))
    d["b1g"] = np.ascontiguousarray(b1[:, HID:].reshape(E, cfg.HJ, 128).transpose(0, 2, 1))
    d["b2"] = np.ascontiguousarray(iv["L0_ffn_b2"], dtype=np.float32)
    return d


class Sem:
    __slots__ = ("h", "n", "name")
    def __init__(self, h, name):
        self.h, self.n, self.name = h, 0, name


class P:
    def __init__(self):
        self.seen = {}
        self.slot = {}
        self.bufc = {}

    def inc(self, instr, sem, n=1):
        instr.then_inc(sem.h, n)
        sem.n += n
        return sem.n

    def wait(self, eng, ename, sem, val):
        if val <= 0:
            return
        key = (ename, sem.name)
        if self.seen.get(key, 0) >= val:
            return
        eng.wait_ge(sem.h, val)
        self.seen[key] = val

    def _w(self, eng, ename, d):
        for sem, tok in d.values():
            self.wait(eng, ename, sem, tok)

    def ps_acquire(self, teng, slots):
        for s_ in slots:
            self._w(teng, "t", self.slot.get(s_, {}))

    def ps_release(self, slots, sem, tok):
        for s_ in slots:
            self.slot.setdefault(s_, {})[sem.name] = (sem, tok)

    def buf_wait(self, eng, ename, name):
        self._w(eng, ename, self.bufc.get(name, {}))

    def buf_use(self, name, sem, tok):
        self.bufc.setdefault(name, {})[sem.name] = (sem, tok)


def build_program(cfg):
    from contextlib import ExitStack
    nc = bass.Bass("TRN2", target_bir_lowering=False)
    T, D, L, HID, E, B = cfg.T, cfg.D, cfg.L, cfg.HID, cfg.E, cfg.B
    DC, TC, NS, LC, HJ = cfg.DC, cfg.TC, cfg.NS, cfg.LC, cfg.HJ
    NG, HPG, CPG, UJG, NJG = cfg.NG, cfg.HPG, cfg.CPG, cfg.UJG, cfg.NJG
    NH, HSZ = cfg.NH, cfg.HSZ
    HC = HPG // 2
    VW = HPG * HD

    def din(name, shape, dty=F32):
        return nc.dram_tensor(name, list(shape), dty, kind="ExternalInput").ap()

    x_h = din("x", (B, T, D))
    wq_h = din("wq", (D, D)); wqsw_h = din("wqsw", (D, D))
    wdkv_h = din("wdkv", (D, L))
    wuk_h = din("wuk", (L, D)); wuksw_h = din("wuksw", (L, D))
    wuv_h = din("wuv", (L, D)); wo_h = din("wo", (D, D))
    lngb_h = din("lngb", (2, 128, L))
    rw_h = din("rw", (D, E)); rbt_h = din("rbt", (128, E))
    w1_h = din("w1", (E, D, 2 * HID)); w2_h = din("w2", (E, HID, D))
    b1a_h = din("b1a", (E, 128, HJ)); b1g_h = din("b1g", (E, 128, HJ))
    b2_h = din("b2", (E, D))
    cos_h = din("cos2t", (64, T), BF16); sin_h = din("sin2st", (64, T), BF16)
    masks_h = din("masks", (4, 128, S), BF16)
    iota_h = din("iota8", (128, E)); identf_h = din("identf", (128, 128))
    ones_h = din("ones128", (128, 128), BF16)
    rssel_h = din("rs_sel", (8, E * 128), BF16)
    epsc_h = din("epsc", (128, 1))
    out_h = nc.dram_tensor("out", [B, T, D], F32, kind="ExternalOutput").ap()

    with ExitStack() as _ctx:
        def _sb(*a):
            return _ctx.enter_context(nc.sbuf_tensor(*a))
        def _ps(*a):
            return _ctx.enter_context(nc.psum_tensor(*a))
        def _sem(n):
            return _ctx.enter_context(nc.semaphore(n))
        xnT = _sb("g_xnT", [128, DC, T], BF16)
        acc = _sb("g_acc", [128, DC, T], F32)
        ohT = _sb("g_ohT", [8, T], BF16)
        iota8 = _sb("g_iota8", [128, E], F32)
        identf = _sb("g_identf", [128, 128], F32)
        ones128 = _sb("g_ones128", [128, 128], BF16)
        rssel = _sb("g_rssel", [8, E * 128], BF16)
        rbt = _sb("g_rbt", [128, E], F32)
        lngb = _sb("g_lngb", [128, 2, L], F32)
        b1a = _sb("g_b1a", [128, E, HJ], F32)
        b1g = _sb("g_b1g", [128, E, HJ], F32)
        rwbf = _sb("g_rwbf", [128, DC, E], BF16)
        b2bf = _sb("g_b2bf", [8, D], BF16)
        st_a = _sb("g_st_a", [128, DC, 128], F32)
        st_2 = _sb("g_st_2", [128, D], F32)
        bf_a = _sb("g_bf_a", [128, 2, DC, 128], BF16)
        bf_g = _sb("g_bf_g", [128, 2, DC, 128], BF16)
        xcb = _sb("g_xcb", [128, 2, D], F32)
        stat = _sb("g_stat", [128, 12], F32)
        lgb = _sb("g_lgb", [128, E], F32)
        mx8 = _sb("g_mx8", [128, 8], F32)
        idx8 = _sb("g_idx8", [128, 8], U32)
        ohf = _sb("g_ohf", [128, E], F32)
        tiny = _sb("g_tiny", [1, 16], F32)
        tinyb = _sb("g_tinyb", [1, 2], BF16)
        negc = _sb("g_negc", [128, 1], F32)
        epsc = _sb("g_epsc", [128, 1], F32)
        psA = _ps("ps_A", [128, 2048], F32)
        psB = _ps("ps_B", [128, 2048], F32)
        h_ld = _sem("sldc"); h_pe = _sem("spe")
        h_dv = _sem("sdv"); h_ac = _sem("sac"); h_gp = _sem("sgp")
        h_la = _sem("slda"); h_l2 = _sem("sld2")
        h_lx0 = _sem("sldx0"); h_lx1 = _sem("sldx1")
        h_ld0 = _sem("sldd0"); h_ld1 = _sem("sldd1")
        h_st0 = _sem("sst0"); h_st1 = _sem("sst1")
        block = _ctx.enter_context(nc.Block())

        p = P()
        sld = Sem(h_ld, "ldc"); spe = Sem(h_pe, "pe")
        sdv = Sem(h_dv, "dv"); sac = Sem(h_ac, "ac"); sgp = Sem(h_gp, "gp")
        slda = Sem(h_la, "lda"); sld2 = Sem(h_l2, "ld2")
        sldx = [Sem(h_lx0, "ldx0"), Sem(h_lx1, "ldx1")]
        sldd = [Sem(h_ld0, "ldd0"), Sem(h_ld1, "ldd1")]
        sstb = [Sem(h_st0, "st0"), Sem(h_st1, "st1")]
        sq, t, v, sc, g = nc.sync, nc.tensor, nc.vector, nc.scalar, nc.gpsimd
        A = [psA[:, i * 512:(i + 1) * 512] for i in range(4)]
        Bp = [psB[:, i * 512:(i + 1) * 512] for i in range(4)]
        ALL_SEMS = (sld, spe, sdv, sac, sgp, slda, sld2, sldx[0], sldx[1],
                    sldd[0], sldd[1], sstb[0], sstb[1])
        ENGS = ((sq, "sq"), (t, "t"), (v, "v"), (sc, "sc"), (g, "g"))

        def ld(dst, src, sem=None):
            sem = sem or sld
            return p.inc(sq.dma_start(out=dst, in_=src), sem, 16)

        def mmacc(out, pairs, slots, extra=()):
            p.ps_acquire(t, slots)
            for (sem, tok) in extra:
                p.wait(t, "t", sem, tok)
            n = len(pairs)
            last = None
            for i_, (l_, r_) in enumerate(pairs):
                last = t.matmul(out, l_, r_, start=(i_ == 0), stop=(i_ == n - 1))
            return last

        def barrier():
            for eng, en in ENGS:
                for sem in ALL_SEMS:
                    p.wait(eng, en, sem, sem.n)

        def stage_cast(dst, src_h, stage, stage_name, ceng, cename, csem, dst_name=None):
            lsem = {"st_a": slda, "st_2": sld2}[stage_name]
            p.buf_wait(sq, "sq", stage_name)
            tok = ld(stage, src_h, lsem)
            p.wait(ceng, cename, lsem, tok)
            if dst_name:
                p.buf_wait(ceng, cename, dst_name)
            if ceng is sc:
                tc = p.inc(sc.copy(dst, stage), csem, 1)
            elif ceng is v:
                tc = p.inc(v.tensor_copy(dst, stage), csem, 1)
            else:
                tc = p.inc(g.tensor_copy(dst, stage), csem, 1)
            p.buf_use(stage_name, csem, tc)
            return tc

        ld(iota8[:], iota_h); ld(identf[:], identf_h)
        ld(ones128[:], ones_h); ld(rssel[:], rssel_h)
        ld(rbt[:], rbt_h)
        ld(epsc[:], epsc_h)
        ld(lngb[:], lngb_h.rearrange("o p q -> p o q"))
        ld(b1a[:], b1a_h.rearrange("e p j -> p e j"))
        ld(b1g[:], b1g_h.rearrange("e p j -> p e j"))
        stage_cast(rwbf[:], rw_h.rearrange("(c p) e -> p c e", p=128),
                   st_a[:, :, 0:E], "st_a", g, "g", sgp)
        stage_cast(b2bf[:], b2_h, st_2[0:8, :], "st_2", g, "g", sgp)
        barrier()

        for b in range(B):
            with ExitStack() as bctx:
                def _bb(nm, shape, dty):
                    return bctx.enter_context(nc.sbuf_tensor("b%d_%s" % (b, nm), shape, dty))
                dumpA = _bb("dumpA", [128, D], F32)
                wdkvbf = _bb("wdkv", [128, DC, L], BF16)
                wukbf = _bb("wuk", [128, LC, D], BF16)
                wukswbf = _bb("wuksw", [128, LC, D], BF16)
                wuvbf = _bb("wuv", [128, LC, D], BF16)
                latnT = _bb("latnT", [128, LC, T], BF16)
                kroT = _bb("kroT", [128, HC, T], BF16)
                qroT = _bb("qroT", [128, HC, T], BF16)
                v65 = _bb("v65", [128, TC, HPG, 65], BF16)
                catT = _bb("catT", [128, CPG, S], BF16)
                ropet = _bb("ropet", [128, 2, S], F32)
                attnT = _bb("attnT", [128, 2, S], BF16)
                recipb = _bb("recipb", [128, S], BF16)
                bc_sb = _bb("bc_sb", [64, S], BF16)
                masksb = _bb("masksb", [128, 4, S], BF16)
                cos2t = _bb("cos2t", [64, T], BF16)
                sin2st = _bb("sin2st", [64, T], BF16)
                _hold = {}

                prelude = []
                def _step_consts():
                    ld(masksb[:], masks_h.rearrange("o p q -> p o q"))
                    ld(cos2t[:], cos_h)
                    _hold["b_ld"] = ld(sin2st[:], sin_h)
                prelude.append(_step_consts)
                for lc2 in range(L // 128):
                    prelude.append(lambda lc2=lc2: stage_cast(
                        wdkvbf[:, :, lc2 * 128:(lc2 + 1) * 128],
                        wdkv_h[:, lc2 * 128:(lc2 + 1) * 128].rearrange("(c p) l -> p c l", p=128),
                        st_a[:], "st_a", g, "g", sgp))
                for (hsrc, dstw) in ((wuk_h, wukbf), (wuksw_h, wukswbf), (wuv_h, wuvbf)):
                    for lc in range(LC):
                        prelude.append(lambda hsrc=hsrc, dstw=dstw, lc=lc: stage_cast(
                            dstw[:, lc, :], hsrc[lc * 128:(lc + 1) * 128, :],
                            st_2[:], "st_2", g, "g", sgp))
                def _step_ones():
                    v.memset(v65[:, :, :, 64:65], 1.0)
                    p.inc(v.drain(), sdv, 1)
                prelude.append(_step_ones)

                for i in range(TC):
                    bi = i % 2
                    xn_ = "xcb%d" % bi
                    p.buf_wait(sq, "sq", xn_)
                    tokx = ld(xcb[:, bi], x_h[b, i * 128:(i + 1) * 128, :], sldx[bi])
                    if prelude:
                        prelude.pop(0)()
                    p.wait(v, "v", sldx[bi], tokx)
                    v.reduce_sum(stat[:, 0:1], xcb[:, bi], axis=AX.X)
                    v.drain()
                    v.tensor_scalar(stat[:, 1:2], stat[:, 0:1], 1.0 / D, None, op0=ALU.mult)
                    v.drain()
                    tv = p.inc(v.tensor_scalar(xcb[:, bi], xcb[:, bi], stat[:, 1:2], None,
                                               op0=ALU.subtract), sdv, 1)
                    p.wait(sc, "sc", sdv, tv)
                    p.buf_wait(sc, "sc", "dumpA")
                    sc.activation(dumpA[:], xcb[:, bi], AF.Square, accum_out=stat[:, 2:3])
                    sc.drain()
                    sc.activation(stat[:, 3:4], stat[:, 2:3], AF.Ln, bias=epsc[:], scale=1.0 / D)
                    sc.drain()
                    ta = p.inc(sc.activation(stat[:, 4:5], stat[:, 3:4], AF.Exp, scale=-0.5), sac, 1)
                    p.buf_use("dumpA", sac, ta)
                    p.wait(v, "v", sac, ta)
                    v.drain()
                    tv2 = p.inc(v.tensor_scalar(xcb[:, bi], xcb[:, bi], stat[:, 4:5], None,
                                                op0=ALU.mult), sdv, 1)
                    p.ps_acquire(t, ["A0", "A1", "A2", "A3"])
                    p.wait(t, "t", sdv, tv2)
                    ti = None
                    for c in range(DC):
                        ti = t.transpose(psA[:, c * 256:c * 256 + 128],
                                         xcb[:, bi, c * 128:(c + 1) * 128], identf[:])
                    tpe = p.inc(ti, spe, 1)
                    p.wait(v, "v", spe, tpe)
                    last = None
                    for c in range(DC):
                        last = v.tensor_copy(xnT[:, c, i * 128:(i + 1) * 128],
                                             psA[:, c * 256:c * 256 + 128])
                    tv3 = p.inc(last, sdv, 1)
                    p.buf_use(xn_, sdv, tv3)
                    p.ps_release(["A0", "A1", "A2", "A3"], sdv, tv3)
                    last = mmacc(psB[:, 0:E],
                                 [(xnT[:, c, i * 128:(i + 1) * 128], rwbf[:, c, :]) for c in range(DC)],
                                 ["B0"], extra=[(sdv, tv3)])
                    tpe2 = p.inc(last, spe, 1)
                    p.wait(v, "v", spe, tpe2)
                    v.tensor_tensor(lgb[:], psB[:, 0:E], rbt[:], op=ALU.add)
                    v.drain()
                    v.max(mx8[:], lgb[:])
                    v.drain()
                    v.max_index(idx8[:], mx8[:], lgb[:])
                    v.drain()
                    v.tensor_copy(stat[:, 5:6], idx8[:, 0:1])
                    v.drain()
                    tv4 = p.inc(v.tensor_scalar(ohf[:], iota8[:], stat[:, 5:6], None,
                                                op0=ALU.is_equal), sdv, 1)
                    p.ps_release(["B0"], sdv, tv4)
                    p.ps_acquire(t, ["B1"])
                    p.wait(t, "t", sdv, tv4)
                    ti = t.transpose(psB[0:E, 512:640], ohf[:], identf[:])
                    tpe3 = p.inc(ti, spe, 1)
                    p.wait(v, "v", spe, tpe3)
                    tv5 = p.inc(v.tensor_copy(ohT[:, i * 128:(i + 1) * 128], psB[0:E, 512:640]), sdv, 1)
                    p.ps_release(["B1"], sdv, tv5)
                oh_v = sdv.n
                for st in prelude:
                    st()
                prelude = []
                b_ld = _hold["b_ld"]
                barrier()

                for i in range(TC):
                    last = mmacc(psB[:, 1024:1280],
                                 [(xnT[:, c, i * 128:(i + 1) * 128], wdkvbf[:, c, :]) for c in range(DC)],
                                 ["B2"], extra=[(sgp, sgp.n)])
                    tpe = p.inc(last, spe, 1)
                    p.wait(v, "v", spe, tpe)
                    p.buf_wait(v, "v", "xcb0")
                    v.reduce_sum(stat[:, 6:7], psB[:, 1024:1280], axis=AX.X)
                    v.drain()
                    v.tensor_scalar(stat[:, 7:8], stat[:, 6:7], 1.0 / L, None, op0=ALU.mult)
                    v.drain()
                    tv = p.inc(v.tensor_scalar(xcb[:, 0, 0:L], psB[:, 1024:1280], stat[:, 7:8], None,
                                               op0=ALU.subtract), sdv, 1)
                    p.ps_release(["B2"], sdv, tv)
                    p.wait(sc, "sc", sdv, tv)
                    sc.activation(ropet[:, 0, 0:L], xcb[:, 0, 0:L], AF.Square, accum_out=stat[:, 8:9])
                    sc.drain()
                    sc.activation(stat[:, 9:10], stat[:, 8:9], AF.Ln, bias=epsc[:], scale=1.0 / L)
                    sc.drain()
                    ta = p.inc(sc.activation(stat[:, 10:11], stat[:, 9:10], AF.Exp, scale=-0.5), sac, 1)
                    p.wait(v, "v", sac, ta)
                    v.drain()
                    v.tensor_scalar(xcb[:, 1, 0:L], xcb[:, 0, 0:L], stat[:, 10:11], None, op0=ALU.mult)
                    v.drain()
                    v.tensor_tensor(xcb[:, 0, 0:L], xcb[:, 1, 0:L], lngb[:, 0, :], op=ALU.mult)
                    v.drain()
                    tv2 = p.inc(v.tensor_tensor(xcb[:, 1, 0:L], xcb[:, 0, 0:L], lngb[:, 1, :],
                                                op=ALU.add), sdv, 1)
                    p.buf_use("xcb0", sdv, tv2)
                    p.buf_use("xcb1", sdv, tv2)
                    p.ps_acquire(t, ["B3"])
                    p.wait(t, "t", sdv, tv2)
                    ti = None
                    for lc in range(LC):
                        ti = t.transpose(psB[:, 1536 + lc * 128:1536 + (lc + 1) * 128],
                                         xcb[:, 1, lc * 128:(lc + 1) * 128], identf[:])
                    tpe2 = p.inc(ti, spe, 1)
                    p.wait(v, "v", spe, tpe2)
                    last = None
                    for lc in range(LC):
                        last = v.tensor_copy(latnT[:, lc, i * 128:(i + 1) * 128],
                                             psB[:, 1536 + lc * 128:1536 + (lc + 1) * 128])
                    tv3 = p.inc(last, sdv, 1)
                    p.ps_release(["B3"], sdv, tv3)
                latn_v = sdv.n

                for grp in range(NG):
                    h0 = grp * HPG
                    for cc in range(HC):
                        c0 = (h0 + 2 * cc) * HD
                        bufq = cc % 2
                        tca = stage_cast(bf_a[:, bufq],
                                         wq_h[:, c0:c0 + 128].rearrange("(c p) m -> p c m", p=128),
                                         st_a[:], "st_a",
                                         g, "g", sgp, dst_name="bf_a%d" % bufq)
                        tcg = stage_cast(bf_g[:, bufq],
                                         wqsw_h[:, c0:c0 + 128].rearrange("(c p) m -> p c m", p=128),
                                         st_2[:].rearrange("p (c m) -> p c m", m=128), "st_2",
                                         g, "g", sgp, dst_name="bf_g%d" % bufq)
                        for par in range(2):
                            hh = h0 + 2 * cc + par
                            for s_ in range(NS):
                                mmacc(A[0][0:64, :],
                                      [(wukbf[:, lc, hh * HD:(hh + 1) * HD],
                                        latnT[:, lc, s_ * S:(s_ + 1) * S]) for lc in range(LC)],
                                      ["A0"], extra=[(sdv, latn_v)])
                                last = mmacc(A[1][0:64, :],
                                             [(wukswbf[:, lc, hh * HD:(hh + 1) * HD],
                                               latnT[:, lc, s_ * S:(s_ + 1) * S]) for lc in range(LC)],
                                             ["A1"])
                                tpe_k = p.inc(last, spe, 1)
                                mmacc(A[2][0:64, :],
                                      [(bf_a[:, bufq, c_, par * 64:par * 64 + 64],
                                        xnT[:, c_, s_ * S:(s_ + 1) * S]) for c_ in range(DC)],
                                      ["A2"], extra=[(sgp, max(tca, tcg))])
                                last = mmacc(A[3][0:64, :],
                                             [(bf_g[:, bufq, c_, par * 64:par * 64 + 64],
                                               xnT[:, c_, s_ * S:(s_ + 1) * S]) for c_ in range(DC)],
                                             ["A3"])
                                tpe_q = p.inc(last, spe, 1)
                                p.buf_use("bf_a%d" % bufq, spe, tpe_q)
                                p.buf_use("bf_g%d" % bufq, spe, tpe_q)
                                p.wait(v, "v", spe, tpe_k)
                                p.wait(v, "v", sld, b_ld)
                                p.buf_wait(v, "v", "kroT")
                                v.drain()
                                v.tensor_tensor(ropet[0:64, 0, :], A[0][0:64, :],
                                                cos2t[:, s_ * S:(s_ + 1) * S], op=ALU.mult)
                                v.tensor_tensor(ropet[0:64, 1, :], A[1][0:64, :],
                                                sin2st[:, s_ * S:(s_ + 1) * S], op=ALU.mult)
                                v.drain()
                                tv = p.inc(v.tensor_tensor(
                                    kroT[par * 64:par * 64 + 64, cc, s_ * S:(s_ + 1) * S],
                                    ropet[0:64, 0, :], ropet[0:64, 1, :], op=ALU.add), sdv, 1)
                                p.ps_release(["A0", "A1"], sdv, tv)
                                p.wait(v, "v", spe, tpe_q)
                                p.buf_wait(v, "v", "qroT")
                                v.drain()
                                v.tensor_tensor(ropet[0:64, 0, :], A[2][0:64, :],
                                                cos2t[:, s_ * S:(s_ + 1) * S], op=ALU.mult)
                                v.tensor_tensor(ropet[0:64, 1, :], A[3][0:64, :],
                                                sin2st[:, s_ * S:(s_ + 1) * S], op=ALU.mult)
                                v.drain()
                                tv = p.inc(v.tensor_tensor(
                                    qroT[par * 64:par * 64 + 64, cc, s_ * S:(s_ + 1) * S],
                                    ropet[0:64, 0, :], ropet[0:64, 1, :], op=ALU.add), sdv, 1)
                                p.ps_release(["A2", "A3"], sdv, tv)
                    kq_v = sdv.n

                    for i in range(TC):
                        p.buf_wait(v, "v", "v65")
                        last = mmacc(A[0][:, 0:VW],
                                     [(latnT[:, lc, i * 128:(i + 1) * 128],
                                       wuvbf[:, lc, h0 * HD:h0 * HD + VW]) for lc in range(LC)],
                                     ["A0"])
                        tpe = p.inc(last, spe, 1)
                        p.wait(v, "v", spe, tpe)
                        tv = p.inc(v.tensor_copy(v65[:, i, :, 0:64],
                                                 A[0][:, 0:VW].rearrange("p (h d) -> p h d", d=64)),
                                   sdv, 1)
                        p.ps_release(["A0"], sdv, tv)
                    v65_v = sdv.n

                    first = True
                    units = [(s_, par, cc) for s_ in range(NS) for par in range(2)
                             for cc in range(HC)]
                    for g0 in range(0, len(units), 4):
                        grp_u = units[g0:g0 + 4]
                        toks = []
                        for ui, (s_, par, cc) in enumerate(grp_u):
                            sb2 = ui % 2
                            p.buf_wait(sc, "sc", "attn%d" % sb2)
                            p.wait(sc, "sc", sdv, kq_v)
                            tq = p.inc(sc.activation(attnT[par * 64:par * 64 + 64, sb2, :],
                                                     kroT[par * 64:par * 64 + 64, cc,
                                                          s_ * S:(s_ + 1) * S],
                                                     AF.Square), sac, 1)
                            last = mmacc(Bp[ui][0:1, :],
                                         [(ones128[par * 64:par * 64 + 64, 0:1],
                                           attnT[par * 64:par * 64 + 64, sb2, :])],
                                         ["B%d" % ui], extra=[(sac, tq)])
                            tpe = p.inc(last, spe, 1)
                            p.buf_use("attn%d" % sb2, spe, tpe)
                        p.wait(v, "v", spe, spe.n)
                        v.reduce_max(tiny[0:1, 1:2], psB[0:1, 0:512 * len(grp_u)], axis=AX.X)
                        v.drain()
                        if first:
                            v.tensor_copy(tiny[0:1, 0:1], tiny[0:1, 1:2])
                            first = False
                        else:
                            v.tensor_tensor(tiny[0:1, 0:1], tiny[0:1, 0:1], tiny[0:1, 1:2],
                                            op=ALU.max)
                        tvm = p.inc(v.drain(), sdv, 1)
                        p.ps_release(["B0", "B1", "B2", "B3"], sdv, tvm)
                    mk_v = sdv.n

                    for s_ in range(NS):
                        units_q = [(par, cc) for par in range(2) for cc in range(HC)]
                        for g0 in range(0, len(units_q), 4):
                            grp_u = units_q[g0:g0 + 4]
                            for ui, (par, cc) in enumerate(grp_u):
                                sb2 = ui % 2
                                p.buf_wait(sc, "sc", "attn%d" % sb2)
                                tq = p.inc(sc.activation(attnT[par * 64:par * 64 + 64, sb2, :],
                                                         qroT[par * 64:par * 64 + 64, cc,
                                                              s_ * S:(s_ + 1) * S],
                                                         AF.Square), sac, 1)
                                last = mmacc(Bp[ui][0:1, :],
                                             [(ones128[par * 64:par * 64 + 64, 0:1],
                                               attnT[par * 64:par * 64 + 64, sb2, :])],
                                             ["B%d" % ui], extra=[(sac, tq)])
                                tpe = p.inc(last, spe, 1)
                                p.buf_use("attn%d" % sb2, spe, tpe)
                            p.wait(v, "v", spe, spe.n)
                            v.reduce_max(tiny[0:1, 2:3], psB[0:1, 0:512 * len(grp_u)], axis=AX.X)
                            v.drain()
                            if g0 == 0:
                                v.tensor_copy(tiny[0:1, 3:4], tiny[0:1, 2:3])
                            else:
                                v.tensor_tensor(tiny[0:1, 3:4], tiny[0:1, 3:4], tiny[0:1, 2:3],
                                                op=ALU.max)
                            tvq = p.inc(v.drain(), sdv, 1)
                            p.ps_release(["B0", "B1", "B2", "B3"], sdv, tvq)
                        p.wait(v, "v", sdv, mk_v)
                        tv = p.inc(v.tensor_tensor(tiny[0:1, 4:5], tiny[0:1, 3:4], tiny[0:1, 0:1],
                                                   op=ALU.mult), sdv, 1)
                        p.wait(sc, "sc", sdv, tv)
                        sc.activation(tiny[0:1, 5:6], tiny[0:1, 4:5], AF.Ln)
                        sc.drain()
                        ta = p.inc(sc.activation(tiny[0:1, 6:7], tiny[0:1, 5:6], AF.Exp, scale=0.5),
                                   sac, 1)
                        p.wait(v, "v", sac, ta)
                        v.tensor_scalar(tiny[0:1, 7:8], tiny[0:1, 6:7], -0.125, -2.0,
                                        op0=ALU.mult, op1=ALU.add)
                        v.drain()
                        tv = p.inc(v.tensor_copy(tinyb[0:1, 0:1], tiny[0:1, 7:8]), sdv, 1)
                        p.buf_wait(v, "v", "negc")
                        last = mmacc(psB[:, 1024:1025], [(ones128[0:1, :], tinyb[0:1, 0:1])],
                                     ["B2"], extra=[(sdv, tv)])
                        tpe = p.inc(last, spe, 1)
                        p.wait(v, "v", spe, tpe)
                        tnegc = p.inc(v.tensor_copy(negc[:, 0:1], psB[:, 1024:1025]), sdv, 1)
                        p.ps_release(["B2"], sdv, tnegc)

                        nkt = 4 * s_ + 4
                        for lh in range(HPG):
                            cc = lh // 2; par = lh % 2
                            ob = [0, 1, 3][lh % 3]
                            o65 = Bp[ob][0:65, :]
                            p.ps_acquire(t, ["B%d" % ob])
                            def emit_score(kt):
                                sb_ = kt % 2
                                p.ps_acquire(t, ["A%d" % sb_])
                                p.wait(t, "t", sdv, kq_v)
                                tmm = t.matmul(A[sb_],
                                               kroT[par * 64:par * 64 + 64, cc, kt * 128:(kt + 1) * 128],
                                               qroT[par * 64:par * 64 + 64, cc, s_ * S:(s_ + 1) * S],
                                               start=True, stop=True)
                                return p.inc(tmm, spe, 1)

                            tpe_s = emit_score(0)
                            for kt in range(nkt):
                                sb_ = kt % 2
                                stp = A[sb_]
                                abuf = attnT[:, sb_, :]
                                tpe = tpe_s
                                p.buf_wait(sc, "sc", "attn%d" % sb_)
                                p.wait(sc, "sc", sdv, tnegc)
                                p.wait(sc, "sc", spe, tpe)
                                tex = p.inc(sc.activation(abuf, stp, AF.Exp,
                                                          bias=negc[:, 0:1], scale=0.125), sac, 1)
                                p.buf_use("negc", sac, tex)
                                p.ps_release(["A%d" % sb_], sac, tex)
                                if kt + 1 < nkt:
                                    tpe_s = emit_score(kt + 1)
                                off = kt - 4 * s_
                                if off >= 0:
                                    p.wait(v, "v", sac, tex)
                                    tex2 = p.inc(v.tensor_tensor(abuf, abuf, masksb[:, off, :],
                                                                 op=ALU.mult), sdv, 1)
                                    p.wait(t, "t", sdv, tex2)
                                else:
                                    p.wait(t, "t", sac, tex)
                                p.wait(t, "t", sdv, v65_v)
                                last = t.matmul(o65, v65[:, kt, lh, :], abuf,
                                                start=(kt == 0), stop=(kt == nkt - 1))
                                tpe2 = p.inc(last, spe, 1)
                                p.buf_use("attn%d" % sb_, spe, tpe2)
                                p.buf_use("v65", spe, tpe2)
                            p.wait(sc, "sc", spe, spe.n)
                            p.buf_wait(sc, "sc", "recipb")
                            sc.activation(ropet[64:65, 0, :], o65[64:65, :], AF.Ln)
                            sc.drain()
                            trc = p.inc(sc.activation(recipb[64:65, :], ropet[64:65, 0, :],
                                                      AF.Exp, scale=-1.0), sac, 1)
                            last = mmacc(psB[0:64, 1024:1536],
                                         [(ones128[64:65, 0:64], recipb[64:65, :])],
                                         ["B2"], extra=[(sac, trc)])
                            tpe3 = p.inc(last, spe, 1)
                            p.buf_use("recipb", spe, tpe3)
                            p.wait(v, "v", spe, tpe3)
                            v.tensor_copy(bc_sb[:], psB[0:64, 1024:1536])
                            v.drain()
                            p.buf_wait(v, "v", "catT")
                            tv = p.inc(v.tensor_tensor(catT[par * 64:par * 64 + 64, cc, :],
                                                       o65[0:64, :], bc_sb[:], op=ALU.mult), sdv, 1)
                            p.ps_release(["B2"], sdv, tv)
                            p.ps_release(["B%d" % ob], sdv, tv)
                        cat_v = sdv.n

                        wo_tgt = [("bf_a0", bf_a[:, 0]), ("bf_g0", bf_g[:, 0])]
                        for ccw in range(CPG):
                            r0 = h0 * HD + ccw * 128
                            nm, tgt = wo_tgt[ccw % 2]
                            stage_cast(tgt.rearrange("p c m -> p (c m)"), wo_h[r0:r0 + 128, :],
                                       st_2[:], "st_2", g, "g", sgp, dst_name=nm)
                        wo_g = sgp.n
                        for dch in range(DC):
                            wsl = "A%d" % (2 + dch % 2)
                            wop = A[2 + dch % 2]
                            last = mmacc(wop,
                                         [(wo_tgt[ccw % 2][1][:, dch, :], catT[:, ccw, :])
                                          for ccw in range(CPG)],
                                         [wsl], extra=[(sgp, wo_g), (sdv, cat_v)])
                            tpe = p.inc(last, spe, 1)
                            p.buf_use("bf_a0", spe, tpe)
                            p.buf_use("bf_g0", spe, tpe)
                            p.buf_use("catT", spe, tpe)
                            p.wait(v, "v", spe, tpe)
                            dst = acc[:, dch, s_ * S:(s_ + 1) * S]
                            if grp == 0:
                                tv = p.inc(v.tensor_copy(dst, wop), sdv, 1)
                            else:
                                tv = p.inc(v.tensor_tensor(dst, dst, wop, op=ALU.add), sdv, 1)
                            p.ps_release([wsl], sdv, tv)
            barrier()

            with ExitStack() as cctx:
                def _cb(nm, shape, dty):
                    return cctx.enter_context(nc.sbuf_tensor("c%d_%s" % (b, nm), shape, dty))
                ubuf = _cb("ubuf", [128, UJG, T], BF16)
                w2g = _cb("w2g", [128, UJG, D], BF16)
                maskb = _cb("maskb", [128, T], BF16)
                smb = _cb("smb", [128, 2, S], BF16)
                for e in range(E):
                    for s_ in range(NS):
                        ys = s_ % 4
                        last = mmacc(Bp[ys], [(rssel[:, e * 128:(e + 1) * 128],
                                               ohT[:, s_ * S:(s_ + 1) * S])],
                                     ["B%d" % ys], extra=[(sdv, oh_v)])
                        tpe = p.inc(last, spe, 1)
                        p.wait(v, "v", spe, tpe)
                        p.buf_wait(v, "v", "maskb")
                        tv = p.inc(v.tensor_copy(maskb[:, s_ * S:(s_ + 1) * S], Bp[ys]), sdv, 1)
                        p.ps_release(["B%d" % ys], sdv, tv)
                    mb_v = sdv.n

                    for jg in range(NJG):
                        for j8 in range(UJG):
                            j = jg * UJG + j8
                            buf = j % 2
                            tca = stage_cast(bf_a[:, buf],
                                             w1_h[e, :, j * 128:(j + 1) * 128]
                                             .rearrange("(c p) m -> p c m", p=128),
                                             st_a[:], "st_a",
                                             sc, "sc", sac, dst_name="bf_a%d" % buf)
                            xn_ = "xcb%d" % buf
                            p.buf_wait(sq, "sq", xn_)
                            tg_ = ld(xcb[:, buf].rearrange("p (c m) -> p c m", m=128),
                                     w1_h[e, :, HID + j * 128:HID + (j + 1) * 128]
                                     .rearrange("(c p) m -> p c m", p=128), sldx[buf])
                            p.buf_wait(v, "v", "bf_g%d" % buf)
                            p.wait(v, "v", sldx[buf], tg_)
                            tcg = p.inc(v.tensor_copy(bf_g[:, buf].rearrange("p c m -> p (c m)"),
                                                      xcb[:, buf]), sdv, 1)
                            p.buf_use(xn_, sdv, tcg)
                            tc2 = stage_cast(w2g[:, j8, :], w2_h[e, j * 128:(j + 1) * 128, :],
                                             st_2[:], "st_2", g, "g", sgp, dst_name="w2g%d" % j8)
                            for q_ in range(NS):
                                us = q_ % 2
                                apn = psA[:, us * 1024:us * 1024 + 512]
                                gpn = psA[:, us * 1024 + 512:us * 1024 + 1024]
                                sla = "A%d" % (us * 2); slg = "A%d" % (us * 2 + 1)
                                mmacc(apn, [(bf_a[:, buf, c, :], xnT[:, c, q_ * S:(q_ + 1) * S])
                                            for c in range(DC)], [sla], extra=[(sac, tca)])
                                last = mmacc(gpn, [(bf_g[:, buf, c, :], xnT[:, c, q_ * S:(q_ + 1) * S])
                                                   for c in range(DC)], [slg], extra=[(sdv, tcg)])
                                tpe = p.inc(last, spe, 1)
                                p.buf_use("bf_a%d" % buf, spe, tpe)
                                p.buf_use("bf_g%d" % buf, spe, tpe)
                                p.buf_wait(sc, "sc", "smb%d" % us)
                                p.wait(sc, "sc", spe, tpe)
                                tsg = p.inc(sc.activation(smb[:, us, :], gpn, AF.Sigmoid,
                                                          bias=b1g[:, e, j:j + 1]), sac, 1)
                                p.wait(v, "v", sac, tsg)
                                p.wait(v, "v", sdv, mb_v)
                                v.tensor_tensor(smb[:, us, :], smb[:, us, :],
                                                maskb[:, q_ * S:(q_ + 1) * S], op=ALU.mult)
                                v.drain()
                                p.buf_wait(v, "v", "ubuf")
                                tv = p.inc(v.scalar_tensor_tensor(ubuf[:, j8, q_ * S:(q_ + 1) * S],
                                                                  apn, b1a[:, e, j:j + 1],
                                                                  smb[:, us, :],
                                                                  op0=ALU.add, op1=ALU.mult), sdv, 1)
                                p.buf_use("smb%d" % us, sdv, tv)
                                p.buf_use("maskb", sdv, tv)
                                p.ps_release([sla, slg], sdv, tv)
                        u_v = sdv.n
                        w2_ready = sgp.n
                        for dch in range(DC):
                            for hh in range(NH):
                                ys = (dch * NH + hh) % 2
                                yp = psB[:, ys * 1024:ys * 1024 + 1024]
                                ysl = ["B%d" % (ys * 2), "B%d" % (ys * 2 + 1)]
                                p.ps_acquire(t, ysl)
                                p.wait(t, "t", sdv, u_v)
                                p.wait(t, "t", sgp, w2_ready)
                                last = None
                                for nb in range(HSZ // 512):
                                    col = hh * HSZ + nb * 512
                                    for j8 in range(UJG):
                                        last = t.matmul(yp[:, nb * 512:(nb + 1) * 512],
                                                        w2g[:, j8, dch * 128:(dch + 1) * 128],
                                                        ubuf[:, j8, col:col + 512],
                                                        start=(j8 == 0), stop=(j8 == UJG - 1))
                                tpe = p.inc(last, spe, 1)
                                for j8 in range(UJG):
                                    p.buf_use("w2g%d" % j8, spe, tpe)
                                p.buf_use("ubuf", spe, tpe)
                                p.wait(v, "v", spe, tpe)
                                dst = acc[:, dch, hh * HSZ:(hh + 1) * HSZ]
                                tv = p.inc(v.tensor_tensor(dst, dst, yp, op=ALU.add), sdv, 1)
                                p.ps_release(ysl, sdv, tv)
                for dch in range(DC):
                    for s_ in range(NS):
                        ys = (dch * NS + s_) % 4
                        last = mmacc(Bp[ys], [(b2bf[:, dch * 128:(dch + 1) * 128],
                                               ohT[:, s_ * S:(s_ + 1) * S])], ["B%d" % ys])
                        tpe = p.inc(last, spe, 1)
                        p.wait(v, "v", spe, tpe)
                        dst = acc[:, dch, s_ * S:(s_ + 1) * S]
                        tv = p.inc(v.tensor_tensor(dst, dst, Bp[ys], op=ALU.add), sdv, 1)
                        p.ps_release(["B%d" % ys], sdv, tv)
            acc_done = sdv.n
            barrier()

            with ExitStack() as dctx:
                xtD = dctx.enter_context(nc.sbuf_tensor("d%d_xt" % b, [128, 2, D], F32))
                for i in range(TC):
                    bi = i % 2
                    p.buf_wait(sq, "sq", "xcb%d" % bi)
                    p.buf_wait(sq, "sq", "xtD%d" % bi)
                    tokx = ld(xtD[:, bi], x_h[b, i * 128:(i + 1) * 128, :], sldd[bi])
                    p.ps_acquire(t, ["A0", "A1", "A2", "A3"])
                    p.wait(t, "t", sdv, acc_done)
                    ti = None
                    for c in range(DC):
                        ti = t.transpose(psA[:, c * 256:c * 256 + 128],
                                         acc[:, c, i * 128:(i + 1) * 128], identf[:])
                    tpe = p.inc(ti, spe, 1)
                    p.wait(v, "v", spe, tpe)
                    p.wait(v, "v", sldd[bi], tokx)
                    last = None
                    for c in range(DC):
                        last = v.tensor_tensor(xcb[:, bi, c * 128:(c + 1) * 128],
                                               psA[:, c * 256:c * 256 + 128],
                                               xtD[:, bi, c * 128:(c + 1) * 128], op=ALU.add)
                    tv = p.inc(last, sdv, 1)
                    p.buf_use("xtD%d" % bi, sdv, tv)
                    p.ps_release(["A0", "A1", "A2", "A3"], sdv, tv)
                    p.wait(sc, "sc", sdv, tv)
                    tst = p.inc(sc.dma_start(out=out_h[b, i * 128:(i + 1) * 128, :],
                                             in_=xcb[:, bi]), sstb[bi], 16)
                    p.buf_use("xcb%d" % bi, sstb[bi], tst)
            barrier()

        sc.wait_ge(sstb[0].h, sstb[0].n)
        sc.wait_ge(sstb[1].h, sstb[1].n)
        sq.wait_ge(sstb[0].h, sstb[0].n)
        sq.wait_ge(sstb[1].h, sstb[1].n)
    return nc



def kernel(**inputs):
    cfg = REAL
    shared = host_prep(cfg, inputs)
    x = np.ascontiguousarray(np.asarray(inputs["L0_attn_x"], dtype=np.float32))
    n_cores = 8
    per = x.shape[0] // n_cores
    nc = build_program(cfg)
    in_maps = []
    for c in range(n_cores):
        m = dict(shared)
        m["x"] = x[c * per:(c + 1) * per]
        in_maps.append(m)
    res = run_bass_kernel_spmd(nc, in_maps, core_ids=list(range(n_cores)))
    return np.concatenate([r["out"] for r in res.results], axis=0)


# version 09
# speedup vs baseline: 1.2662x; 1.0754x over previous
import numpy as np
import ml_dtypes
import concourse.bass as bass
import concourse.mybir as mybir
from concourse.bass_utils import run_bass_kernel_spmd

F32 = mybir.dt.float32
BF16 = mybir.dt.bfloat16
U32 = mybir.dt.uint32
AF = mybir.ActivationFunctionType
ALU = mybir.AluOpType
AX = mybir.AxisListType
BF = ml_dtypes.bfloat16

HD = 64
HALF = 32
S = 512


class CFG:
    def __init__(self, T=2048, D=1024, L=256, HID=4096, E=8, HPG=4, B=2):
        self.T, self.D, self.L, self.HID, self.E, self.HPG, self.B = T, D, L, HID, E, HPG, B
        self.HEADS = D // HD
        self.NG = self.HEADS // HPG
        self.DC = D // 128
        self.TC = T // 128
        self.NS = T // S
        self.LC = L // 128
        self.HJ = HID // 128
        self.UJG = min(8, self.HJ)
        self.NJG = self.HJ // self.UJG
        self.NH = max(1, T // 1024)
        self.HSZ = T // self.NH
        self.CPG = HPG * HD // 128
        self.CAPQ = 2
        assert D % 128 == 0 and T % S == 0 and L % 128 == 0
        assert HID % (128 * self.UJG) == 0
        assert self.HEADS % HPG == 0 and HPG % 2 == 0 and self.CPG >= 1
        assert HPG * HD <= 512


REAL = CFG()

DBG = {"plain_tok": False, "plain_gather": False, "plain_yscat": False, "skip_eq": False, "skip_slot": False, "no_xnscr": False, "eq_noxg": False, "eq_nou": False, "eq_noy": False, "eq_noflush": False}


def _swap_cols(w, heads):
    out = np.empty_like(w)
    for h in range(heads):
        b = h * HD
        out[:, b:b + HALF] = w[:, b + HALF:b + HD]
        out[:, b + HALF:b + HD] = w[:, b:b + HALF]
    return out


def host_consts(cfg):
    T = cfg.T
    pos = np.arange(T, dtype=np.float64)[:, None]
    inv = 1.0 / 10000.0 ** (np.arange(HALF, dtype=np.float64) / HALF)
    ang = pos * inv[None, :]
    cosv = np.cos(ang).astype(np.float32).T
    sinv = np.sin(ang).astype(np.float32).T
    cos2 = np.concatenate([cosv, cosv], 0)
    sin2 = np.concatenate([-sinv, sinv], 0)
    masks = np.zeros((4, 128, S), BF)
    qf = np.arange(S)[None, :]
    kp = np.arange(128)[:, None]
    for off in range(4):
        masks[off] = ((kp + off * 128) <= qf).astype(BF)
    iota8 = np.broadcast_to(np.arange(cfg.E, dtype=np.float32), (128, cfg.E)).copy()
    ident = np.eye(128, dtype=np.float32)
    ones128 = np.ones((128, 128), BF)
    epsc = np.full((128, 1), 1e-5, np.float32)
    T, E = cfg.T, cfg.E
    tokv = np.broadcast_to(np.arange(T, dtype=np.float32)[:, None], (T, 8)).copy()
    sent8 = np.broadcast_to((T + np.arange(E * T, dtype=np.float32) % 128)[:, None],
                            (E * T, 8)).copy()
    etm1 = np.zeros((128, 1), np.float32)
    etm1[0:E, 0] = np.arange(E, dtype=np.float32) * T - 1.0
    onescf = np.ones((128, 1), np.float32)
    return dict(cos2t=cos2.astype(BF), sin2st=sin2.astype(BF), masks=masks,
                iota8=iota8, identf=ident, ones128=ones128, epsc=epsc,
                tokv=tokv, sent8=sent8, etm1=etm1, onescf=onescf)


def host_prep(cfg, iv):
    E, L, HID = cfg.E, cfg.L, cfg.HID
    d = dict(host_consts(cfg))
    d["wq"] = np.ascontiguousarray(iv["L0_attn_W_q"], dtype=np.float32)
    d["wqsw"] = np.ascontiguousarray(_swap_cols(d["wq"], cfg.HEADS))
    d["wdkv"] = np.ascontiguousarray(iv["L0_attn_W_dkv"], dtype=np.float32)
    d["wuk"] = np.ascontiguousarray(iv["L0_attn_W_uk"], dtype=np.float32)
    d["wuksw"] = np.ascontiguousarray(_swap_cols(d["wuk"], cfg.HEADS))
    d["wuv"] = np.ascontiguousarray(iv["L0_attn_W_uv"], dtype=np.float32)
    d["wo"] = np.ascontiguousarray(iv["L0_attn_W_o"], dtype=np.float32)
    lng = np.broadcast_to(np.asarray(iv["L0_attn_ln_gamma"], np.float32), (128, L))
    lnb = np.broadcast_to(np.asarray(iv["L0_attn_ln_beta"], np.float32), (128, L))
    d["lngb"] = np.ascontiguousarray(np.stack([lng, lnb]))
    d["rw"] = np.ascontiguousarray(iv["L0_ffn_router_w"], dtype=np.float32)
    d["rbt"] = np.ascontiguousarray(
        np.broadcast_to(np.asarray(iv["L0_ffn_router_b"], np.float32), (128, E)))
    d["w1"] = np.ascontiguousarray(iv["L0_ffn_w1"], dtype=np.float32)
    d["w2"] = np.ascontiguousarray(iv["L0_ffn_w2"], dtype=np.float32)
    b1 = np.asarray(iv["L0_ffn_b1"], np.float32)
    d["b1a"] = np.ascontiguousarray(b1[:, :HID].reshape(E, cfg.HJ, 128).transpose(0, 2, 1))
    d["b1g"] = np.ascontiguousarray(b1[:, HID:].reshape(E, cfg.HJ, 128).transpose(0, 2, 1))
    b2 = np.asarray(iv["L0_ffn_b2"], np.float32)
    d["b2pm"] = np.ascontiguousarray(
        b2.reshape(E, cfg.DC, 128).transpose(2, 0, 1))
    return d


class Sem:
    __slots__ = ("h", "n", "name")
    def __init__(self, h, name):
        self.h, self.n, self.name = h, 0, name


class P:
    def __init__(self):
        self.seen = {}
        self.slot = {}
        self.bufc = {}

    def inc(self, instr, sem, n=1):
        instr.then_inc(sem.h, n)
        sem.n += n
        return sem.n

    def wait(self, eng, ename, sem, val):
        if val <= 0:
            return
        key = (ename, sem.name)
        if self.seen.get(key, 0) >= val:
            return
        eng.wait_ge(sem.h, val)
        self.seen[key] = val

    def _w(self, eng, ename, d):
        for sem, tok in d.values():
            self.wait(eng, ename, sem, tok)

    def ps_acquire(self, teng, slots):
        for s_ in slots:
            self._w(teng, "t", self.slot.get(s_, {}))

    def ps_release(self, slots, sem, tok):
        for s_ in slots:
            self.slot.setdefault(s_, {})[sem.name] = (sem, tok)

    def buf_wait(self, eng, ename, name):
        self._w(eng, ename, self.bufc.get(name, {}))

    def buf_use(self, name, sem, tok):
        self.bufc.setdefault(name, {})[sem.name] = (sem, tok)


def build_program(cfg):
    from contextlib import ExitStack
    nc = bass.Bass("TRN2", target_bir_lowering=False, detect_race_conditions=False)
    T, D, L, HID, E, B = cfg.T, cfg.D, cfg.L, cfg.HID, cfg.E, cfg.B
    DC, TC, NS, LC, HJ = cfg.DC, cfg.TC, cfg.NS, cfg.LC, cfg.HJ
    NG, HPG, CPG, UJG, NJG = cfg.NG, cfg.HPG, cfg.CPG, cfg.UJG, cfg.NJG
    CAPQ = cfg.CAPQ
    NH, HSZ = cfg.NH, cfg.HSZ
    HC = HPG // 2
    VW = HPG * HD

    def din(name, shape, dty=F32):
        return nc.dram_tensor(name, list(shape), dty, kind="ExternalInput").ap()

    x_h = din("x", (B, T, D))
    wq_h = din("wq", (D, D)); wqsw_h = din("wqsw", (D, D))
    wdkv_h = din("wdkv", (D, L))
    wuk_h = din("wuk", (L, D)); wuksw_h = din("wuksw", (L, D))
    wuv_h = din("wuv", (L, D)); wo_h = din("wo", (D, D))
    lngb_h = din("lngb", (2, 128, L))
    rw_h = din("rw", (D, E)); rbt_h = din("rbt", (128, E))
    w1_h = din("w1", (E, D, 2 * HID)); w2_h = din("w2", (E, HID, D))
    b1a_h = din("b1a", (E, 128, HJ)); b1g_h = din("b1g", (E, 128, HJ))
    b2pm_h = din("b2pm", (128, E, DC))
    tokv_h = din("tokv", (T, 8))
    sent8_h = din("sent8", (E * T, 8))
    etm1_h = din("etm1", (128, 1))
    onescf_h = din("onescf", (128, 1))
    xnscr_h = nc.dram_tensor("xnscr", [T + 128, D], F32, kind="Internal").ap()
    yscr_h = nc.dram_tensor("yscr", [T + 128, D], F32, kind="Internal").ap()
    tokf_h = nc.dram_tensor("tokf", [E * T, 8], F32, kind="Internal").ap()
    cos_h = din("cos2t", (64, T), BF16); sin_h = din("sin2st", (64, T), BF16)
    masks_h = din("masks", (4, 128, S), BF16)
    iota_h = din("iota8", (128, E)); identf_h = din("identf", (128, 128))
    ones_h = din("ones128", (128, 128), BF16)
    epsc_h = din("epsc", (128, 1))
    out_h = nc.dram_tensor("out", [B, T, D], F32, kind="ExternalOutput").ap()

    with ExitStack() as _ctx:
        def _sb(*a):
            return _ctx.enter_context(nc.sbuf_tensor(*a))
        def _ps(*a):
            return _ctx.enter_context(nc.psum_tensor(*a))
        def _sem(n):
            return _ctx.enter_context(nc.semaphore(n))
        xnT = _sb("g_xnT", [128, DC, T], BF16)
        acc = _sb("g_acc", [128, DC, T], F32)
        ohT = _sb("g_ohT", [8, T], BF16)
        iota8 = _sb("g_iota8", [128, E], F32)
        identf = _sb("g_identf", [128, 128], F32)
        ones128 = _sb("g_ones128", [128, 128], BF16)
        rbt = _sb("g_rbt", [128, E], F32)
        lngb = _sb("g_lngb", [128, 2, L], F32)
        b1a = _sb("g_b1a", [128, E, HJ], F32)
        b1g = _sb("g_b1g", [128, E, HJ], F32)
        rwbf = _sb("g_rwbf", [128, DC, E], BF16)
        etm1 = _sb("g_etm1", [128, 1], F32)
        onescol_f = _sb("g_onescf", [128, 1], F32)
        b2pm = _sb("g_b2pm", [128, E, DC], F32)
        st_a = _sb("g_st_a", [128, DC, 128], F32)
        st_2 = _sb("g_st_2", [128, D], F32)
        bf_a = _sb("g_bf_a", [128, 2, DC, 128], BF16)
        bf_g = _sb("g_bf_g", [128, 2, DC, 128], BF16)
        xcb = _sb("g_xcb", [128, 2, D], F32)
        stat = _sb("g_stat", [128, 12], F32)
        lgb = _sb("g_lgb", [128, E], F32)
        mx8 = _sb("g_mx8", [128, 8], F32)
        idx8 = _sb("g_idx8", [128, 8], U32)
        ohf = _sb("g_ohf", [128, E], F32)
        tiny = _sb("g_tiny", [1, 16], F32)
        tinyb = _sb("g_tinyb", [1, 2], BF16)
        negc = _sb("g_negc", [128, 1], F32)
        epsc = _sb("g_epsc", [128, 1], F32)
        psA = _ps("ps_A", [128, 2048], F32)
        psB = _ps("ps_B", [128, 2048], F32)
        h_ld = _sem("sldc"); h_pe = _sem("spe")
        h_dv = _sem("sdv"); h_ac = _sem("sac"); h_gp = _sem("sgp")
        h_la = _sem("slda"); h_l2 = _sem("sld2")
        h_lx0 = _sem("sldx0"); h_lx1 = _sem("sldx1")
        h_ld0 = _sem("sldd0"); h_ld1 = _sem("sldd1")
        h_st0 = _sem("sst0"); h_st1 = _sem("sst1")
        h_xn = _sem("sxn")
        h_gi = _sem("sgi")
        h_gd = _sem("sgd")
        block = _ctx.enter_context(nc.Block())

        p = P()
        sld = Sem(h_ld, "ldc"); spe = Sem(h_pe, "pe")
        sdv = Sem(h_dv, "dv"); sac = Sem(h_ac, "ac"); sgp = Sem(h_gp, "gp")
        slda = Sem(h_la, "lda"); sld2 = Sem(h_l2, "ld2")
        sldx = [Sem(h_lx0, "ldx0"), Sem(h_lx1, "ldx1")]
        sldd = [Sem(h_ld0, "ldd0"), Sem(h_ld1, "ldd1")]
        sstb = [Sem(h_st0, "st0"), Sem(h_st1, "st1")]
        sxn = Sem(h_xn, "xn")
        sgi = Sem(h_gi, "gi")
        sgd = Sem(h_gd, "gd")
        sq, t, v, sc, g = nc.sync, nc.tensor, nc.vector, nc.scalar, nc.gpsimd
        A = [psA[:, i * 512:(i + 1) * 512] for i in range(4)]
        Bp = [psB[:, i * 512:(i + 1) * 512] for i in range(4)]
        ALL_SEMS = (sld, spe, sdv, sac, sgp, slda, sld2, sldx[0], sldx[1],
                    sldd[0], sldd[1], sstb[0], sstb[1], sxn, sgi)
        ENGS = ((sq, "sq"), (t, "t"), (v, "v"), (sc, "sc"), (g, "g"))

        def ld(dst, src, sem=None):
            sem = sem or sld
            return p.inc(sq.dma_start(out=dst, in_=src), sem, 16)

        def mmacc(out, pairs, slots, extra=()):
            p.ps_acquire(t, slots)
            for (sem, tok) in extra:
                p.wait(t, "t", sem, tok)
            n = len(pairs)
            last = None
            for i_, (l_, r_) in enumerate(pairs):
                last = t.matmul(out, l_, r_, start=(i_ == 0), stop=(i_ == n - 1))
            return last

        def barrier():
            for eng, en in ENGS:
                for sem in ALL_SEMS:
                    p.wait(eng, en, sem, sem.n)

        def stage_cast(dst, src_h, stage, stage_name, ceng, cename, csem, dst_name=None):
            lsem = {"st_a": slda, "st_2": sld2}[stage_name]
            p.buf_wait(sq, "sq", stage_name)
            tok = ld(stage, src_h, lsem)
            p.wait(ceng, cename, lsem, tok)
            if dst_name:
                p.buf_wait(ceng, cename, dst_name)
            if ceng is sc:
                tc = p.inc(sc.copy(dst, stage), csem, 1)
            elif ceng is v:
                tc = p.inc(v.tensor_copy(dst, stage), csem, 1)
            else:
                tc = p.inc(g.tensor_copy(dst, stage), csem, 1)
            p.buf_use(stage_name, csem, tc)
            return tc

        ld(iota8[:], iota_h); ld(identf[:], identf_h)
        ld(ones128[:], ones_h)
        ld(etm1[:], etm1_h); ld(onescol_f[:], onescf_h)
        ld(b2pm[:], b2pm_h)
        ld(rbt[:], rbt_h)
        ld(epsc[:], epsc_h)
        ld(lngb[:], lngb_h.rearrange("o p q -> p o q"))
        ld(b1a[:], b1a_h.rearrange("e p j -> p e j"))
        ld(b1g[:], b1g_h.rearrange("e p j -> p e j"))
        stage_cast(rwbf[:], rw_h.rearrange("(c p) e -> p c e", p=128),
                   st_a[:, :, 0:E], "st_a", g, "g", sgp)
        v.memset(xcb[:, 0], 0.0)
        tvz = p.inc(v.drain(), sdv, 1)
        p.wait(sq, "sq", sdv, tvz)
        p.inc(sq.dma_start(out=xnscr_h[T:T + 128, :], in_=xcb[:, 0]), sld, 16)
        p.inc(sq.dma_start(out=yscr_h[T:T + 128, :], in_=xcb[:, 0]), sld, 16)
        barrier()

        for b in range(B):
            with ExitStack() as bctx:
                def _bb(nm, shape, dty):
                    return bctx.enter_context(nc.sbuf_tensor("b%d_%s" % (b, nm), shape, dty))
                dumpA = _bb("dumpA", [128, D], F32)
                wdkvbf = _bb("wdkv", [128, DC, L], BF16)
                wukbf = _bb("wuk", [128, LC, D], BF16)
                wukswbf = _bb("wuksw", [128, LC, D], BF16)
                wuvbf = _bb("wuv", [128, LC, D], BF16)
                latnT = _bb("latnT", [128, LC, T], BF16)
                kroT = _bb("kroT", [128, HC, T], BF16)
                qroT = _bb("qroT", [128, HC, T], BF16)
                v65 = _bb("v65", [128, TC, HPG, 65], BF16)
                catT = _bb("catT", [128, CPG, S], BF16)
                ropet = _bb("ropet", [128, 2, S], F32)
                attnT = _bb("attnT", [128, 2, S], BF16)
                recipb = _bb("recipb", [128, S], BF16)
                bc_sb = _bb("bc_sb", [64, S], BF16)
                masksb = _bb("masksb", [128, 4, S], BF16)
                cos2t = _bb("cos2t", [64, T], BF16)
                sin2st = _bb("sin2st", [64, T], BF16)
                _hold = {}

                prelude = []
                def _step_consts():
                    ld(masksb[:], masks_h.rearrange("o p q -> p o q"))
                    ld(cos2t[:], cos_h)
                    _hold["b_ld"] = ld(sin2st[:], sin_h)
                prelude.append(_step_consts)
                for lc2 in range(L // 128):
                    prelude.append(lambda lc2=lc2: stage_cast(
                        wdkvbf[:, :, lc2 * 128:(lc2 + 1) * 128],
                        wdkv_h[:, lc2 * 128:(lc2 + 1) * 128].rearrange("(c p) l -> p c l", p=128),
                        st_a[:], "st_a", g, "g", sgp))
                for (hsrc, dstw) in ((wuk_h, wukbf), (wuksw_h, wukswbf), (wuv_h, wuvbf)):
                    for lc in range(LC):
                        prelude.append(lambda hsrc=hsrc, dstw=dstw, lc=lc: stage_cast(
                            dstw[:, lc, :], hsrc[lc * 128:(lc + 1) * 128, :],
                            st_2[:], "st_2", g, "g", sgp))
                def _step_ones():
                    v.memset(v65[:, :, :, 64:65], 1.0)
                    p.inc(v.drain(), sdv, 1)
                prelude.append(_step_ones)

                for i in range(TC):
                    bi = i % 2
                    xn_ = "xcb%d" % bi
                    p.buf_wait(sq, "sq", xn_)
                    tokx = ld(xcb[:, bi], x_h[b, i * 128:(i + 1) * 128, :], sldx[bi])
                    if prelude:
                        prelude.pop(0)()
                    p.wait(v, "v", sldx[bi], tokx)
                    v.reduce_sum(stat[:, 0:1], xcb[:, bi], axis=AX.X)
                    v.drain()
                    v.tensor_scalar(stat[:, 1:2], stat[:, 0:1], 1.0 / D, None, op0=ALU.mult)
                    v.drain()
                    tv = p.inc(v.tensor_scalar(xcb[:, bi], xcb[:, bi], stat[:, 1:2], None,
                                               op0=ALU.subtract), sdv, 1)
                    p.wait(sc, "sc", sdv, tv)
                    p.buf_wait(sc, "sc", "dumpA")
                    sc.activation(dumpA[:], xcb[:, bi], AF.Square, accum_out=stat[:, 2:3])
                    sc.drain()
                    sc.activation(stat[:, 3:4], stat[:, 2:3], AF.Ln, bias=epsc[:], scale=1.0 / D)
                    sc.drain()
                    ta = p.inc(sc.activation(stat[:, 4:5], stat[:, 3:4], AF.Exp, scale=-0.5), sac, 1)
                    p.buf_use("dumpA", sac, ta)
                    p.wait(v, "v", sac, ta)
                    v.drain()
                    tv2 = p.inc(v.tensor_scalar(xcb[:, bi], xcb[:, bi], stat[:, 4:5], None,
                                                op0=ALU.mult), sdv, 1)
                    if not DBG["no_xnscr"]:
                        p.wait(sc, "sc", sdv, tv2)
                        p.wait(sc, "sc", sxn, sxn.n)
                        txs = p.inc(sc.dma_start(out=xnscr_h[i * 128:(i + 1) * 128, :],
                                                 in_=xcb[:, bi]), sxn, 16)
                        p.buf_use("xcb%d" % bi, sxn, txs)
                    p.ps_acquire(t, ["A0", "A1", "A2", "A3"])
                    p.wait(t, "t", sdv, tv2)
                    ti = None
                    for c in range(DC):
                        ti = t.transpose(psA[:, c * 256:c * 256 + 128],
                                         xcb[:, bi, c * 128:(c + 1) * 128], identf[:])
                    tpe = p.inc(ti, spe, 1)
                    p.wait(v, "v", spe, tpe)
                    last = None
                    for c in range(DC):
                        last = v.tensor_copy(xnT[:, c, i * 128:(i + 1) * 128],
                                             psA[:, c * 256:c * 256 + 128])
                    tv3 = p.inc(last, sdv, 1)
                    p.buf_use(xn_, sdv, tv3)
                    p.ps_release(["A0", "A1", "A2", "A3"], sdv, tv3)
                    last = mmacc(psB[:, 0:E],
                                 [(xnT[:, c, i * 128:(i + 1) * 128], rwbf[:, c, :]) for c in range(DC)],
                                 ["B0"], extra=[(sdv, tv3)])
                    tpe2 = p.inc(last, spe, 1)
                    p.wait(v, "v", spe, tpe2)
                    v.tensor_tensor(lgb[:], psB[:, 0:E], rbt[:], op=ALU.add)
                    v.drain()
                    v.max(mx8[:], lgb[:])
                    v.drain()
                    v.max_index(idx8[:], mx8[:], lgb[:])
                    v.drain()
                    v.tensor_copy(stat[:, 5:6], idx8[:, 0:1])
                    v.drain()
                    tv4 = p.inc(v.tensor_scalar(ohf[:], iota8[:], stat[:, 5:6], None,
                                                op0=ALU.is_equal), sdv, 1)
                    p.ps_release(["B0"], sdv, tv4)
                    p.ps_acquire(t, ["B1"])
                    p.wait(t, "t", sdv, tv4)
                    ti = t.transpose(psB[0:E, 512:640], ohf[:], identf[:])
                    tpe3 = p.inc(ti, spe, 1)
                    p.wait(v, "v", spe, tpe3)
                    tv5 = p.inc(v.tensor_copy(ohT[:, i * 128:(i + 1) * 128], psB[0:E, 512:640]), sdv, 1)
                    p.ps_release(["B1"], sdv, tv5)
                oh_v = sdv.n
                for st in prelude:
                    st()
                prelude = []
                b_ld = _hold["b_ld"]
                barrier()

                for i in range(TC):
                    last = mmacc(psB[:, 1024:1280],
                                 [(xnT[:, c, i * 128:(i + 1) * 128], wdkvbf[:, c, :]) for c in range(DC)],
                                 ["B2"], extra=[(sgp, sgp.n)])
                    tpe = p.inc(last, spe, 1)
                    p.wait(v, "v", spe, tpe)
                    p.buf_wait(v, "v", "xcb0")
                    v.reduce_sum(stat[:, 6:7], psB[:, 1024:1280], axis=AX.X)
                    v.drain()
                    v.tensor_scalar(stat[:, 7:8], stat[:, 6:7], 1.0 / L, None, op0=ALU.mult)
                    v.drain()
                    tv = p.inc(v.tensor_scalar(xcb[:, 0, 0:L], psB[:, 1024:1280], stat[:, 7:8], None,
                                               op0=ALU.subtract), sdv, 1)
                    p.ps_release(["B2"], sdv, tv)
                    p.wait(sc, "sc", sdv, tv)
                    sc.activation(ropet[:, 0, 0:L], xcb[:, 0, 0:L], AF.Square, accum_out=stat[:, 8:9])
                    sc.drain()
                    sc.activation(stat[:, 9:10], stat[:, 8:9], AF.Ln, bias=epsc[:], scale=1.0 / L)
                    sc.drain()
                    ta = p.inc(sc.activation(stat[:, 10:11], stat[:, 9:10], AF.Exp, scale=-0.5), sac, 1)
                    p.wait(v, "v", sac, ta)
                    v.drain()
                    v.tensor_scalar(xcb[:, 1, 0:L], xcb[:, 0, 0:L], stat[:, 10:11], None, op0=ALU.mult)
                    v.drain()
                    v.tensor_tensor(xcb[:, 0, 0:L], xcb[:, 1, 0:L], lngb[:, 0, :], op=ALU.mult)
                    v.drain()
                    tv2 = p.inc(v.tensor_tensor(xcb[:, 1, 0:L], xcb[:, 0, 0:L], lngb[:, 1, :],
                                                op=ALU.add), sdv, 1)
                    p.buf_use("xcb0", sdv, tv2)
                    p.buf_use("xcb1", sdv, tv2)
                    p.ps_acquire(t, ["B3"])
                    p.wait(t, "t", sdv, tv2)
                    ti = None
                    for lc in range(LC):
                        ti = t.transpose(psB[:, 1536 + lc * 128:1536 + (lc + 1) * 128],
                                         xcb[:, 1, lc * 128:(lc + 1) * 128], identf[:])
                    tpe2 = p.inc(ti, spe, 1)
                    p.wait(v, "v", spe, tpe2)
                    last = None
                    for lc in range(LC):
                        last = v.tensor_copy(latnT[:, lc, i * 128:(i + 1) * 128],
                                             psB[:, 1536 + lc * 128:1536 + (lc + 1) * 128])
                    tv3 = p.inc(last, sdv, 1)
                    p.ps_release(["B3"], sdv, tv3)
                latn_v = sdv.n

                for grp in range(NG):
                    h0 = grp * HPG
                    for cc in range(HC):
                        c0 = (h0 + 2 * cc) * HD
                        bufq = cc % 2
                        tca = stage_cast(bf_a[:, bufq],
                                         wq_h[:, c0:c0 + 128].rearrange("(c p) m -> p c m", p=128),
                                         st_a[:], "st_a",
                                         g, "g", sgp, dst_name="bf_a%d" % bufq)
                        tcg = stage_cast(bf_g[:, bufq],
                                         wqsw_h[:, c0:c0 + 128].rearrange("(c p) m -> p c m", p=128),
                                         st_2[:].rearrange("p (c m) -> p c m", m=128), "st_2",
                                         g, "g", sgp, dst_name="bf_g%d" % bufq)
                        for par in range(2):
                            hh = h0 + 2 * cc + par
                            for s_ in range(NS):
                                mmacc(A[0][0:64, :],
                                      [(wukbf[:, lc, hh * HD:(hh + 1) * HD],
                                        latnT[:, lc, s_ * S:(s_ + 1) * S]) for lc in range(LC)],
                                      ["A0"], extra=[(sdv, latn_v)])
                                last = mmacc(A[1][0:64, :],
                                             [(wukswbf[:, lc, hh * HD:(hh + 1) * HD],
                                               latnT[:, lc, s_ * S:(s_ + 1) * S]) for lc in range(LC)],
                                             ["A1"])
                                tpe_k = p.inc(last, spe, 1)
                                mmacc(A[2][0:64, :],
                                      [(bf_a[:, bufq, c_, par * 64:par * 64 + 64],
                                        xnT[:, c_, s_ * S:(s_ + 1) * S]) for c_ in range(DC)],
                                      ["A2"], extra=[(sgp, max(tca, tcg))])
                                last = mmacc(A[3][0:64, :],
                                             [(bf_g[:, bufq, c_, par * 64:par * 64 + 64],
                                               xnT[:, c_, s_ * S:(s_ + 1) * S]) for c_ in range(DC)],
                                             ["A3"])
                                tpe_q = p.inc(last, spe, 1)
                                p.buf_use("bf_a%d" % bufq, spe, tpe_q)
                                p.buf_use("bf_g%d" % bufq, spe, tpe_q)
                                p.wait(v, "v", spe, tpe_k)
                                p.wait(v, "v", sld, b_ld)
                                p.buf_wait(v, "v", "kroT")
                                v.drain()
                                v.tensor_tensor(ropet[0:64, 0, :], A[0][0:64, :],
                                                cos2t[:, s_ * S:(s_ + 1) * S], op=ALU.mult)
                                v.tensor_tensor(ropet[0:64, 1, :], A[1][0:64, :],
                                                sin2st[:, s_ * S:(s_ + 1) * S], op=ALU.mult)
                                v.drain()
                                tv = p.inc(v.tensor_tensor(
                                    kroT[par * 64:par * 64 + 64, cc, s_ * S:(s_ + 1) * S],
                                    ropet[0:64, 0, :], ropet[0:64, 1, :], op=ALU.add), sdv, 1)
                                p.ps_release(["A0", "A1"], sdv, tv)
                                p.wait(v, "v", spe, tpe_q)
                                p.buf_wait(v, "v", "qroT")
                                v.drain()
                                v.tensor_tensor(ropet[0:64, 0, :], A[2][0:64, :],
                                                cos2t[:, s_ * S:(s_ + 1) * S], op=ALU.mult)
                                v.tensor_tensor(ropet[0:64, 1, :], A[3][0:64, :],
                                                sin2st[:, s_ * S:(s_ + 1) * S], op=ALU.mult)
                                v.drain()
                                tv = p.inc(v.tensor_tensor(
                                    qroT[par * 64:par * 64 + 64, cc, s_ * S:(s_ + 1) * S],
                                    ropet[0:64, 0, :], ropet[0:64, 1, :], op=ALU.add), sdv, 1)
                                p.ps_release(["A2", "A3"], sdv, tv)
                    kq_v = sdv.n

                    for i in range(TC):
                        p.buf_wait(v, "v", "v65")
                        last = mmacc(A[0][:, 0:VW],
                                     [(latnT[:, lc, i * 128:(i + 1) * 128],
                                       wuvbf[:, lc, h0 * HD:h0 * HD + VW]) for lc in range(LC)],
                                     ["A0"])
                        tpe = p.inc(last, spe, 1)
                        p.wait(v, "v", spe, tpe)
                        tv = p.inc(v.tensor_copy(v65[:, i, :, 0:64],
                                                 A[0][:, 0:VW].rearrange("p (h d) -> p h d", d=64)),
                                   sdv, 1)
                        p.ps_release(["A0"], sdv, tv)
                    v65_v = sdv.n

                    first = True
                    units = [(s_, par, cc) for s_ in range(NS) for par in range(2)
                             for cc in range(HC)]
                    for g0 in range(0, len(units), 4):
                        grp_u = units[g0:g0 + 4]
                        toks = []
                        for ui, (s_, par, cc) in enumerate(grp_u):
                            sb2 = ui % 2
                            p.buf_wait(sc, "sc", "attn%d" % sb2)
                            p.wait(sc, "sc", sdv, kq_v)
                            tq = p.inc(sc.activation(attnT[par * 64:par * 64 + 64, sb2, :],
                                                     kroT[par * 64:par * 64 + 64, cc,
                                                          s_ * S:(s_ + 1) * S],
                                                     AF.Square), sac, 1)
                            last = mmacc(Bp[ui][0:1, :],
                                         [(ones128[par * 64:par * 64 + 64, 0:1],
                                           attnT[par * 64:par * 64 + 64, sb2, :])],
                                         ["B%d" % ui], extra=[(sac, tq)])
                            tpe = p.inc(last, spe, 1)
                            p.buf_use("attn%d" % sb2, spe, tpe)
                        p.wait(v, "v", spe, spe.n)
                        v.reduce_max(tiny[0:1, 1:2], psB[0:1, 0:512 * len(grp_u)], axis=AX.X)
                        v.drain()
                        if first:
                            v.tensor_copy(tiny[0:1, 0:1], tiny[0:1, 1:2])
                            first = False
                        else:
                            v.tensor_tensor(tiny[0:1, 0:1], tiny[0:1, 0:1], tiny[0:1, 1:2],
                                            op=ALU.max)
                        tvm = p.inc(v.drain(), sdv, 1)
                        p.ps_release(["B0", "B1", "B2", "B3"], sdv, tvm)
                    mk_v = sdv.n

                    for s_ in range(NS):
                        units_q = [(par, cc) for par in range(2) for cc in range(HC)]
                        for g0 in range(0, len(units_q), 4):
                            grp_u = units_q[g0:g0 + 4]
                            for ui, (par, cc) in enumerate(grp_u):
                                sb2 = ui % 2
                                p.buf_wait(sc, "sc", "attn%d" % sb2)
                                tq = p.inc(sc.activation(attnT[par * 64:par * 64 + 64, sb2, :],
                                                         qroT[par * 64:par * 64 + 64, cc,
                                                              s_ * S:(s_ + 1) * S],
                                                         AF.Square), sac, 1)
                                last = mmacc(Bp[ui][0:1, :],
                                             [(ones128[par * 64:par * 64 + 64, 0:1],
                                               attnT[par * 64:par * 64 + 64, sb2, :])],
                                             ["B%d" % ui], extra=[(sac, tq)])
                                tpe = p.inc(last, spe, 1)
                                p.buf_use("attn%d" % sb2, spe, tpe)
                            p.wait(v, "v", spe, spe.n)
                            v.reduce_max(tiny[0:1, 2:3], psB[0:1, 0:512 * len(grp_u)], axis=AX.X)
                            v.drain()
                            if g0 == 0:
                                v.tensor_copy(tiny[0:1, 3:4], tiny[0:1, 2:3])
                            else:
                                v.tensor_tensor(tiny[0:1, 3:4], tiny[0:1, 3:4], tiny[0:1, 2:3],
                                                op=ALU.max)
                            tvq = p.inc(v.drain(), sdv, 1)
                            p.ps_release(["B0", "B1", "B2", "B3"], sdv, tvq)
                        p.wait(v, "v", sdv, mk_v)
                        tv = p.inc(v.tensor_tensor(tiny[0:1, 4:5], tiny[0:1, 3:4], tiny[0:1, 0:1],
                                                   op=ALU.mult), sdv, 1)
                        p.wait(sc, "sc", sdv, tv)
                        sc.activation(tiny[0:1, 5:6], tiny[0:1, 4:5], AF.Ln)
                        sc.drain()
                        ta = p.inc(sc.activation(tiny[0:1, 6:7], tiny[0:1, 5:6], AF.Exp, scale=0.5),
                                   sac, 1)
                        p.wait(v, "v", sac, ta)
                        v.tensor_scalar(tiny[0:1, 7:8], tiny[0:1, 6:7], -0.125, -2.0,
                                        op0=ALU.mult, op1=ALU.add)
                        v.drain()
                        tv = p.inc(v.tensor_copy(tinyb[0:1, 0:1], tiny[0:1, 7:8]), sdv, 1)
                        p.buf_wait(v, "v", "negc")
                        last = mmacc(psB[:, 1024:1025], [(ones128[0:1, :], tinyb[0:1, 0:1])],
                                     ["B2"], extra=[(sdv, tv)])
                        tpe = p.inc(last, spe, 1)
                        p.wait(v, "v", spe, tpe)
                        tnegc = p.inc(v.tensor_copy(negc[:, 0:1], psB[:, 1024:1025]), sdv, 1)
                        p.ps_release(["B2"], sdv, tnegc)

                        nkt = 4 * s_ + 4
                        for lh in range(HPG):
                            cc = lh // 2; par = lh % 2
                            ob = [0, 1, 3][lh % 3]
                            o65 = Bp[ob][0:65, :]
                            p.ps_acquire(t, ["B%d" % ob])
                            def emit_score(kt):
                                sb_ = kt % 2
                                p.ps_acquire(t, ["A%d" % sb_])
                                p.wait(t, "t", sdv, kq_v)
                                tmm = t.matmul(A[sb_],
                                               kroT[par * 64:par * 64 + 64, cc, kt * 128:(kt + 1) * 128],
                                               qroT[par * 64:par * 64 + 64, cc, s_ * S:(s_ + 1) * S],
                                               start=True, stop=True)
                                return p.inc(tmm, spe, 1)

                            tpe_s = emit_score(0)
                            for kt in range(nkt):
                                sb_ = kt % 2
                                stp = A[sb_]
                                abuf = attnT[:, sb_, :]
                                tpe = tpe_s
                                p.buf_wait(sc, "sc", "attn%d" % sb_)
                                p.wait(sc, "sc", sdv, tnegc)
                                p.wait(sc, "sc", spe, tpe)
                                tex = p.inc(sc.activation(abuf, stp, AF.Exp,
                                                          bias=negc[:, 0:1], scale=0.125), sac, 1)
                                p.buf_use("negc", sac, tex)
                                p.ps_release(["A%d" % sb_], sac, tex)
                                if kt + 1 < nkt:
                                    tpe_s = emit_score(kt + 1)
                                off = kt - 4 * s_
                                if off >= 0:
                                    p.wait(v, "v", sac, tex)
                                    tex2 = p.inc(v.tensor_tensor(abuf, abuf, masksb[:, off, :],
                                                                 op=ALU.mult), sdv, 1)
                                    p.wait(t, "t", sdv, tex2)
                                else:
                                    p.wait(t, "t", sac, tex)
                                p.wait(t, "t", sdv, v65_v)
                                last = t.matmul(o65, v65[:, kt, lh, :], abuf,
                                                start=(kt == 0), stop=(kt == nkt - 1))
                                tpe2 = p.inc(last, spe, 1)
                                p.buf_use("attn%d" % sb_, spe, tpe2)
                                p.buf_use("v65", spe, tpe2)
                            p.wait(sc, "sc", spe, spe.n)
                            p.buf_wait(sc, "sc", "recipb")
                            sc.activation(ropet[64:65, 0, :], o65[64:65, :], AF.Ln)
                            sc.drain()
                            trc = p.inc(sc.activation(recipb[64:65, :], ropet[64:65, 0, :],
                                                      AF.Exp, scale=-1.0), sac, 1)
                            last = mmacc(psB[0:64, 1024:1536],
                                         [(ones128[64:65, 0:64], recipb[64:65, :])],
                                         ["B2"], extra=[(sac, trc)])
                            tpe3 = p.inc(last, spe, 1)
                            p.buf_use("recipb", spe, tpe3)
                            p.wait(v, "v", spe, tpe3)
                            v.tensor_copy(bc_sb[:], psB[0:64, 1024:1536])
                            v.drain()
                            p.buf_wait(v, "v", "catT")
                            tv = p.inc(v.tensor_tensor(catT[par * 64:par * 64 + 64, cc, :],
                                                       o65[0:64, :], bc_sb[:], op=ALU.mult), sdv, 1)
                            p.ps_release(["B2"], sdv, tv)
                            p.ps_release(["B%d" % ob], sdv, tv)
                        cat_v = sdv.n

                        wo_tgt = [("bf_a0", bf_a[:, 0]), ("bf_g0", bf_g[:, 0])]
                        for ccw in range(CPG):
                            r0 = h0 * HD + ccw * 128
                            nm, tgt = wo_tgt[ccw % 2]
                            stage_cast(tgt.rearrange("p c m -> p (c m)"), wo_h[r0:r0 + 128, :],
                                       st_2[:], "st_2", g, "g", sgp, dst_name=nm)
                        wo_g = sgp.n
                        for dch in range(DC):
                            wsl = "A%d" % (2 + dch % 2)
                            wop = A[2 + dch % 2]
                            last = mmacc(wop,
                                         [(wo_tgt[ccw % 2][1][:, dch, :], catT[:, ccw, :])
                                          for ccw in range(CPG)],
                                         [wsl], extra=[(sgp, wo_g), (sdv, cat_v)])
                            tpe = p.inc(last, spe, 1)
                            p.buf_use("bf_a0", spe, tpe)
                            p.buf_use("bf_g0", spe, tpe)
                            p.buf_use("catT", spe, tpe)
                            p.wait(v, "v", spe, tpe)
                            dst = acc[:, dch, s_ * S:(s_ + 1) * S]
                            if grp == 0:
                                tv = p.inc(v.tensor_copy(dst, wop), sdv, 1)
                            else:
                                tv = p.inc(v.tensor_tensor(dst, dst, wop, op=ALU.add), sdv, 1)
                            p.ps_release([wsl], sdv, tv)
            barrier()

            with ExitStack() as cctx:
                def _cb(nm, shape, dty):
                    return cctx.enter_context(nc.sbuf_tensor("c%d_%s" % (b, nm), shape, dty))
                ubuf = _cb("ubuf", [128, 2, UJG, S], BF16)
                w2g = _cb("w2g", [128, UJG, D], BF16)
                smb = _cb("smb", [128, 2, S], BF16)
                cumsb = _cb("cumsb", [8, S], F32)
                sidx = _cb("sidx", [128, 4], U32)
                tokvs = _cb("tokvs", [128, 4, 8], F32)
                tokw = _cb("tokw", [128, 2, 4, 8], F32)
                tidx = _cb("tidx", [128, 2, 4], U32)
                xgs = _cb("xgs", [128, D], F32)
                xgT = _cb("xgT", [128, 2, DC, S], BF16)
                ySBq = _cb("ySBq", [128, DC, S], F32)
                ytok = _cb("ytok", [128, D], F32)
                slotb = ySBq[0:1, 0, 0:S]

                tok_sent = p.inc(sq.dma_start(out=tokf_h, in_=sent8_h), sld, 16)

                for s_ in ([] if DBG["skip_slot"] else range(NS)):
                    p.wait(v, "v", sdv, oh_v)
                    v.drain()
                    if s_ == 0:
                        v.tensor_tensor_scan(cumsb[:], ohT[:, s_ * S:(s_ + 1) * S],
                                             ohT[:, s_ * S:(s_ + 1) * S], 0.0,
                                             op0=ALU.add, op1=ALU.bypass)
                    else:
                        v.tensor_tensor_scan(cumsb[:], ohT[:, s_ * S:(s_ + 1) * S],
                                             ohT[:, s_ * S:(s_ + 1) * S], stat[0:8, 11:12],
                                             op0=ALU.add, op1=ALU.bypass)
                    v.drain()
                    v.tensor_copy(stat[0:8, 11:12], cumsb[:, S - 1:S])
                    v.drain()
                    v.tensor_scalar(cumsb[:], cumsb[:], etm1[0:8, :], None, op0=ALU.add)
                    v.drain()
                    tv = p.inc(v.tensor_tensor(cumsb[:], cumsb[:], ohT[:, s_ * S:(s_ + 1) * S],
                                               op=ALU.mult), sdv, 1)
                    last = mmacc(Bp[s_ % 4][0:1, :], [(onescol_f[0:8, :], cumsb[:])],
                                 ["B%d" % (s_ % 4)], extra=[(sdv, tv)])
                    tpe = p.inc(last, spe, 1)
                    p.wait(v, "v", spe, tpe)
                    p.buf_wait(v, "v", "slotb")
                    p.buf_wait(v, "v", "ySBq")
                    v.drain()
                    tv2 = p.inc(v.tensor_copy(slotb, Bp[s_ % 4][0:1, :]), sdv, 1)
                    p.buf_use("slotb", sdv, tv2)
                    p.ps_release(["B%d" % (s_ % 4)], sdv, tv2)
                    p.buf_wait(sq, "sq", "tokvs")
                    tokld = ld(tokvs[:], tokv_h[s_ * S:(s_ + 1) * S, :]
                               .rearrange("(c p) w -> p c w", p=128), sld2)
                    p.buf_wait(v, "v", "sidx")
                    for cpart in range(4):
                        last = mmacc(psA[:, 0:1], [(ySBq[0:1, 0, cpart * 128:(cpart + 1) * 128],
                                                    onescol_f[0:1, :])],
                                     ["A0"], extra=[(sdv, tv2)])
                        tpe2 = p.inc(last, spe, 1)
                        p.buf_use("slotb", spe, tpe2)
                        p.buf_use("ySBq", spe, tpe2)
                        p.wait(v, "v", spe, tpe2)
                        v.drain()
                        tv3 = p.inc(v.tensor_copy(sidx[:, cpart:cpart + 1], psA[:, 0:1]), sdv, 1)
                        p.ps_release(["A0"], sdv, tv3)
                    p.wait(g, "g", sld, tok_sent)
                    p.wait(g, "g", sld2, tokld)
                    p.wait(g, "g", sdv, sdv.n)
                    p.wait(g, "g", sxn, sxn.n)
                    for cpart in range(4):
                        if DBG["plain_tok"]:
                            g.dma_start(out=tokf_h[(s_ * 4 + cpart) * 128:(s_ * 4 + cpart + 1) * 128, :],
                                        in_=tokvs[:, cpart]).then_inc(sgd.h, 16)
                        else:
                            g.indirect_dma_start(
                                out=tokf_h,
                                out_offset=bass.IndirectOffsetOnAxis(ap=sidx[:, cpart:cpart + 1], axis=0),
                                in_=tokvs[:, cpart], in_offset=None).then_inc(sgd.h, 16)
                    tsc = p.inc(g.drain(), sgi, 1)
                    p.buf_use("tokvs", sgi, tsc)
                    p.buf_use("sidx", sgi, tsc)
                tokf_done = sgi.n

                equs = [] if DBG["skip_eq"] else [(e, q) for e in range(E) for q in range(CAPQ)]
                for eqi, (e, q) in enumerate(equs):
                    pb = eqi % 2
                    p.wait(sq, "sq", sgi, tokf_done)
                    p.buf_wait(sq, "sq", "tokw%d" % pb)
                    tokld = ld(tokw[:, pb], tokf_h[e * T + q * S:e * T + (q + 1) * S, :]
                               .rearrange("(c p) w -> p c w", p=128), sldx[pb])
                    p.wait(g, "g", sldx[pb], tokld)
                    p.wait(g, "g", sgi, tokf_done)
                    p.buf_wait(g, "g", "tidx%d" % pb)
                    tgi = p.inc(g.tensor_copy(tidx[:, pb], tokw[:, pb, :, 0]), sgp, 1)
                    p.buf_use("tokw%d" % pb, sgp, tgi)
                    for cpart in ([] if DBG["eq_noxg"] else range(4)):
                        p.buf_wait(g, "g", "xgs")
                        p.wait(g, "g", sxn, sxn.n)
                        p.wait(g, "g", sgp, tgi)
                        if DBG["plain_gather"]:
                            g.dma_start(out=xgs[:],
                                        in_=xnscr_h[cpart * 128:(cpart + 1) * 128, :]).then_inc(sgd.h, 16)
                        else:
                            g.indirect_dma_start(
                                out=xgs[:], out_offset=None,
                                in_=xnscr_h, in_offset=bass.IndirectOffsetOnAxis(
                                    ap=tidx[:, pb, cpart:cpart + 1], axis=0)).then_inc(sgd.h, 16)
                        tgg = p.inc(g.drain(), sgi, 1)
                        p.buf_use("tidx%d" % pb, sgi, tgg)
                        for c0 in range(0, DC, 4):
                            cs = list(range(c0, min(c0 + 4, DC)))
                            p.ps_acquire(t, ["A0", "A1"])
                            p.wait(t, "t", sgi, tgg)
                            ti = None
                            for c in cs:
                                ti = t.transpose(psA[:, (c % 4) * 256:(c % 4) * 256 + 128],
                                                 xgs[:, c * 128:(c + 1) * 128], identf[:])
                            tpe = p.inc(ti, spe, 1)
                            p.buf_use("xgs", spe, tpe)
                            p.wait(v, "v", spe, tpe)
                            p.buf_wait(v, "v", "xgT%d" % pb)
                            last = None
                            for c2 in cs:
                                last = v.tensor_copy(
                                    xgT[:, pb, c2, cpart * 128:(cpart + 1) * 128],
                                    psA[:, (c2 % 4) * 256:(c2 % 4) * 256 + 128])
                            tvx = p.inc(last, sdv, 1)
                            p.ps_release(["A0", "A1"], sdv, tvx)
                    xg_v = sdv.n
                    p.buf_use("xgT%d" % pb, sdv, xg_v)

                    for jg in range(NJG):
                        ub = jg % 2
                        for j8 in range(UJG):
                            j = jg * UJG + j8
                            wbuf = j % 2
                            tca = stage_cast(bf_a[:, wbuf],
                                             w1_h[e, :, j * 128:(j + 1) * 128]
                                             .rearrange("(c p) m -> p c m", p=128),
                                             st_a[:], "st_a",
                                             sc, "sc", sac, dst_name="bf_a%d" % wbuf)
                            xn_ = "xcb%d" % wbuf
                            p.buf_wait(sq, "sq", xn_)
                            tg_ = ld(xcb[:, wbuf].rearrange("p (c m) -> p c m", m=128),
                                     w1_h[e, :, HID + j * 128:HID + (j + 1) * 128]
                                     .rearrange("(c p) m -> p c m", p=128), sldd[wbuf])
                            p.buf_wait(v, "v", "bf_g%d" % wbuf)
                            p.wait(v, "v", sldd[wbuf], tg_)
                            tcg = p.inc(v.tensor_copy(bf_g[:, wbuf].rearrange("p c m -> p (c m)"),
                                                      xcb[:, wbuf]), sdv, 1)
                            p.buf_use(xn_, sdv, tcg)
                            tc2 = stage_cast(w2g[:, j8, :], w2_h[e, j * 128:(j + 1) * 128, :],
                                             st_2[:], "st_2", g, "g", sgp, dst_name="w2g%d" % j8)
                            if DBG["eq_nou"]:
                                continue
                            us = j % 2
                            apn = psA[:, us * 1024:us * 1024 + 512]
                            gpn = psA[:, us * 1024 + 512:us * 1024 + 1024]
                            sla = "A%d" % (us * 2); slg = "A%d" % (us * 2 + 1)
                            mmacc(apn, [(bf_a[:, wbuf, c, :], xgT[:, pb, c, :])
                                        for c in range(DC)], [sla],
                                  extra=[(sac, tca), (sdv, xg_v)])
                            last = mmacc(gpn, [(bf_g[:, wbuf, c, :], xgT[:, pb, c, :])
                                               for c in range(DC)], [slg], extra=[(sdv, tcg)])
                            tpe = p.inc(last, spe, 1)
                            p.buf_use("bf_a%d" % wbuf, spe, tpe)
                            p.buf_use("bf_g%d" % wbuf, spe, tpe)
                            p.buf_use("xgT%d" % pb, spe, tpe)
                            p.buf_wait(sc, "sc", "smb%d" % us)
                            p.wait(sc, "sc", spe, tpe)
                            tsg = p.inc(sc.activation(smb[:, us, :], gpn, AF.Sigmoid,
                                                      bias=b1g[:, e, j:j + 1]), sac, 1)
                            p.wait(v, "v", sac, tsg)
                            p.buf_wait(v, "v", "ubuf%d" % ub)
                            tvu = p.inc(v.scalar_tensor_tensor(ubuf[:, ub, j8, :],
                                                               apn, b1a[:, e, j:j + 1],
                                                               smb[:, us, :],
                                                               op0=ALU.add, op1=ALU.mult), sdv, 1)
                            p.buf_use("smb%d" % us, sdv, tvu)
                            p.ps_release([sla, slg], sdv, tvu)
                        u_v = sdv.n
                        w2_ready = sgp.n
                        for dch in ([] if DBG["eq_noy"] else range(DC)):
                            ys = dch % 2
                            yp = Bp[ys]
                            p.ps_acquire(t, ["B%d" % ys])
                            p.wait(t, "t", sdv, u_v)
                            p.wait(t, "t", sgp, w2_ready)
                            last = None
                            for j8 in range(UJG):
                                last = t.matmul(yp, w2g[:, j8, dch * 128:(dch + 1) * 128],
                                                ubuf[:, ub, j8, :],
                                                start=(j8 == 0), stop=(j8 == UJG - 1))
                            tpe = p.inc(last, spe, 1)
                            for j8 in range(UJG):
                                p.buf_use("w2g%d" % j8, spe, tpe)
                            p.buf_use("ubuf%d" % ub, spe, tpe)
                            p.wait(v, "v", spe, tpe)
                            p.buf_wait(v, "v", "ySBq")
                            if jg == 0:
                                tv = p.inc(v.tensor_scalar(ySBq[:, dch, :], yp,
                                                           b2pm[:, e, dch:dch + 1], None,
                                                           op0=ALU.add), sdv, 1)
                            else:
                                tv = p.inc(v.tensor_tensor(ySBq[:, dch, :], ySBq[:, dch, :],
                                                           yp, op=ALU.add), sdv, 1)
                            p.ps_release(["B%d" % ys], sdv, tv)
                    y_v = sdv.n

                    for cpart in ([] if DBG["eq_noflush"] else range(4)):
                        p.buf_wait(v, "v", "ytok")
                        p.ps_acquire(t, ["A0", "A1", "A2", "A3"])
                        p.wait(t, "t", sdv, y_v)
                        ti = None
                        for c in range(DC):
                            ti = t.transpose(psA[:, c * 256:c * 256 + 128],
                                             ySBq[:, c, cpart * 128:(cpart + 1) * 128],
                                             identf[:])
                        tpe = p.inc(ti, spe, 1)
                        p.buf_use("ySBq", spe, tpe)
                        p.wait(v, "v", spe, tpe)
                        last = None
                        for c in range(DC):
                            last = v.tensor_copy(ytok[:, c * 128:(c + 1) * 128],
                                                 psA[:, c * 256:c * 256 + 128])
                        tvy = p.inc(last, sdv, 1)
                        p.ps_release(["A0", "A1", "A2", "A3"], sdv, tvy)
                        p.wait(g, "g", sdv, tvy)
                        p.wait(g, "g", sgp, tgi)
                        if DBG["plain_yscat"]:
                            g.dma_start(out=yscr_h[cpart * 128:(cpart + 1) * 128, :],
                                        in_=ytok[:]).then_inc(sgd.h, 16)
                        else:
                            g.indirect_dma_start(
                                out=yscr_h, out_offset=bass.IndirectOffsetOnAxis(
                                    ap=tidx[:, pb, cpart:cpart + 1], axis=0),
                                in_=ytok[:], in_offset=None).then_inc(sgd.h, 16)
                        tsy = p.inc(g.drain(), sgi, 1)
                        p.buf_use("ytok", sgi, tsy)
                        p.buf_use("tidx%d" % pb, sgi, tsy)
                yscat_done = sgi.n
            acc_done = sdv.n
            barrier()

            with ExitStack() as dctx:
                xtD = dctx.enter_context(nc.sbuf_tensor("d%d_xt" % b, [128, 2, D], F32))
                ytD = dctx.enter_context(nc.sbuf_tensor("d%d_yt" % b, [128, 2, D], F32))
                for i in range(TC):
                    bi = i % 2
                    p.buf_wait(sq, "sq", "xcb%d" % bi)
                    p.buf_wait(sq, "sq", "xtD%d" % bi)
                    p.buf_wait(sq, "sq", "ytD%d" % bi)
                    tokx = ld(xtD[:, bi], x_h[b, i * 128:(i + 1) * 128, :], sldd[bi])
                    toky = ld(ytD[:, bi], yscr_h[i * 128:(i + 1) * 128, :], sldx[bi])
                    p.ps_acquire(t, ["A0", "A1", "A2", "A3"])
                    p.wait(t, "t", sdv, acc_done)
                    ti = None
                    for c in range(DC):
                        ti = t.transpose(psA[:, c * 256:c * 256 + 128],
                                         acc[:, c, i * 128:(i + 1) * 128], identf[:])
                    tpe = p.inc(ti, spe, 1)
                    p.wait(v, "v", spe, tpe)
                    p.wait(v, "v", sldd[bi], tokx)
                    p.wait(v, "v", sldx[bi], toky)
                    for c in range(DC):
                        v.tensor_tensor(xcb[:, bi, c * 128:(c + 1) * 128],
                                        psA[:, c * 256:c * 256 + 128],
                                        xtD[:, bi, c * 128:(c + 1) * 128], op=ALU.add)
                    v.drain()
                    tv = p.inc(v.tensor_tensor(xcb[:, bi], xcb[:, bi], ytD[:, bi],
                                               op=ALU.add), sdv, 1)
                    p.buf_use("xtD%d" % bi, sdv, tv)
                    p.buf_use("ytD%d" % bi, sdv, tv)
                    p.ps_release(["A0", "A1", "A2", "A3"], sdv, tv)
                    p.wait(sc, "sc", sdv, tv)
                    tst = p.inc(sc.dma_start(out=out_h[b, i * 128:(i + 1) * 128, :],
                                             in_=xcb[:, bi]), sstb[bi], 16)
                    p.buf_use("xcb%d" % bi, sstb[bi], tst)
            barrier()

        sc.wait_ge(sstb[0].h, sstb[0].n)
        sc.wait_ge(sstb[1].h, sstb[1].n)
        sq.wait_ge(sstb[0].h, sstb[0].n)
        sq.wait_ge(sstb[1].h, sstb[1].n)
    return nc



def kernel(**inputs):
    cfg = REAL
    shared = host_prep(cfg, inputs)
    x = np.ascontiguousarray(np.asarray(inputs["L0_attn_x"], dtype=np.float32))
    n_cores = 8
    per = x.shape[0] // n_cores
    nc = build_program(cfg)
    in_maps = []
    for c in range(n_cores):
        m = dict(shared)
        m["x"] = x[c * per:(c + 1) * per]
        in_maps.append(m)
    res = run_bass_kernel_spmd(nc, in_maps, core_ids=list(range(n_cores)))
    return np.concatenate([r["out"] for r in res.results], axis=0)


# version 11
# speedup vs baseline: 1.5400x; 1.2162x over previous
import numpy as np
import ml_dtypes
import concourse.bass as bass
import concourse.mybir as mybir
from concourse.bass_utils import run_bass_kernel_spmd

F32 = mybir.dt.float32
BF16 = mybir.dt.bfloat16
U32 = mybir.dt.uint32
AF = mybir.ActivationFunctionType
ALU = mybir.AluOpType
AX = mybir.AxisListType
BF = ml_dtypes.bfloat16

HD = 64
HALF = 32
S = 512


class CFG:
    def __init__(self, T=2048, D=1024, L=256, HID=4096, E=8, HPG=4, B=2):
        self.T, self.D, self.L, self.HID, self.E, self.HPG, self.B = T, D, L, HID, E, HPG, B
        self.HEADS = D // HD
        self.NG = self.HEADS // HPG
        self.DC = D // 128
        self.TC = T // 128
        self.NS = T // S
        self.LC = L // 128
        self.HJ = HID // 128
        self.UJG = min(8, self.HJ)
        self.NJG = self.HJ // self.UJG
        self.NH = max(1, T // 1024)
        self.HSZ = T // self.NH
        self.CPG = HPG * HD // 128
        self.CAPQ = 2
        assert D % 128 == 0 and T % S == 0 and L % 128 == 0
        assert HID % (128 * self.UJG) == 0
        assert self.HEADS % HPG == 0 and HPG % 2 == 0 and self.CPG >= 1
        assert HPG * HD <= 512


REAL = CFG()

DBG = {"plain_tok": False, "plain_gather": False, "plain_yscat": False, "skip_eq": False, "skip_slot": False, "no_xnscr": False, "eq_noxg": False, "eq_nou": False, "eq_noy": False, "eq_noflush": False}


def _swap_cols(w, heads):
    out = np.empty_like(w)
    for h in range(heads):
        b = h * HD
        out[:, b:b + HALF] = w[:, b + HALF:b + HD]
        out[:, b + HALF:b + HD] = w[:, b:b + HALF]
    return out


def host_consts(cfg):
    T = cfg.T
    pos = np.arange(T, dtype=np.float64)[:, None]
    inv = 1.0 / 10000.0 ** (np.arange(HALF, dtype=np.float64) / HALF)
    ang = pos * inv[None, :]
    cosv = np.cos(ang).astype(np.float32).T
    sinv = np.sin(ang).astype(np.float32).T
    cos2 = np.concatenate([cosv, cosv], 0)
    sin2 = np.concatenate([-sinv, sinv], 0)
    masks = np.zeros((4, 128, S), BF)
    qf = np.arange(S)[None, :]
    kp = np.arange(128)[:, None]
    for off in range(4):
        masks[off] = ((kp + off * 128) <= qf).astype(BF)
    iota8 = np.broadcast_to(np.arange(cfg.E, dtype=np.float32), (128, cfg.E)).copy()
    ident = np.eye(128, dtype=np.float32)
    ones128 = np.ones((128, 128), BF)
    epsc = np.full((128, 1), 1e-5, np.float32)
    T, E = cfg.T, cfg.E
    tokv = np.broadcast_to(np.arange(T, dtype=np.float32)[:, None], (T, 8)).copy()
    sent8 = np.broadcast_to((T + np.arange(E * T, dtype=np.float32) % 128)[:, None],
                            (E * T, 8)).copy()
    etm1 = np.zeros((128, 1), np.float32)
    etm1[0:E, 0] = np.arange(E, dtype=np.float32) * T - 1.0
    onescf = np.ones((128, 1), np.float32)
    return dict(cos2t=cos2.astype(BF), sin2st=sin2.astype(BF), masks=masks,
                iota8=iota8, identf=ident, ones128=ones128, epsc=epsc,
                tokv=tokv, sent8=sent8, etm1=etm1, onescf=onescf)


def host_prep(cfg, iv):
    E, L, HID = cfg.E, cfg.L, cfg.HID
    d = dict(host_consts(cfg))
    d["wq"] = np.ascontiguousarray(iv["L0_attn_W_q"], dtype=np.float32)
    d["wqsw"] = np.ascontiguousarray(_swap_cols(d["wq"], cfg.HEADS))
    d["wdkv"] = np.ascontiguousarray(iv["L0_attn_W_dkv"], dtype=np.float32)
    d["wuk"] = np.ascontiguousarray(iv["L0_attn_W_uk"], dtype=np.float32)
    d["wuksw"] = np.ascontiguousarray(_swap_cols(d["wuk"], cfg.HEADS))
    d["wuv"] = np.ascontiguousarray(iv["L0_attn_W_uv"], dtype=np.float32)
    d["wo"] = np.ascontiguousarray(iv["L0_attn_W_o"], dtype=np.float32)
    lng = np.broadcast_to(np.asarray(iv["L0_attn_ln_gamma"], np.float32), (128, L))
    lnb = np.broadcast_to(np.asarray(iv["L0_attn_ln_beta"], np.float32), (128, L))
    d["lngb"] = np.ascontiguousarray(np.stack([lng, lnb]))
    d["rw"] = np.ascontiguousarray(iv["L0_ffn_router_w"], dtype=np.float32)
    d["rbt"] = np.ascontiguousarray(
        np.broadcast_to(np.asarray(iv["L0_ffn_router_b"], np.float32), (128, E)))
    d["w1"] = np.ascontiguousarray(iv["L0_ffn_w1"], dtype=np.float32)
    d["w2"] = np.ascontiguousarray(iv["L0_ffn_w2"], dtype=np.float32)
    b1 = np.asarray(iv["L0_ffn_b1"], np.float32)
    d["b1a"] = np.ascontiguousarray(b1[:, :HID].reshape(E, cfg.HJ, 128).transpose(0, 2, 1))
    d["b1g"] = np.ascontiguousarray(b1[:, HID:].reshape(E, cfg.HJ, 128).transpose(0, 2, 1))
    b2 = np.asarray(iv["L0_ffn_b2"], np.float32)
    d["b2pm"] = np.ascontiguousarray(
        b2.reshape(E, cfg.DC, 128).transpose(2, 0, 1))
    return d


class Sem:
    __slots__ = ("h", "n", "name")
    def __init__(self, h, name):
        self.h, self.n, self.name = h, 0, name


class P:
    def __init__(self):
        self.seen = {}
        self.slot = {}
        self.bufc = {}

    def inc(self, instr, sem, n=1):
        instr.then_inc(sem.h, n)
        sem.n += n
        return sem.n

    def wait(self, eng, ename, sem, val):
        if val <= 0:
            return
        key = (ename, sem.name)
        if self.seen.get(key, 0) >= val:
            return
        eng.wait_ge(sem.h, val)
        self.seen[key] = val

    def _w(self, eng, ename, d):
        for sem, tok in d.values():
            self.wait(eng, ename, sem, tok)

    def ps_acquire(self, teng, slots):
        for s_ in slots:
            self._w(teng, "t", self.slot.get(s_, {}))

    def ps_release(self, slots, sem, tok):
        for s_ in slots:
            self.slot.setdefault(s_, {})[sem.name] = (sem, tok)

    def buf_wait(self, eng, ename, name):
        self._w(eng, ename, self.bufc.get(name, {}))

    def buf_use(self, name, sem, tok):
        self.bufc.setdefault(name, {})[sem.name] = (sem, tok)


def build_program(cfg):
    from contextlib import ExitStack
    nc = bass.Bass("TRN2", target_bir_lowering=False, detect_race_conditions=False)
    T, D, L, HID, E, B = cfg.T, cfg.D, cfg.L, cfg.HID, cfg.E, cfg.B
    DC, TC, NS, LC, HJ = cfg.DC, cfg.TC, cfg.NS, cfg.LC, cfg.HJ
    NG, HPG, CPG, UJG, NJG = cfg.NG, cfg.HPG, cfg.CPG, cfg.UJG, cfg.NJG
    CAPQ = cfg.CAPQ
    NH, HSZ = cfg.NH, cfg.HSZ
    HC = HPG // 2
    VW = HPG * HD

    def din(name, shape, dty=F32):
        return nc.dram_tensor(name, list(shape), dty, kind="ExternalInput").ap()

    x_h = din("x", (B, T, D))
    wq_h = din("wq", (D, D)); wqsw_h = din("wqsw", (D, D))
    wdkv_h = din("wdkv", (D, L))
    wuk_h = din("wuk", (L, D)); wuksw_h = din("wuksw", (L, D))
    wuv_h = din("wuv", (L, D)); wo_h = din("wo", (D, D))
    lngb_h = din("lngb", (2, 128, L))
    rw_h = din("rw", (D, E)); rbt_h = din("rbt", (128, E))
    w1_h = din("w1", (E, D, 2 * HID)); w2_h = din("w2", (E, HID, D))
    b1a_h = din("b1a", (E, 128, HJ)); b1g_h = din("b1g", (E, 128, HJ))
    b2pm_h = din("b2pm", (128, E, DC))
    tokv_h = din("tokv", (T, 8))
    sent8_h = din("sent8", (E * T, 8))
    etm1_h = din("etm1", (128, 1))
    onescf_h = din("onescf", (128, 1))
    xnscr_h = nc.dram_tensor("xnscr", [T + 128, D], F32, kind="Internal").ap()
    yscr_h = nc.dram_tensor("yscr", [T + 128, D], F32, kind="Internal").ap()
    tokf_h = nc.dram_tensor("tokf", [E * T, 8], F32, kind="Internal").ap()
    cos_h = din("cos2t", (64, T), BF16); sin_h = din("sin2st", (64, T), BF16)
    masks_h = din("masks", (4, 128, S), BF16)
    iota_h = din("iota8", (128, E)); identf_h = din("identf", (128, 128))
    ones_h = din("ones128", (128, 128), BF16)
    epsc_h = din("epsc", (128, 1))
    out_h = nc.dram_tensor("out", [B, T, D], F32, kind="ExternalOutput").ap()

    with ExitStack() as _ctx:
        def _sb(*a):
            return _ctx.enter_context(nc.sbuf_tensor(*a))
        def _ps(*a):
            return _ctx.enter_context(nc.psum_tensor(*a))
        def _sem(n):
            return _ctx.enter_context(nc.semaphore(n))
        acc = _sb("g_acc", [128, DC, T], F32)
        ohT = _sb("g_ohT", [8, T], BF16)
        iota8 = _sb("g_iota8", [128, E], F32)
        identf = _sb("g_identf", [128, 128], F32)
        ones128 = _sb("g_ones128", [128, 128], BF16)
        rbt = _sb("g_rbt", [128, E], F32)
        lngb = _sb("g_lngb", [128, 2, L], F32)
        b1a = _sb("g_b1a", [128, E, HJ], F32)
        b1g = _sb("g_b1g", [128, E, HJ], F32)
        rwbf = _sb("g_rwbf", [128, DC, E], BF16)
        etm1 = _sb("g_etm1", [128, 1], F32)
        onescol_f = _sb("g_onescf", [128, 1], F32)
        b2pm = _sb("g_b2pm", [128, E, DC], F32)
        st_a = _sb("g_st_a", [128, DC, 128], F32)
        st_2 = _sb("g_st_2", [128, D], F32)
        bf_a = _sb("g_bf_a", [128, 2, DC, 128], BF16)
        bf_g = _sb("g_bf_g", [128, 2, DC, 128], BF16)
        xcb = _sb("g_xcb", [128, 2, D], F32)
        stat = _sb("g_stat", [128, 12], F32)
        lgb = _sb("g_lgb", [128, E], F32)
        mx8 = _sb("g_mx8", [128, 8], F32)
        idx8 = _sb("g_idx8", [128, 8], U32)
        ohf = _sb("g_ohf", [128, E], F32)
        tiny = _sb("g_tiny", [1, 16], F32)
        tinyb = _sb("g_tinyb", [1, 2], BF16)
        negc = _sb("g_negc", [128, 1], F32)
        epsc = _sb("g_epsc", [128, 1], F32)
        psA = _ps("ps_A", [128, 2048], F32)
        psB = _ps("ps_B", [128, 2048], F32)
        h_ld = _sem("sldc"); h_pe = _sem("spe")
        h_dv = _sem("sdv"); h_ac = _sem("sac"); h_gp = _sem("sgp")
        h_la = _sem("slda"); h_l2 = _sem("sld2")
        h_lx0 = _sem("sldx0"); h_lx1 = _sem("sldx1")
        h_ld0 = _sem("sldd0"); h_ld1 = _sem("sldd1")
        h_st0 = _sem("sst0"); h_st1 = _sem("sst1")
        h_xn = _sem("sxn")
        h_gi = _sem("sgi")
        h_gd = _sem("sgd")
        block = _ctx.enter_context(nc.Block())

        p = P()
        sld = Sem(h_ld, "ldc"); spe = Sem(h_pe, "pe")
        sdv = Sem(h_dv, "dv"); sac = Sem(h_ac, "ac"); sgp = Sem(h_gp, "gp")
        slda = Sem(h_la, "lda"); sld2 = Sem(h_l2, "ld2")
        sldx = [Sem(h_lx0, "ldx0"), Sem(h_lx1, "ldx1")]
        sldd = [Sem(h_ld0, "ldd0"), Sem(h_ld1, "ldd1")]
        sstb = [Sem(h_st0, "st0"), Sem(h_st1, "st1")]
        sxn = Sem(h_xn, "xn")
        sgi = Sem(h_gi, "gi")
        sgd = Sem(h_gd, "gd")
        sq, t, v, sc, g = nc.sync, nc.tensor, nc.vector, nc.scalar, nc.gpsimd
        A = [psA[:, i * 512:(i + 1) * 512] for i in range(4)]
        Bp = [psB[:, i * 512:(i + 1) * 512] for i in range(4)]
        ALL_SEMS = (sld, spe, sdv, sac, sgp, slda, sld2, sldx[0], sldx[1],
                    sldd[0], sldd[1], sstb[0], sstb[1], sxn, sgi)
        ENGS = ((sq, "sq"), (t, "t"), (v, "v"), (sc, "sc"), (g, "g"))

        def ld(dst, src, sem=None):
            sem = sem or sld
            return p.inc(sq.dma_start(out=dst, in_=src), sem, 16)

        def mmacc(out, pairs, slots, extra=()):
            p.ps_acquire(t, slots)
            for (sem, tok) in extra:
                p.wait(t, "t", sem, tok)
            n = len(pairs)
            last = None
            for i_, (l_, r_) in enumerate(pairs):
                last = t.matmul(out, l_, r_, start=(i_ == 0), stop=(i_ == n - 1))
            return last

        def barrier():
            for eng, en in ENGS:
                for sem in ALL_SEMS:
                    p.wait(eng, en, sem, sem.n)

        def stage_cast(dst, src_h, stage, stage_name, ceng, cename, csem, dst_name=None):
            lsem = {"st_a": slda, "st_2": sld2}[stage_name]
            p.buf_wait(sq, "sq", stage_name)
            tok = ld(stage, src_h, lsem)
            p.wait(ceng, cename, lsem, tok)
            if dst_name:
                p.buf_wait(ceng, cename, dst_name)
            if ceng is sc:
                tc = p.inc(sc.copy(dst, stage), csem, 1)
            elif ceng is v:
                tc = p.inc(v.tensor_copy(dst, stage), csem, 1)
            else:
                tc = p.inc(g.tensor_copy(dst, stage), csem, 1)
            p.buf_use(stage_name, csem, tc)
            return tc

        ld(iota8[:], iota_h); ld(identf[:], identf_h)
        ld(ones128[:], ones_h)
        ld(etm1[:], etm1_h); ld(onescol_f[:], onescf_h)
        ld(b2pm[:], b2pm_h)
        ld(rbt[:], rbt_h)
        ld(epsc[:], epsc_h)
        ld(lngb[:], lngb_h.rearrange("o p q -> p o q"))
        ld(b1a[:], b1a_h.rearrange("e p j -> p e j"))
        ld(b1g[:], b1g_h.rearrange("e p j -> p e j"))
        stage_cast(rwbf[:], rw_h.rearrange("(c p) e -> p c e", p=128),
                   st_a[:, :, 0:E], "st_a", g, "g", sgp)
        v.memset(xcb[:, 0], 0.0)
        tvz = p.inc(v.drain(), sdv, 1)
        p.wait(sq, "sq", sdv, tvz)
        p.inc(sq.dma_start(out=xnscr_h[T:T + 128, :], in_=xcb[:, 0]), sld, 16)
        p.inc(sq.dma_start(out=yscr_h[T:T + 128, :], in_=xcb[:, 0]), sld, 16)
        barrier()

        for b in range(B):
            with ExitStack() as bctx:
                def _bb(nm, shape, dty):
                    return bctx.enter_context(nc.sbuf_tensor("b%d_%s" % (b, nm), shape, dty))
                dumpA = _bb("dumpA", [128, D], F32)
                xnT = _bb("xnT", [128, DC, T], BF16)
                wdkvbf = _bb("wdkv", [128, DC, L], BF16)
                wukbf = _bb("wuk", [128, LC, D], BF16)
                wukswbf = _bb("wuksw", [128, LC, D], BF16)
                wuvbf = _bb("wuv", [128, LC, D], BF16)
                latnT = _bb("latnT", [128, LC, T], BF16)
                kroT = _bb("kroT", [128, HC, T], BF16)
                qroT = _bb("qroT", [128, HC, T], BF16)
                v65 = _bb("v65", [128, TC, HPG, 65], BF16)
                catT = _bb("catT", [128, CPG, S], BF16)
                ropet = _bb("ropet", [128, 2, S], F32)
                attnT = _bb("attnT", [128, 2, S], BF16)
                recipb = _bb("recipb", [128, S], BF16)
                bc_sb = _bb("bc_sb", [64, S], BF16)
                masksb = _bb("masksb", [128, 4, S], BF16)
                cos2t = _bb("cos2t", [64, T], BF16)
                sin2st = _bb("sin2st", [64, T], BF16)
                _hold = {}

                prelude = []
                def _step_consts():
                    ld(masksb[:], masks_h.rearrange("o p q -> p o q"))
                    ld(cos2t[:], cos_h)
                    _hold["b_ld"] = ld(sin2st[:], sin_h)
                prelude.append(_step_consts)
                for lc2 in range(L // 128):
                    prelude.append(lambda lc2=lc2: stage_cast(
                        wdkvbf[:, :, lc2 * 128:(lc2 + 1) * 128],
                        wdkv_h[:, lc2 * 128:(lc2 + 1) * 128].rearrange("(c p) l -> p c l", p=128),
                        st_a[:], "st_a", g, "g", sgp))
                for (hsrc, dstw) in ((wuk_h, wukbf), (wuksw_h, wukswbf), (wuv_h, wuvbf)):
                    for lc in range(LC):
                        prelude.append(lambda hsrc=hsrc, dstw=dstw, lc=lc: stage_cast(
                            dstw[:, lc, :], hsrc[lc * 128:(lc + 1) * 128, :],
                            st_2[:], "st_2", g, "g", sgp))
                def _step_ones():
                    v.memset(v65[:, :, :, 64:65], 1.0)
                    p.inc(v.drain(), sdv, 1)
                prelude.append(_step_ones)

                for i in range(TC):
                    bi = i % 2
                    xn_ = "xcb%d" % bi
                    p.buf_wait(sq, "sq", xn_)
                    tokx = ld(xcb[:, bi], x_h[b, i * 128:(i + 1) * 128, :], sldx[bi])
                    if prelude:
                        prelude.pop(0)()
                    p.wait(v, "v", sldx[bi], tokx)
                    v.reduce_sum(stat[:, 0:1], xcb[:, bi], axis=AX.X)
                    v.drain()
                    v.tensor_scalar(stat[:, 1:2], stat[:, 0:1], 1.0 / D, None, op0=ALU.mult)
                    v.drain()
                    tv = p.inc(v.tensor_scalar(xcb[:, bi], xcb[:, bi], stat[:, 1:2], None,
                                               op0=ALU.subtract), sdv, 1)
                    p.wait(sc, "sc", sdv, tv)
                    p.buf_wait(sc, "sc", "dumpA")
                    sc.activation(dumpA[:], xcb[:, bi], AF.Square, accum_out=stat[:, 2:3])
                    sc.drain()
                    sc.activation(stat[:, 3:4], stat[:, 2:3], AF.Ln, bias=epsc[:], scale=1.0 / D)
                    sc.drain()
                    ta = p.inc(sc.activation(stat[:, 4:5], stat[:, 3:4], AF.Exp, scale=-0.5), sac, 1)
                    p.buf_use("dumpA", sac, ta)
                    p.wait(v, "v", sac, ta)
                    v.drain()
                    tv2 = p.inc(v.tensor_scalar(xcb[:, bi], xcb[:, bi], stat[:, 4:5], None,
                                                op0=ALU.mult), sdv, 1)
                    if not DBG["no_xnscr"]:
                        p.wait(sc, "sc", sdv, tv2)
                        p.wait(sc, "sc", sxn, sxn.n)
                        txs = p.inc(sc.dma_start(out=xnscr_h[i * 128:(i + 1) * 128, :],
                                                 in_=xcb[:, bi]), sxn, 16)
                        p.buf_use("xcb%d" % bi, sxn, txs)
                    p.ps_acquire(t, ["A0", "A1", "A2", "A3"])
                    p.wait(t, "t", sdv, tv2)
                    ti = None
                    for c in range(DC):
                        ti = t.transpose(psA[:, c * 256:c * 256 + 128],
                                         xcb[:, bi, c * 128:(c + 1) * 128], identf[:])
                    tpe = p.inc(ti, spe, 1)
                    p.wait(v, "v", spe, tpe)
                    last = None
                    for c in range(DC):
                        last = v.tensor_copy(xnT[:, c, i * 128:(i + 1) * 128],
                                             psA[:, c * 256:c * 256 + 128])
                    tv3 = p.inc(last, sdv, 1)
                    p.buf_use(xn_, sdv, tv3)
                    p.ps_release(["A0", "A1", "A2", "A3"], sdv, tv3)
                    last = mmacc(psB[:, 0:E],
                                 [(xnT[:, c, i * 128:(i + 1) * 128], rwbf[:, c, :]) for c in range(DC)],
                                 ["B0"], extra=[(sdv, tv3)])
                    tpe2 = p.inc(last, spe, 1)
                    p.wait(v, "v", spe, tpe2)
                    v.tensor_tensor(lgb[:], psB[:, 0:E], rbt[:], op=ALU.add)
                    v.drain()
                    v.max(mx8[:], lgb[:])
                    v.drain()
                    v.max_index(idx8[:], mx8[:], lgb[:])
                    v.drain()
                    v.tensor_copy(stat[:, 5:6], idx8[:, 0:1])
                    v.drain()
                    tv4 = p.inc(v.tensor_scalar(ohf[:], iota8[:], stat[:, 5:6], None,
                                                op0=ALU.is_equal), sdv, 1)
                    p.ps_release(["B0"], sdv, tv4)
                    p.ps_acquire(t, ["B1"])
                    p.wait(t, "t", sdv, tv4)
                    ti = t.transpose(psB[0:E, 512:640], ohf[:], identf[:])
                    tpe3 = p.inc(ti, spe, 1)
                    p.wait(v, "v", spe, tpe3)
                    tv5 = p.inc(v.tensor_copy(ohT[:, i * 128:(i + 1) * 128], psB[0:E, 512:640]), sdv, 1)
                    p.ps_release(["B1"], sdv, tv5)
                oh_v = sdv.n
                for st in prelude:
                    st()
                prelude = []
                b_ld = _hold["b_ld"]
                barrier()

                for i in range(TC):
                    last = mmacc(psB[:, 1024:1280],
                                 [(xnT[:, c, i * 128:(i + 1) * 128], wdkvbf[:, c, :]) for c in range(DC)],
                                 ["B2"], extra=[(sgp, sgp.n)])
                    tpe = p.inc(last, spe, 1)
                    p.wait(v, "v", spe, tpe)
                    p.buf_wait(v, "v", "xcb0")
                    v.reduce_sum(stat[:, 6:7], psB[:, 1024:1280], axis=AX.X)
                    v.drain()
                    v.tensor_scalar(stat[:, 7:8], stat[:, 6:7], 1.0 / L, None, op0=ALU.mult)
                    v.drain()
                    tv = p.inc(v.tensor_scalar(xcb[:, 0, 0:L], psB[:, 1024:1280], stat[:, 7:8], None,
                                               op0=ALU.subtract), sdv, 1)
                    p.ps_release(["B2"], sdv, tv)
                    p.wait(sc, "sc", sdv, tv)
                    sc.activation(ropet[:, 0, 0:L], xcb[:, 0, 0:L], AF.Square, accum_out=stat[:, 8:9])
                    sc.drain()
                    sc.activation(stat[:, 9:10], stat[:, 8:9], AF.Ln, bias=epsc[:], scale=1.0 / L)
                    sc.drain()
                    ta = p.inc(sc.activation(stat[:, 10:11], stat[:, 9:10], AF.Exp, scale=-0.5), sac, 1)
                    p.wait(v, "v", sac, ta)
                    v.drain()
                    v.tensor_scalar(xcb[:, 1, 0:L], xcb[:, 0, 0:L], stat[:, 10:11], None, op0=ALU.mult)
                    v.drain()
                    v.tensor_tensor(xcb[:, 0, 0:L], xcb[:, 1, 0:L], lngb[:, 0, :], op=ALU.mult)
                    v.drain()
                    tv2 = p.inc(v.tensor_tensor(xcb[:, 1, 0:L], xcb[:, 0, 0:L], lngb[:, 1, :],
                                                op=ALU.add), sdv, 1)
                    p.buf_use("xcb0", sdv, tv2)
                    p.buf_use("xcb1", sdv, tv2)
                    p.ps_acquire(t, ["B3"])
                    p.wait(t, "t", sdv, tv2)
                    ti = None
                    for lc in range(LC):
                        ti = t.transpose(psB[:, 1536 + lc * 128:1536 + (lc + 1) * 128],
                                         xcb[:, 1, lc * 128:(lc + 1) * 128], identf[:])
                    tpe2 = p.inc(ti, spe, 1)
                    p.wait(v, "v", spe, tpe2)
                    last = None
                    for lc in range(LC):
                        last = v.tensor_copy(latnT[:, lc, i * 128:(i + 1) * 128],
                                             psB[:, 1536 + lc * 128:1536 + (lc + 1) * 128])
                    tv3 = p.inc(last, sdv, 1)
                    p.ps_release(["B3"], sdv, tv3)
                latn_v = sdv.n

                for grp in range(NG):
                    h0 = grp * HPG
                    for cc in range(HC):
                        c0 = (h0 + 2 * cc) * HD
                        bufq = cc % 2
                        tca = stage_cast(bf_a[:, bufq],
                                         wq_h[:, c0:c0 + 128].rearrange("(c p) m -> p c m", p=128),
                                         st_a[:], "st_a",
                                         g, "g", sgp, dst_name="bf_a%d" % bufq)
                        tcg = stage_cast(bf_g[:, bufq],
                                         wqsw_h[:, c0:c0 + 128].rearrange("(c p) m -> p c m", p=128),
                                         st_2[:].rearrange("p (c m) -> p c m", m=128), "st_2",
                                         g, "g", sgp, dst_name="bf_g%d" % bufq)
                        for par in range(2):
                            hh = h0 + 2 * cc + par
                            for s_ in range(NS):
                                mmacc(A[0][0:64, :],
                                      [(wukbf[:, lc, hh * HD:(hh + 1) * HD],
                                        latnT[:, lc, s_ * S:(s_ + 1) * S]) for lc in range(LC)],
                                      ["A0"], extra=[(sdv, latn_v)])
                                last = mmacc(A[1][0:64, :],
                                             [(wukswbf[:, lc, hh * HD:(hh + 1) * HD],
                                               latnT[:, lc, s_ * S:(s_ + 1) * S]) for lc in range(LC)],
                                             ["A1"])
                                tpe_k = p.inc(last, spe, 1)
                                mmacc(A[2][0:64, :],
                                      [(bf_a[:, bufq, c_, par * 64:par * 64 + 64],
                                        xnT[:, c_, s_ * S:(s_ + 1) * S]) for c_ in range(DC)],
                                      ["A2"], extra=[(sgp, max(tca, tcg))])
                                last = mmacc(A[3][0:64, :],
                                             [(bf_g[:, bufq, c_, par * 64:par * 64 + 64],
                                               xnT[:, c_, s_ * S:(s_ + 1) * S]) for c_ in range(DC)],
                                             ["A3"])
                                tpe_q = p.inc(last, spe, 1)
                                p.buf_use("bf_a%d" % bufq, spe, tpe_q)
                                p.buf_use("bf_g%d" % bufq, spe, tpe_q)
                                p.wait(v, "v", spe, tpe_k)
                                p.wait(v, "v", sld, b_ld)
                                p.buf_wait(v, "v", "kroT")
                                v.drain()
                                v.tensor_tensor(ropet[0:64, 0, :], A[0][0:64, :],
                                                cos2t[:, s_ * S:(s_ + 1) * S], op=ALU.mult)
                                v.tensor_tensor(ropet[0:64, 1, :], A[1][0:64, :],
                                                sin2st[:, s_ * S:(s_ + 1) * S], op=ALU.mult)
                                v.drain()
                                tv = p.inc(v.tensor_tensor(
                                    kroT[par * 64:par * 64 + 64, cc, s_ * S:(s_ + 1) * S],
                                    ropet[0:64, 0, :], ropet[0:64, 1, :], op=ALU.add), sdv, 1)
                                p.ps_release(["A0", "A1"], sdv, tv)
                                p.wait(v, "v", spe, tpe_q)
                                p.buf_wait(v, "v", "qroT")
                                v.drain()
                                v.tensor_tensor(ropet[0:64, 0, :], A[2][0:64, :],
                                                cos2t[:, s_ * S:(s_ + 1) * S], op=ALU.mult)
                                v.tensor_tensor(ropet[0:64, 1, :], A[3][0:64, :],
                                                sin2st[:, s_ * S:(s_ + 1) * S], op=ALU.mult)
                                v.drain()
                                tv = p.inc(v.tensor_tensor(
                                    qroT[par * 64:par * 64 + 64, cc, s_ * S:(s_ + 1) * S],
                                    ropet[0:64, 0, :], ropet[0:64, 1, :], op=ALU.add), sdv, 1)
                                p.ps_release(["A2", "A3"], sdv, tv)
                    kq_v = sdv.n

                    for i in range(TC):
                        p.buf_wait(v, "v", "v65")
                        last = mmacc(A[0][:, 0:VW],
                                     [(latnT[:, lc, i * 128:(i + 1) * 128],
                                       wuvbf[:, lc, h0 * HD:h0 * HD + VW]) for lc in range(LC)],
                                     ["A0"])
                        tpe = p.inc(last, spe, 1)
                        p.wait(v, "v", spe, tpe)
                        tv = p.inc(v.tensor_copy(v65[:, i, :, 0:64],
                                                 A[0][:, 0:VW].rearrange("p (h d) -> p h d", d=64)),
                                   sdv, 1)
                        p.ps_release(["A0"], sdv, tv)
                    v65_v = sdv.n

                    first = True
                    units = [(s_, par, cc) for s_ in range(NS) for par in range(2)
                             for cc in range(HC)]
                    for g0 in range(0, len(units), 4):
                        grp_u = units[g0:g0 + 4]
                        toks = []
                        for ui, (s_, par, cc) in enumerate(grp_u):
                            sb2 = ui % 2
                            p.buf_wait(sc, "sc", "attn%d" % sb2)
                            p.wait(sc, "sc", sdv, kq_v)
                            tq = p.inc(sc.activation(attnT[par * 64:par * 64 + 64, sb2, :],
                                                     kroT[par * 64:par * 64 + 64, cc,
                                                          s_ * S:(s_ + 1) * S],
                                                     AF.Square), sac, 1)
                            last = mmacc(Bp[ui][0:1, :],
                                         [(ones128[par * 64:par * 64 + 64, 0:1],
                                           attnT[par * 64:par * 64 + 64, sb2, :])],
                                         ["B%d" % ui], extra=[(sac, tq)])
                            tpe = p.inc(last, spe, 1)
                            p.buf_use("attn%d" % sb2, spe, tpe)
                        p.wait(v, "v", spe, spe.n)
                        v.reduce_max(tiny[0:1, 1:2], psB[0:1, 0:512 * len(grp_u)], axis=AX.X)
                        v.drain()
                        if first:
                            v.tensor_copy(tiny[0:1, 0:1], tiny[0:1, 1:2])
                            first = False
                        else:
                            v.tensor_tensor(tiny[0:1, 0:1], tiny[0:1, 0:1], tiny[0:1, 1:2],
                                            op=ALU.max)
                        tvm = p.inc(v.drain(), sdv, 1)
                        p.ps_release(["B0", "B1", "B2", "B3"], sdv, tvm)
                    mk_v = sdv.n

                    for s_ in range(NS):
                        units_q = [(par, cc) for par in range(2) for cc in range(HC)]
                        for g0 in range(0, len(units_q), 4):
                            grp_u = units_q[g0:g0 + 4]
                            for ui, (par, cc) in enumerate(grp_u):
                                sb2 = ui % 2
                                p.buf_wait(sc, "sc", "attn%d" % sb2)
                                tq = p.inc(sc.activation(attnT[par * 64:par * 64 + 64, sb2, :],
                                                         qroT[par * 64:par * 64 + 64, cc,
                                                              s_ * S:(s_ + 1) * S],
                                                         AF.Square), sac, 1)
                                last = mmacc(Bp[ui][0:1, :],
                                             [(ones128[par * 64:par * 64 + 64, 0:1],
                                               attnT[par * 64:par * 64 + 64, sb2, :])],
                                             ["B%d" % ui], extra=[(sac, tq)])
                                tpe = p.inc(last, spe, 1)
                                p.buf_use("attn%d" % sb2, spe, tpe)
                            p.wait(v, "v", spe, spe.n)
                            v.reduce_max(tiny[0:1, 2:3], psB[0:1, 0:512 * len(grp_u)], axis=AX.X)
                            v.drain()
                            if g0 == 0:
                                v.tensor_copy(tiny[0:1, 3:4], tiny[0:1, 2:3])
                            else:
                                v.tensor_tensor(tiny[0:1, 3:4], tiny[0:1, 3:4], tiny[0:1, 2:3],
                                                op=ALU.max)
                            tvq = p.inc(v.drain(), sdv, 1)
                            p.ps_release(["B0", "B1", "B2", "B3"], sdv, tvq)
                        p.wait(v, "v", sdv, mk_v)
                        tv = p.inc(v.tensor_tensor(tiny[0:1, 4:5], tiny[0:1, 3:4], tiny[0:1, 0:1],
                                                   op=ALU.mult), sdv, 1)
                        p.wait(sc, "sc", sdv, tv)
                        sc.activation(tiny[0:1, 5:6], tiny[0:1, 4:5], AF.Ln)
                        sc.drain()
                        ta = p.inc(sc.activation(tiny[0:1, 6:7], tiny[0:1, 5:6], AF.Exp, scale=0.5),
                                   sac, 1)
                        p.wait(v, "v", sac, ta)
                        v.tensor_scalar(tiny[0:1, 7:8], tiny[0:1, 6:7], -0.125, -2.0,
                                        op0=ALU.mult, op1=ALU.add)
                        v.drain()
                        tv = p.inc(v.tensor_copy(tinyb[0:1, 0:1], tiny[0:1, 7:8]), sdv, 1)
                        p.buf_wait(v, "v", "negc")
                        last = mmacc(psB[:, 1024:1025], [(ones128[0:1, :], tinyb[0:1, 0:1])],
                                     ["B2"], extra=[(sdv, tv)])
                        tpe = p.inc(last, spe, 1)
                        p.wait(v, "v", spe, tpe)
                        tnegc = p.inc(v.tensor_copy(negc[:, 0:1], psB[:, 1024:1025]), sdv, 1)
                        p.ps_release(["B2"], sdv, tnegc)

                        nkt = 4 * s_ + 4
                        for lh in range(HPG):
                            cc = lh // 2; par = lh % 2
                            ob = [0, 1, 3][lh % 3]
                            o65 = Bp[ob][0:65, :]
                            p.ps_acquire(t, ["B%d" % ob])
                            def emit_score(kt):
                                sb_ = kt % 2
                                p.ps_acquire(t, ["A%d" % sb_])
                                p.wait(t, "t", sdv, kq_v)
                                tmm = t.matmul(A[sb_],
                                               kroT[par * 64:par * 64 + 64, cc, kt * 128:(kt + 1) * 128],
                                               qroT[par * 64:par * 64 + 64, cc, s_ * S:(s_ + 1) * S],
                                               start=True, stop=True)
                                return p.inc(tmm, spe, 1)

                            tpe_s = emit_score(0)
                            for kt in range(nkt):
                                sb_ = kt % 2
                                stp = A[sb_]
                                abuf = attnT[:, sb_, :]
                                tpe = tpe_s
                                p.buf_wait(sc, "sc", "attn%d" % sb_)
                                p.wait(sc, "sc", sdv, tnegc)
                                p.wait(sc, "sc", spe, tpe)
                                tex = p.inc(sc.activation(abuf, stp, AF.Exp,
                                                          bias=negc[:, 0:1], scale=0.125), sac, 1)
                                p.buf_use("negc", sac, tex)
                                p.ps_release(["A%d" % sb_], sac, tex)
                                if kt + 1 < nkt:
                                    tpe_s = emit_score(kt + 1)
                                off = kt - 4 * s_
                                if off >= 0:
                                    p.wait(v, "v", sac, tex)
                                    tex2 = p.inc(v.tensor_tensor(abuf, abuf, masksb[:, off, :],
                                                                 op=ALU.mult), sdv, 1)
                                    p.wait(t, "t", sdv, tex2)
                                else:
                                    p.wait(t, "t", sac, tex)
                                p.wait(t, "t", sdv, v65_v)
                                last = t.matmul(o65, v65[:, kt, lh, :], abuf,
                                                start=(kt == 0), stop=(kt == nkt - 1))
                                tpe2 = p.inc(last, spe, 1)
                                p.buf_use("attn%d" % sb_, spe, tpe2)
                                p.buf_use("v65", spe, tpe2)
                            p.wait(sc, "sc", spe, spe.n)
                            p.buf_wait(sc, "sc", "recipb")
                            sc.activation(ropet[64:65, 0, :], o65[64:65, :], AF.Ln)
                            sc.drain()
                            trc = p.inc(sc.activation(recipb[64:65, :], ropet[64:65, 0, :],
                                                      AF.Exp, scale=-1.0), sac, 1)
                            last = mmacc(psB[0:64, 1024:1536],
                                         [(ones128[64:65, 0:64], recipb[64:65, :])],
                                         ["B2"], extra=[(sac, trc)])
                            tpe3 = p.inc(last, spe, 1)
                            p.buf_use("recipb", spe, tpe3)
                            p.wait(v, "v", spe, tpe3)
                            v.tensor_copy(bc_sb[:], psB[0:64, 1024:1536])
                            v.drain()
                            p.buf_wait(v, "v", "catT")
                            tv = p.inc(v.tensor_tensor(catT[par * 64:par * 64 + 64, cc, :],
                                                       o65[0:64, :], bc_sb[:], op=ALU.mult), sdv, 1)
                            p.ps_release(["B2"], sdv, tv)
                            p.ps_release(["B%d" % ob], sdv, tv)
                        cat_v = sdv.n

                        wo_tgt = [("bf_a0", bf_a[:, 0]), ("bf_g0", bf_g[:, 0])]
                        for ccw in range(CPG):
                            r0 = h0 * HD + ccw * 128
                            nm, tgt = wo_tgt[ccw % 2]
                            stage_cast(tgt.rearrange("p c m -> p (c m)"), wo_h[r0:r0 + 128, :],
                                       st_2[:], "st_2", g, "g", sgp, dst_name=nm)
                        wo_g = sgp.n
                        for dch in range(DC):
                            wsl = "A%d" % (2 + dch % 2)
                            wop = A[2 + dch % 2]
                            last = mmacc(wop,
                                         [(wo_tgt[ccw % 2][1][:, dch, :], catT[:, ccw, :])
                                          for ccw in range(CPG)],
                                         [wsl], extra=[(sgp, wo_g), (sdv, cat_v)])
                            tpe = p.inc(last, spe, 1)
                            p.buf_use("bf_a0", spe, tpe)
                            p.buf_use("bf_g0", spe, tpe)
                            p.buf_use("catT", spe, tpe)
                            p.wait(v, "v", spe, tpe)
                            dst = acc[:, dch, s_ * S:(s_ + 1) * S]
                            if grp == 0:
                                tv = p.inc(v.tensor_copy(dst, wop), sdv, 1)
                            else:
                                tv = p.inc(v.tensor_tensor(dst, dst, wop, op=ALU.add), sdv, 1)
                            p.ps_release([wsl], sdv, tv)
            barrier()

            with ExitStack() as cctx:
                def _cb(nm, shape, dty):
                    return cctx.enter_context(nc.sbuf_tensor("c%d_%s" % (b, nm), shape, dty))
                ubuf = _cb("ubuf", [128, UJG, CAPQ * S], BF16)
                w2g = _cb("w2g", [128, UJG, D], BF16)
                smb = _cb("smb", [128, 2, S], BF16)
                cumsb = _cb("cumsb", [8, S], F32)
                sidx = _cb("sidx", [128, 4], U32)
                tokvs = _cb("tokvs", [128, 4, 8], F32)
                tokw = _cb("tokw", [128, CAPQ, 4, 8], F32)
                tidx = _cb("tidx", [128, CAPQ, 4], U32)
                xgs = _cb("xgs", [128, D], F32)
                xgT = _cb("xgT", [128, CAPQ, DC, S], BF16)
                ySBq = _cb("ySBq", [128, DC, CAPQ * S], F32)
                ytok = _cb("ytok", [128, D], F32)
                slotb = ySBq[0:1, 0, 0:S]

                tok_sent = p.inc(sq.dma_start(out=tokf_h, in_=sent8_h), sld, 16)

                for s_ in ([] if DBG["skip_slot"] else range(NS)):
                    p.wait(v, "v", sdv, oh_v)
                    v.drain()
                    if s_ == 0:
                        v.tensor_tensor_scan(cumsb[:], ohT[:, s_ * S:(s_ + 1) * S],
                                             ohT[:, s_ * S:(s_ + 1) * S], 0.0,
                                             op0=ALU.add, op1=ALU.bypass)
                    else:
                        v.tensor_tensor_scan(cumsb[:], ohT[:, s_ * S:(s_ + 1) * S],
                                             ohT[:, s_ * S:(s_ + 1) * S], stat[0:8, 11:12],
                                             op0=ALU.add, op1=ALU.bypass)
                    v.drain()
                    v.tensor_copy(stat[0:8, 11:12], cumsb[:, S - 1:S])
                    v.drain()
                    v.tensor_scalar(cumsb[:], cumsb[:], etm1[0:8, :], None, op0=ALU.add)
                    v.drain()
                    tv = p.inc(v.tensor_tensor(cumsb[:], cumsb[:], ohT[:, s_ * S:(s_ + 1) * S],
                                               op=ALU.mult), sdv, 1)
                    last = mmacc(Bp[s_ % 4][0:1, :], [(onescol_f[0:8, :], cumsb[:])],
                                 ["B%d" % (s_ % 4)], extra=[(sdv, tv)])
                    tpe = p.inc(last, spe, 1)
                    p.wait(v, "v", spe, tpe)
                    p.buf_wait(v, "v", "slotb")
                    p.buf_wait(v, "v", "ySBq")
                    v.drain()
                    tv2 = p.inc(v.tensor_copy(slotb, Bp[s_ % 4][0:1, :]), sdv, 1)
                    p.buf_use("slotb", sdv, tv2)
                    p.ps_release(["B%d" % (s_ % 4)], sdv, tv2)
                    p.buf_wait(sq, "sq", "tokvs")
                    tokld = ld(tokvs[:], tokv_h[s_ * S:(s_ + 1) * S, :]
                               .rearrange("(c p) w -> p c w", p=128), sld2)
                    p.buf_wait(v, "v", "sidx")
                    for cpart in range(4):
                        last = mmacc(psA[:, 0:1], [(ySBq[0:1, 0, cpart * 128:(cpart + 1) * 128],
                                                    onescol_f[0:1, :])],
                                     ["A0"], extra=[(sdv, tv2)])
                        tpe2 = p.inc(last, spe, 1)
                        p.buf_use("slotb", spe, tpe2)
                        p.buf_use("ySBq", spe, tpe2)
                        p.wait(v, "v", spe, tpe2)
                        v.drain()
                        tv3 = p.inc(v.tensor_copy(sidx[:, cpart:cpart + 1], psA[:, 0:1]), sdv, 1)
                        p.ps_release(["A0"], sdv, tv3)
                    p.wait(g, "g", sld, tok_sent)
                    p.wait(g, "g", sld2, tokld)
                    p.wait(g, "g", sdv, sdv.n)
                    p.wait(g, "g", sxn, sxn.n)
                    for cpart in range(4):
                        if DBG["plain_tok"]:
                            g.dma_start(out=tokf_h[(s_ * 4 + cpart) * 128:(s_ * 4 + cpart + 1) * 128, :],
                                        in_=tokvs[:, cpart]).then_inc(sgd.h, 16)
                        else:
                            g.indirect_dma_start(
                                out=tokf_h,
                                out_offset=bass.IndirectOffsetOnAxis(ap=sidx[:, cpart:cpart + 1], axis=0),
                                in_=tokvs[:, cpart], in_offset=None).then_inc(sgd.h, 16)
                    tsc = p.inc(g.drain(), sgi, 1)
                    p.buf_use("tokvs", sgi, tsc)
                    p.buf_use("sidx", sgi, tsc)
                tokf_done = sgi.n

                for e in range(E):
                    p.wait(sq, "sq", sgi, tokf_done)
                    p.buf_wait(sq, "sq", "tokw")
                    tokld = ld(tokw[:, 0:CAPQ], tokf_h[e * T:e * T + CAPQ * S, :]
                               .rearrange("(c p) w -> p c w", p=128), sldx[e % 2])
                    p.wait(g, "g", sldx[e % 2], tokld)
                    p.wait(g, "g", sgi, tokf_done)
                    p.buf_wait(g, "g", "tidx")
                    tgi = p.inc(g.tensor_copy(tidx[:, 0:CAPQ], tokw[:, 0:CAPQ, :, 0]), sgp, 1)
                    p.buf_use("tokw", sgp, tgi)
                    for q in range(CAPQ):
                        for cpart in range(4):
                            p.buf_wait(g, "g", "xgs")
                            p.wait(g, "g", sxn, sxn.n)
                            p.wait(g, "g", sgp, tgi)
                            if DBG["plain_gather"]:
                                g.dma_start(out=xgs[:],
                                            in_=xnscr_h[cpart * 128:(cpart + 1) * 128, :]).then_inc(sgd.h, 16)
                            else:
                                g.indirect_dma_start(
                                    out=xgs[:], out_offset=None,
                                    in_=xnscr_h, in_offset=bass.IndirectOffsetOnAxis(
                                        ap=tidx[:, q, cpart:cpart + 1], axis=0)).then_inc(sgd.h, 16)
                            tgg = p.inc(g.drain(), sgi, 1)
                            p.buf_use("tidx", sgi, tgg)
                            for c0 in range(0, DC, 4):
                                cs = list(range(c0, min(c0 + 4, DC)))
                                p.ps_acquire(t, ["A0", "A1"])
                                p.wait(t, "t", sgi, tgg)
                                ti = None
                                for c in cs:
                                    ti = t.transpose(psA[:, (c % 4) * 256:(c % 4) * 256 + 128],
                                                     xgs[:, c * 128:(c + 1) * 128], identf[:])
                                tpe = p.inc(ti, spe, 1)
                                p.buf_use("xgs", spe, tpe)
                                p.wait(v, "v", spe, tpe)
                                p.buf_wait(v, "v", "xgT")
                                last = None
                                for c2 in cs:
                                    last = v.tensor_copy(
                                        xgT[:, q, c2, cpart * 128:(cpart + 1) * 128],
                                        psA[:, (c2 % 4) * 256:(c2 % 4) * 256 + 128])
                                tvx = p.inc(last, sdv, 1)
                                p.ps_release(["A0", "A1"], sdv, tvx)
                    xg_v = sdv.n
                    p.buf_use("xgT", sdv, xg_v)

                    for jg in range(NJG):
                        for j8 in range(UJG):
                            j = jg * UJG + j8
                            wbuf = j % 2
                            tca = stage_cast(bf_a[:, wbuf],
                                             w1_h[e, :, j * 128:(j + 1) * 128]
                                             .rearrange("(c p) m -> p c m", p=128),
                                             st_a[:], "st_a",
                                             sc, "sc", sac, dst_name="bf_a%d" % wbuf)
                            xn_ = "xcb%d" % wbuf
                            p.buf_wait(sq, "sq", xn_)
                            tg_ = ld(xcb[:, wbuf].rearrange("p (c m) -> p c m", m=128),
                                     w1_h[e, :, HID + j * 128:HID + (j + 1) * 128]
                                     .rearrange("(c p) m -> p c m", p=128), sldd[wbuf])
                            p.buf_wait(v, "v", "bf_g%d" % wbuf)
                            p.wait(v, "v", sldd[wbuf], tg_)
                            tcg = p.inc(v.tensor_copy(bf_g[:, wbuf].rearrange("p c m -> p (c m)"),
                                                      xcb[:, wbuf]), sdv, 1)
                            p.buf_use(xn_, sdv, tcg)
                            tc2 = stage_cast(w2g[:, j8, :], w2_h[e, j * 128:(j + 1) * 128, :],
                                             st_2[:], "st_2", g, "g", sgp, dst_name="w2g%d" % j8)
                            for q in range(CAPQ):
                                if DBG["eq_nou"]:
                                    continue
                                us = (j * CAPQ + q) % 2
                                apn = psA[:, us * 1024:us * 1024 + 512]
                                gpn = psA[:, us * 1024 + 512:us * 1024 + 1024]
                                sla = "A%d" % (us * 2); slg = "A%d" % (us * 2 + 1)
                                mmacc(apn, [(bf_a[:, wbuf, c, :], xgT[:, q, c, :])
                                            for c in range(DC)], [sla],
                                      extra=[(sac, tca), (sdv, xg_v)])
                                last = mmacc(gpn, [(bf_g[:, wbuf, c, :], xgT[:, q, c, :])
                                                   for c in range(DC)], [slg], extra=[(sdv, tcg)])
                                tpe = p.inc(last, spe, 1)
                                p.buf_use("bf_a%d" % wbuf, spe, tpe)
                                p.buf_use("bf_g%d" % wbuf, spe, tpe)
                                p.buf_use("xgT", spe, tpe)
                                p.buf_wait(sc, "sc", "smb%d" % us)
                                p.wait(sc, "sc", spe, tpe)
                                tsg = p.inc(sc.activation(smb[:, us, :], gpn, AF.Sigmoid,
                                                          bias=b1g[:, e, j:j + 1]), sac, 1)
                                p.wait(v, "v", sac, tsg)
                                p.buf_wait(v, "v", "ubuf")
                                tvu = p.inc(v.scalar_tensor_tensor(
                                    ubuf[:, j8, q * S:(q + 1) * S],
                                    apn, b1a[:, e, j:j + 1], smb[:, us, :],
                                    op0=ALU.add, op1=ALU.mult), sdv, 1)
                                p.buf_use("smb%d" % us, sdv, tvu)
                                p.ps_release([sla, slg], sdv, tvu)
                        u_v = sdv.n
                        w2_ready = sgp.n
                        for dch in ([] if DBG["eq_noy"] else range(DC)):
                            for q in range(CAPQ):
                                ys = (dch * CAPQ + q) % 2
                                yp = Bp[ys]
                                p.ps_acquire(t, ["B%d" % ys])
                                p.wait(t, "t", sdv, u_v)
                                p.wait(t, "t", sgp, w2_ready)
                                last = None
                                for j8 in range(UJG):
                                    last = t.matmul(yp, w2g[:, j8, dch * 128:(dch + 1) * 128],
                                                    ubuf[:, j8, q * S:(q + 1) * S],
                                                    start=(j8 == 0), stop=(j8 == UJG - 1))
                                tpe = p.inc(last, spe, 1)
                                for j8 in range(UJG):
                                    p.buf_use("w2g%d" % j8, spe, tpe)
                                p.buf_use("ubuf", spe, tpe)
                                p.wait(v, "v", spe, tpe)
                                p.buf_wait(v, "v", "ySBq")
                                dst = ySBq[:, dch, q * S:(q + 1) * S]
                                if jg == 0:
                                    tv = p.inc(v.tensor_scalar(dst, yp,
                                                               b2pm[:, e, dch:dch + 1], None,
                                                               op0=ALU.add), sdv, 1)
                                else:
                                    tv = p.inc(v.tensor_tensor(dst, dst, yp, op=ALU.add), sdv, 1)
                                p.ps_release(["B%d" % ys], sdv, tv)
                    y_v = sdv.n

                    for q in ([] if DBG["eq_noflush"] else range(CAPQ)):
                        for cpart in range(4):
                            p.buf_wait(v, "v", "ytok")
                            p.ps_acquire(t, ["A0", "A1", "A2", "A3"])
                            p.wait(t, "t", sdv, y_v)
                            ti = None
                            for c in range(DC):
                                ti = t.transpose(psA[:, c * 256:c * 256 + 128],
                                                 ySBq[:, c, q * S + cpart * 128:q * S + (cpart + 1) * 128],
                                                 identf[:])
                            tpe = p.inc(ti, spe, 1)
                            p.buf_use("ySBq", spe, tpe)
                            p.wait(v, "v", spe, tpe)
                            last = None
                            for c in range(DC):
                                last = v.tensor_copy(ytok[:, c * 128:(c + 1) * 128],
                                                     psA[:, c * 256:c * 256 + 128])
                            tvy = p.inc(last, sdv, 1)
                            p.ps_release(["A0", "A1", "A2", "A3"], sdv, tvy)
                            p.wait(g, "g", sdv, tvy)
                            p.wait(g, "g", sgp, tgi)
                            if DBG["plain_yscat"]:
                                g.dma_start(out=yscr_h[cpart * 128:(cpart + 1) * 128, :],
                                            in_=ytok[:]).then_inc(sgd.h, 16)
                            else:
                                g.indirect_dma_start(
                                    out=yscr_h, out_offset=bass.IndirectOffsetOnAxis(
                                        ap=tidx[:, q, cpart:cpart + 1], axis=0),
                                    in_=ytok[:], in_offset=None).then_inc(sgd.h, 16)
                            tsy = p.inc(g.drain(), sgi, 1)
                            p.buf_use("ytok", sgi, tsy)
                            p.buf_use("tidx", sgi, tsy)
                yscat_done = sgi.n
            acc_done = sdv.n
            barrier()

            with ExitStack() as dctx:
                xtD = dctx.enter_context(nc.sbuf_tensor("d%d_xt" % b, [128, 2, D], F32))
                ytD = dctx.enter_context(nc.sbuf_tensor("d%d_yt" % b, [128, 2, D], F32))
                for i in range(TC):
                    bi = i % 2
                    p.buf_wait(sq, "sq", "xcb%d" % bi)
                    p.buf_wait(sq, "sq", "xtD%d" % bi)
                    p.buf_wait(sq, "sq", "ytD%d" % bi)
                    tokx = ld(xtD[:, bi], x_h[b, i * 128:(i + 1) * 128, :], sldd[bi])
                    toky = ld(ytD[:, bi], yscr_h[i * 128:(i + 1) * 128, :], sldx[bi])
                    p.ps_acquire(t, ["A0", "A1", "A2", "A3"])
                    p.wait(t, "t", sdv, acc_done)
                    ti = None
                    for c in range(DC):
                        ti = t.transpose(psA[:, c * 256:c * 256 + 128],
                                         acc[:, c, i * 128:(i + 1) * 128], identf[:])
                    tpe = p.inc(ti, spe, 1)
                    p.wait(v, "v", spe, tpe)
                    p.wait(v, "v", sldd[bi], tokx)
                    p.wait(v, "v", sldx[bi], toky)
                    for c in range(DC):
                        v.tensor_tensor(xcb[:, bi, c * 128:(c + 1) * 128],
                                        psA[:, c * 256:c * 256 + 128],
                                        xtD[:, bi, c * 128:(c + 1) * 128], op=ALU.add)
                    v.drain()
                    tv = p.inc(v.tensor_tensor(xcb[:, bi], xcb[:, bi], ytD[:, bi],
                                               op=ALU.add), sdv, 1)
                    p.buf_use("xtD%d" % bi, sdv, tv)
                    p.buf_use("ytD%d" % bi, sdv, tv)
                    p.ps_release(["A0", "A1", "A2", "A3"], sdv, tv)
                    p.wait(sc, "sc", sdv, tv)
                    tst = p.inc(sc.dma_start(out=out_h[b, i * 128:(i + 1) * 128, :],
                                             in_=xcb[:, bi]), sstb[bi], 16)
                    p.buf_use("xcb%d" % bi, sstb[bi], tst)
            barrier()

        sc.wait_ge(sstb[0].h, sstb[0].n)
        sc.wait_ge(sstb[1].h, sstb[1].n)
        sq.wait_ge(sstb[0].h, sstb[0].n)
        sq.wait_ge(sstb[1].h, sstb[1].n)
    return nc



def kernel(**inputs):
    cfg = REAL
    shared = host_prep(cfg, inputs)
    x = np.ascontiguousarray(np.asarray(inputs["L0_attn_x"], dtype=np.float32))
    n_cores = 8
    per = x.shape[0] // n_cores
    nc = build_program(cfg)
    in_maps = []
    for c in range(n_cores):
        m = dict(shared)
        m["x"] = x[c * per:(c + 1) * per]
        in_maps.append(m)
    res = run_bass_kernel_spmd(nc, in_maps, core_ids=list(range(n_cores)))
    return np.concatenate([r["out"] for r in res.results], axis=0)


# version 13
# speedup vs baseline: 1.7587x; 1.1420x over previous
import numpy as np
import ml_dtypes
import concourse.bass as bass
import concourse.mybir as mybir
from concourse.bass_utils import run_bass_kernel_spmd

F32 = mybir.dt.float32
BF16 = mybir.dt.bfloat16
U32 = mybir.dt.uint32
AF = mybir.ActivationFunctionType
ALU = mybir.AluOpType
AX = mybir.AxisListType
BF = ml_dtypes.bfloat16

HD = 64
HALF = 32
S = 512


class CFG:
    def __init__(self, T=2048, D=1024, L=256, HID=4096, E=8, HPG=4, B=2):
        self.T, self.D, self.L, self.HID, self.E, self.HPG, self.B = T, D, L, HID, E, HPG, B
        self.HEADS = D // HD
        self.NG = self.HEADS // HPG
        self.DC = D // 128
        self.TC = T // 128
        self.NS = T // S
        self.LC = L // 128
        self.HJ = HID // 128
        self.UJG = min(8, self.HJ)
        self.NJG = self.HJ // self.UJG
        self.NH = max(1, T // 1024)
        self.HSZ = T // self.NH
        self.CPG = HPG * HD // 128
        self.CAPQ = 2
        assert D % 128 == 0 and T % S == 0 and L % 128 == 0
        assert HID % (128 * self.UJG) == 0
        assert self.HEADS % HPG == 0 and HPG % 2 == 0 and self.CPG >= 1
        assert HPG * HD <= 512


REAL = CFG()

DBG = {"plain_tok": False, "plain_gather": False, "plain_yscat": False, "skip_eq": False, "skip_slot": False, "no_xnscr": False, "eq_noxg": False, "eq_nou": False, "eq_noy": False, "eq_noflush": False, "cond_gather": True, "cond_u": True, "cond_y": True, "cond_flush": True, "cond_y_jgmax": 99, "cond_y_emax": 99, "y_dchmax": 99, "y_mm": True, "y_v": True, "cond_thr": -1, "y_njs": 99, "br_drain": True}


def _swap_cols(w, heads):
    out = np.empty_like(w)
    for h in range(heads):
        b = h * HD
        out[:, b:b + HALF] = w[:, b + HALF:b + HD]
        out[:, b + HALF:b + HD] = w[:, b:b + HALF]
    return out


def host_consts(cfg):
    T = cfg.T
    pos = np.arange(T, dtype=np.float64)[:, None]
    inv = 1.0 / 10000.0 ** (np.arange(HALF, dtype=np.float64) / HALF)
    ang = pos * inv[None, :]
    cosv = np.cos(ang).astype(np.float32).T
    sinv = np.sin(ang).astype(np.float32).T
    cos2 = np.concatenate([cosv, cosv], 0)
    sin2 = np.concatenate([-sinv, sinv], 0)
    masks = np.zeros((4, 128, S), BF)
    qf = np.arange(S)[None, :]
    kp = np.arange(128)[:, None]
    for off in range(4):
        masks[off] = ((kp + off * 128) <= qf).astype(BF)
    iota8 = np.broadcast_to(np.arange(cfg.E, dtype=np.float32), (128, cfg.E)).copy()
    ident = np.eye(128, dtype=np.float32)
    ones128 = np.ones((128, 128), BF)
    epsc = np.full((128, 1), 1e-5, np.float32)
    T, E = cfg.T, cfg.E
    tokv = np.broadcast_to(np.arange(T, dtype=np.float32)[:, None], (T, 8)).copy()
    sent8 = np.broadcast_to((T + np.arange(E * T, dtype=np.float32) % 128)[:, None],
                            (E * T, 8)).copy()
    etm1 = np.zeros((128, 1), np.float32)
    etm1[0:E, 0] = np.arange(E, dtype=np.float32) * T - 1.0
    onescf = np.ones((128, 1), np.float32)
    return dict(cos2t=cos2.astype(BF), sin2st=sin2.astype(BF), masks=masks,
                iota8=iota8, identf=ident, ones128=ones128, epsc=epsc,
                tokv=tokv, sent8=sent8, etm1=etm1, onescf=onescf)


def host_prep(cfg, iv):
    E, L, HID = cfg.E, cfg.L, cfg.HID
    d = dict(host_consts(cfg))
    d["wq"] = np.ascontiguousarray(iv["L0_attn_W_q"], dtype=np.float32)
    d["wqsw"] = np.ascontiguousarray(_swap_cols(d["wq"], cfg.HEADS))
    d["wdkv"] = np.ascontiguousarray(iv["L0_attn_W_dkv"], dtype=np.float32)
    d["wuk"] = np.ascontiguousarray(iv["L0_attn_W_uk"], dtype=np.float32)
    d["wuksw"] = np.ascontiguousarray(_swap_cols(d["wuk"], cfg.HEADS))
    d["wuv"] = np.ascontiguousarray(iv["L0_attn_W_uv"], dtype=np.float32)
    d["wo"] = np.ascontiguousarray(iv["L0_attn_W_o"], dtype=np.float32)
    lng = np.broadcast_to(np.asarray(iv["L0_attn_ln_gamma"], np.float32), (128, L))
    lnb = np.broadcast_to(np.asarray(iv["L0_attn_ln_beta"], np.float32), (128, L))
    d["lngb"] = np.ascontiguousarray(np.stack([lng, lnb]))
    d["rw"] = np.ascontiguousarray(iv["L0_ffn_router_w"], dtype=np.float32)
    d["rbt"] = np.ascontiguousarray(
        np.broadcast_to(np.asarray(iv["L0_ffn_router_b"], np.float32), (128, E)))
    d["w1"] = np.ascontiguousarray(iv["L0_ffn_w1"], dtype=np.float32)
    d["w2"] = np.ascontiguousarray(iv["L0_ffn_w2"], dtype=np.float32)
    b1 = np.asarray(iv["L0_ffn_b1"], np.float32)
    d["b1a"] = np.ascontiguousarray(b1[:, :HID].reshape(E, cfg.HJ, 128).transpose(0, 2, 1))
    d["b1g"] = np.ascontiguousarray(b1[:, HID:].reshape(E, cfg.HJ, 128).transpose(0, 2, 1))
    b2 = np.asarray(iv["L0_ffn_b2"], np.float32)
    d["b2pm"] = np.ascontiguousarray(
        b2.reshape(E, cfg.DC, 128).transpose(2, 0, 1))
    return d


class Sem:
    __slots__ = ("h", "n", "name")
    def __init__(self, h, name):
        self.h, self.n, self.name = h, 0, name


class P:
    def __init__(self):
        self.seen = {}
        self.slot = {}
        self.bufc = {}

    def inc(self, instr, sem, n=1):
        instr.then_inc(sem.h, n)
        sem.n += n
        return sem.n

    def wait(self, eng, ename, sem, val):
        if val <= 0:
            return
        key = (ename, sem.name)
        if self.seen.get(key, 0) >= val:
            return
        eng.wait_ge(sem.h, val)
        self.seen[key] = val

    def _w(self, eng, ename, d):
        for sem, tok in d.values():
            self.wait(eng, ename, sem, tok)

    def ps_acquire(self, teng, slots):
        for s_ in slots:
            self._w(teng, "t", self.slot.get(s_, {}))

    def ps_release(self, slots, sem, tok):
        for s_ in slots:
            self.slot.setdefault(s_, {})[sem.name] = (sem, tok)

    def buf_wait(self, eng, ename, name):
        self._w(eng, ename, self.bufc.get(name, {}))

    def buf_use(self, name, sem, tok):
        self.bufc.setdefault(name, {})[sem.name] = (sem, tok)


def build_program(cfg):
    from contextlib import ExitStack
    nc = bass.Bass("TRN2", target_bir_lowering=False, detect_race_conditions=False)
    T, D, L, HID, E, B = cfg.T, cfg.D, cfg.L, cfg.HID, cfg.E, cfg.B
    DC, TC, NS, LC, HJ = cfg.DC, cfg.TC, cfg.NS, cfg.LC, cfg.HJ
    NG, HPG, CPG, UJG, NJG = cfg.NG, cfg.HPG, cfg.CPG, cfg.UJG, cfg.NJG
    CAPQ = cfg.CAPQ
    NH, HSZ = cfg.NH, cfg.HSZ
    HC = HPG // 2
    VW = HPG * HD

    def din(name, shape, dty=F32):
        return nc.dram_tensor(name, list(shape), dty, kind="ExternalInput").ap()

    x_h = din("x", (B, T, D))
    wq_h = din("wq", (D, D)); wqsw_h = din("wqsw", (D, D))
    wdkv_h = din("wdkv", (D, L))
    wuk_h = din("wuk", (L, D)); wuksw_h = din("wuksw", (L, D))
    wuv_h = din("wuv", (L, D)); wo_h = din("wo", (D, D))
    lngb_h = din("lngb", (2, 128, L))
    rw_h = din("rw", (D, E)); rbt_h = din("rbt", (128, E))
    w1_h = din("w1", (E, D, 2 * HID)); w2_h = din("w2", (E, HID, D))
    b1a_h = din("b1a", (E, 128, HJ)); b1g_h = din("b1g", (E, 128, HJ))
    b2pm_h = din("b2pm", (128, E, DC))
    tokv_h = din("tokv", (T, 8))
    sent8_h = din("sent8", (E * T, 8))
    etm1_h = din("etm1", (128, 1))
    onescf_h = din("onescf", (128, 1))
    xnscr_h = nc.dram_tensor("xnscr", [T + 128, D], F32, kind="Internal").ap()
    yscr_h = nc.dram_tensor("yscr", [T + 128, D], F32, kind="Internal").ap()
    tokf_h = nc.dram_tensor("tokf", [E * T, 8], F32, kind="Internal").ap()
    cos_h = din("cos2t", (64, T), BF16); sin_h = din("sin2st", (64, T), BF16)
    masks_h = din("masks", (4, 128, S), BF16)
    iota_h = din("iota8", (128, E)); identf_h = din("identf", (128, 128))
    ones_h = din("ones128", (128, 128), BF16)
    epsc_h = din("epsc", (128, 1))
    out_h = nc.dram_tensor("out", [B, T, D], F32, kind="ExternalOutput").ap()

    with ExitStack() as _ctx:
        def _sb(*a):
            return _ctx.enter_context(nc.sbuf_tensor(*a))
        def _ps(*a):
            return _ctx.enter_context(nc.psum_tensor(*a))
        def _sem(n):
            return _ctx.enter_context(nc.semaphore(n))
        acc = _sb("g_acc", [128, DC, T], F32)
        ohT = _sb("g_ohT", [8, T], BF16)
        iota8 = _sb("g_iota8", [128, E], F32)
        identf = _sb("g_identf", [128, 128], F32)
        ones128 = _sb("g_ones128", [128, 128], BF16)
        rbt = _sb("g_rbt", [128, E], F32)
        lngb = _sb("g_lngb", [128, 2, L], F32)
        b1a = _sb("g_b1a", [128, E, HJ], F32)
        b1g = _sb("g_b1g", [128, E, HJ], F32)
        rwbf = _sb("g_rwbf", [128, DC, E], BF16)
        etm1 = _sb("g_etm1", [128, 1], F32)
        onescol_f = _sb("g_onescf", [128, 1], F32)
        b2pm = _sb("g_b2pm", [128, E, DC], F32)
        st_a = _sb("g_st_a", [128, DC, 128], F32)
        st_2 = _sb("g_st_2", [128, D], F32)
        bf_a = _sb("g_bf_a", [128, 2, DC, 128], BF16)
        bf_g = _sb("g_bf_g", [128, 2, DC, 128], BF16)
        xcb = _sb("g_xcb", [128, 2, D], F32)
        stat = _sb("g_stat", [128, 12], F32)
        lgb = _sb("g_lgb", [128, E], F32)
        mx8 = _sb("g_mx8", [128, 8], F32)
        idx8 = _sb("g_idx8", [128, 8], U32)
        ohf = _sb("g_ohf", [128, E], F32)
        tiny = _sb("g_tiny", [1, 16], F32)
        tinyi = _sb("g_tinyi", [1, 8], U32)
        tinyb = _sb("g_tinyb", [1, 2], BF16)
        negc = _sb("g_negc", [128, 1], F32)
        epsc = _sb("g_epsc", [128, 1], F32)
        psA = _ps("ps_A", [128, 2048], F32)
        psB = _ps("ps_B", [128, 2048], F32)
        h_ld = _sem("sldc"); h_pe = _sem("spe")
        h_dv = _sem("sdv"); h_ac = _sem("sac"); h_gp = _sem("sgp")
        h_la = _sem("slda"); h_l2 = _sem("sld2")
        h_lx0 = _sem("sldx0"); h_lx1 = _sem("sldx1")
        h_ld0 = _sem("sldd0"); h_ld1 = _sem("sldd1")
        h_st0 = _sem("sst0"); h_st1 = _sem("sst1")
        h_xn = _sem("sxn")
        h_gi = _sem("sgi")
        h_gd = _sem("sgd")
        block = _ctx.enter_context(nc.Block())

        p = P()
        sld = Sem(h_ld, "ldc"); spe = Sem(h_pe, "pe")
        sdv = Sem(h_dv, "dv"); sac = Sem(h_ac, "ac"); sgp = Sem(h_gp, "gp")
        slda = Sem(h_la, "lda"); sld2 = Sem(h_l2, "ld2")
        sldx = [Sem(h_lx0, "ldx0"), Sem(h_lx1, "ldx1")]
        sldd = [Sem(h_ld0, "ldd0"), Sem(h_ld1, "ldd1")]
        sstb = [Sem(h_st0, "st0"), Sem(h_st1, "st1")]
        sxn = Sem(h_xn, "xn")
        sgi = Sem(h_gi, "gi")
        sgd = Sem(h_gd, "gd")
        sq, t, v, sc, g = nc.sync, nc.tensor, nc.vector, nc.scalar, nc.gpsimd
        A = [psA[:, i * 512:(i + 1) * 512] for i in range(4)]
        Bp = [psB[:, i * 512:(i + 1) * 512] for i in range(4)]
        ALL_SEMS = (sld, spe, sdv, sac, sgp, slda, sld2, sldx[0], sldx[1],
                    sldd[0], sldd[1], sstb[0], sstb[1], sxn, sgi)
        ENGS = ((sq, "sq"), (t, "t"), (v, "v"), (sc, "sc"), (g, "g"))
        _COMP = {"pe": t, "dv": v, "ac": sc, "gp": g, "gi": g, "gd": g}

        def ld(dst, src, sem=None):
            sem = sem or sld
            return p.inc(sq.dma_start(out=dst, in_=src), sem, 16)

        def mmacc(out, pairs, slots, extra=()):
            p.ps_acquire(t, slots)
            for (sem, tok) in extra:
                p.wait(t, "t", sem, tok)
            n = len(pairs)
            last = None
            for i_, (l_, r_) in enumerate(pairs):
                last = t.matmul(out, l_, r_, start=(i_ == 0), stop=(i_ == n - 1))
            return last

        def barrier():
            for eng, en in ENGS:
                for sem in ALL_SEMS:
                    p.wait(eng, en, sem, sem.n)

        COMP_ENG = _COMP

        def cond_region(cregs, thr, emit_body):
            if DBG["br_drain"]:
                t.wait_ge(spe.h, spe.n)
            before = [(s, s.n) for s in ALL_SEMS]
            seen_save = dict(p.seen)
            with nc.If_cmp(cregs, thr, "IS_GT"):
                emit_body()
            nc.end_ifs()
            p.seen = seen_save
            with nc.If_cmp(cregs, thr, "IS_LE"):
                for s, n0 in before:
                    d = s.n - n0
                    if d:
                        COMP_ENG[s.name].sem_inc(s.h, d)
            nc.end_ifs()

        def stage_cast(dst, src_h, stage, stage_name, ceng, cename, csem, dst_name=None):
            lsem = {"st_a": slda, "st_2": sld2}[stage_name]
            p.buf_wait(sq, "sq", stage_name)
            tok = ld(stage, src_h, lsem)
            p.wait(ceng, cename, lsem, tok)
            if dst_name:
                p.buf_wait(ceng, cename, dst_name)
            if ceng is sc:
                tc = p.inc(sc.copy(dst, stage), csem, 1)
            elif ceng is v:
                tc = p.inc(v.tensor_copy(dst, stage), csem, 1)
            else:
                tc = p.inc(g.tensor_copy(dst, stage), csem, 1)
            p.buf_use(stage_name, csem, tc)
            return tc

        ld(iota8[:], iota_h); ld(identf[:], identf_h)
        ld(ones128[:], ones_h)
        ld(etm1[:], etm1_h); ld(onescol_f[:], onescf_h)
        ld(b2pm[:], b2pm_h)
        ld(rbt[:], rbt_h)
        ld(epsc[:], epsc_h)
        ld(lngb[:], lngb_h.rearrange("o p q -> p o q"))
        ld(b1a[:], b1a_h.rearrange("e p j -> p e j"))
        ld(b1g[:], b1g_h.rearrange("e p j -> p e j"))
        stage_cast(rwbf[:], rw_h.rearrange("(c p) e -> p c e", p=128),
                   st_a[:, :, 0:E], "st_a", g, "g", sgp)
        v.memset(xcb[:, 0], 0.0)
        tvz = p.inc(v.drain(), sdv, 1)
        p.wait(sq, "sq", sdv, tvz)
        p.inc(sq.dma_start(out=xnscr_h[T:T + 128, :], in_=xcb[:, 0]), sld, 16)
        p.inc(sq.dma_start(out=yscr_h[T:T + 128, :], in_=xcb[:, 0]), sld, 16)
        barrier()

        for b in range(B):
            with ExitStack() as bctx:
                def _bb(nm, shape, dty):
                    return bctx.enter_context(nc.sbuf_tensor("b%d_%s" % (b, nm), shape, dty))
                dumpA = _bb("dumpA", [128, D], F32)
                xnT = _bb("xnT", [128, DC, T], BF16)
                wdkvbf = _bb("wdkv", [128, DC, L], BF16)
                wukbf = _bb("wuk", [128, LC, D], BF16)
                wukswbf = _bb("wuksw", [128, LC, D], BF16)
                wuvbf = _bb("wuv", [128, LC, D], BF16)
                latnT = _bb("latnT", [128, LC, T], BF16)
                kroT = _bb("kroT", [128, HC, T], BF16)
                qroT = _bb("qroT", [128, HC, T], BF16)
                v65 = _bb("v65", [128, TC, HPG, 65], BF16)
                catT = _bb("catT", [128, CPG, S], BF16)
                ropet = _bb("ropet", [128, 2, S], F32)
                attnT = _bb("attnT", [128, 2, S], BF16)
                recipb = _bb("recipb", [128, S], BF16)
                bc_sb = _bb("bc_sb", [64, S], BF16)
                masksb = _bb("masksb", [128, 4, S], BF16)
                cos2t = _bb("cos2t", [64, T], BF16)
                sin2st = _bb("sin2st", [64, T], BF16)
                _hold = {}

                prelude = []
                def _step_consts():
                    ld(masksb[:], masks_h.rearrange("o p q -> p o q"))
                    ld(cos2t[:], cos_h)
                    _hold["b_ld"] = ld(sin2st[:], sin_h)
                prelude.append(_step_consts)
                for lc2 in range(L // 128):
                    prelude.append(lambda lc2=lc2: stage_cast(
                        wdkvbf[:, :, lc2 * 128:(lc2 + 1) * 128],
                        wdkv_h[:, lc2 * 128:(lc2 + 1) * 128].rearrange("(c p) l -> p c l", p=128),
                        st_a[:], "st_a", g, "g", sgp))
                for (hsrc, dstw) in ((wuk_h, wukbf), (wuksw_h, wukswbf), (wuv_h, wuvbf)):
                    for lc in range(LC):
                        prelude.append(lambda hsrc=hsrc, dstw=dstw, lc=lc: stage_cast(
                            dstw[:, lc, :], hsrc[lc * 128:(lc + 1) * 128, :],
                            st_2[:], "st_2", g, "g", sgp))
                def _step_ones():
                    v.memset(v65[:, :, :, 64:65], 1.0)
                    p.inc(v.drain(), sdv, 1)
                prelude.append(_step_ones)

                for i in range(TC):
                    bi = i % 2
                    xn_ = "xcb%d" % bi
                    p.buf_wait(sq, "sq", xn_)
                    tokx = ld(xcb[:, bi], x_h[b, i * 128:(i + 1) * 128, :], sldx[bi])
                    if prelude:
                        prelude.pop(0)()
                    p.wait(v, "v", sldx[bi], tokx)
                    v.reduce_sum(stat[:, 0:1], xcb[:, bi], axis=AX.X)
                    v.drain()
                    v.tensor_scalar(stat[:, 1:2], stat[:, 0:1], 1.0 / D, None, op0=ALU.mult)
                    v.drain()
                    tv = p.inc(v.tensor_scalar(xcb[:, bi], xcb[:, bi], stat[:, 1:2], None,
                                               op0=ALU.subtract), sdv, 1)
                    p.wait(sc, "sc", sdv, tv)
                    p.buf_wait(sc, "sc", "dumpA")
                    sc.activation(dumpA[:], xcb[:, bi], AF.Square, accum_out=stat[:, 2:3])
                    sc.drain()
                    sc.activation(stat[:, 3:4], stat[:, 2:3], AF.Ln, bias=epsc[:], scale=1.0 / D)
                    sc.drain()
                    ta = p.inc(sc.activation(stat[:, 4:5], stat[:, 3:4], AF.Exp, scale=-0.5), sac, 1)
                    p.buf_use("dumpA", sac, ta)
                    p.wait(v, "v", sac, ta)
                    v.drain()
                    tv2 = p.inc(v.tensor_scalar(xcb[:, bi], xcb[:, bi], stat[:, 4:5], None,
                                                op0=ALU.mult), sdv, 1)
                    if not DBG["no_xnscr"]:
                        p.wait(sc, "sc", sdv, tv2)
                        p.wait(sc, "sc", sxn, sxn.n)
                        txs = p.inc(sc.dma_start(out=xnscr_h[i * 128:(i + 1) * 128, :],
                                                 in_=xcb[:, bi]), sxn, 16)
                        p.buf_use("xcb%d" % bi, sxn, txs)
                    p.ps_acquire(t, ["A0", "A1", "A2", "A3"])
                    p.wait(t, "t", sdv, tv2)
                    ti = None
                    for c in range(DC):
                        ti = t.transpose(psA[:, c * 256:c * 256 + 128],
                                         xcb[:, bi, c * 128:(c + 1) * 128], identf[:])
                    tpe = p.inc(ti, spe, 1)
                    p.wait(v, "v", spe, tpe)
                    last = None
                    for c in range(DC):
                        last = v.tensor_copy(xnT[:, c, i * 128:(i + 1) * 128],
                                             psA[:, c * 256:c * 256 + 128])
                    tv3 = p.inc(last, sdv, 1)
                    p.buf_use(xn_, sdv, tv3)
                    p.ps_release(["A0", "A1", "A2", "A3"], sdv, tv3)
                    last = mmacc(psB[:, 0:E],
                                 [(xnT[:, c, i * 128:(i + 1) * 128], rwbf[:, c, :]) for c in range(DC)],
                                 ["B0"], extra=[(sdv, tv3)])
                    tpe2 = p.inc(last, spe, 1)
                    p.wait(v, "v", spe, tpe2)
                    v.tensor_tensor(lgb[:], psB[:, 0:E], rbt[:], op=ALU.add)
                    v.drain()
                    v.max(mx8[:], lgb[:])
                    v.drain()
                    v.max_index(idx8[:], mx8[:], lgb[:])
                    v.drain()
                    v.tensor_copy(stat[:, 5:6], idx8[:, 0:1])
                    v.drain()
                    tv4 = p.inc(v.tensor_scalar(ohf[:], iota8[:], stat[:, 5:6], None,
                                                op0=ALU.is_equal), sdv, 1)
                    p.ps_release(["B0"], sdv, tv4)
                    p.ps_acquire(t, ["B1"])
                    p.wait(t, "t", sdv, tv4)
                    ti = t.transpose(psB[0:E, 512:640], ohf[:], identf[:])
                    tpe3 = p.inc(ti, spe, 1)
                    p.wait(v, "v", spe, tpe3)
                    tv5 = p.inc(v.tensor_copy(ohT[:, i * 128:(i + 1) * 128], psB[0:E, 512:640]), sdv, 1)
                    p.ps_release(["B1"], sdv, tv5)
                oh_v = sdv.n
                for st in prelude:
                    st()
                prelude = []
                b_ld = _hold["b_ld"]
                barrier()

                for i in range(TC):
                    last = mmacc(psB[:, 1024:1280],
                                 [(xnT[:, c, i * 128:(i + 1) * 128], wdkvbf[:, c, :]) for c in range(DC)],
                                 ["B2"], extra=[(sgp, sgp.n)])
                    tpe = p.inc(last, spe, 1)
                    p.wait(v, "v", spe, tpe)
                    p.buf_wait(v, "v", "xcb0")
                    v.reduce_sum(stat[:, 6:7], psB[:, 1024:1280], axis=AX.X)
                    v.drain()
                    v.tensor_scalar(stat[:, 7:8], stat[:, 6:7], 1.0 / L, None, op0=ALU.mult)
                    v.drain()
                    tv = p.inc(v.tensor_scalar(xcb[:, 0, 0:L], psB[:, 1024:1280], stat[:, 7:8], None,
                                               op0=ALU.subtract), sdv, 1)
                    p.ps_release(["B2"], sdv, tv)
                    p.wait(sc, "sc", sdv, tv)
                    sc.activation(ropet[:, 0, 0:L], xcb[:, 0, 0:L], AF.Square, accum_out=stat[:, 8:9])
                    sc.drain()
                    sc.activation(stat[:, 9:10], stat[:, 8:9], AF.Ln, bias=epsc[:], scale=1.0 / L)
                    sc.drain()
                    ta = p.inc(sc.activation(stat[:, 10:11], stat[:, 9:10], AF.Exp, scale=-0.5), sac, 1)
                    p.wait(v, "v", sac, ta)
                    v.drain()
                    v.tensor_scalar(xcb[:, 1, 0:L], xcb[:, 0, 0:L], stat[:, 10:11], None, op0=ALU.mult)
                    v.drain()
                    v.tensor_tensor(xcb[:, 0, 0:L], xcb[:, 1, 0:L], lngb[:, 0, :], op=ALU.mult)
                    v.drain()
                    tv2 = p.inc(v.tensor_tensor(xcb[:, 1, 0:L], xcb[:, 0, 0:L], lngb[:, 1, :],
                                                op=ALU.add), sdv, 1)
                    p.buf_use("xcb0", sdv, tv2)
                    p.buf_use("xcb1", sdv, tv2)
                    p.ps_acquire(t, ["B3"])
                    p.wait(t, "t", sdv, tv2)
                    ti = None
                    for lc in range(LC):
                        ti = t.transpose(psB[:, 1536 + lc * 128:1536 + (lc + 1) * 128],
                                         xcb[:, 1, lc * 128:(lc + 1) * 128], identf[:])
                    tpe2 = p.inc(ti, spe, 1)
                    p.wait(v, "v", spe, tpe2)
                    last = None
                    for lc in range(LC):
                        last = v.tensor_copy(latnT[:, lc, i * 128:(i + 1) * 128],
                                             psB[:, 1536 + lc * 128:1536 + (lc + 1) * 128])
                    tv3 = p.inc(last, sdv, 1)
                    p.ps_release(["B3"], sdv, tv3)
                latn_v = sdv.n

                for grp in range(NG):
                    h0 = grp * HPG
                    for cc in range(HC):
                        c0 = (h0 + 2 * cc) * HD
                        bufq = cc % 2
                        tca = stage_cast(bf_a[:, bufq],
                                         wq_h[:, c0:c0 + 128].rearrange("(c p) m -> p c m", p=128),
                                         st_a[:], "st_a",
                                         g, "g", sgp, dst_name="bf_a%d" % bufq)
                        tcg = stage_cast(bf_g[:, bufq],
                                         wqsw_h[:, c0:c0 + 128].rearrange("(c p) m -> p c m", p=128),
                                         st_2[:].rearrange("p (c m) -> p c m", m=128), "st_2",
                                         g, "g", sgp, dst_name="bf_g%d" % bufq)
                        for par in range(2):
                            hh = h0 + 2 * cc + par
                            for s_ in range(NS):
                                mmacc(A[0][0:64, :],
                                      [(wukbf[:, lc, hh * HD:(hh + 1) * HD],
                                        latnT[:, lc, s_ * S:(s_ + 1) * S]) for lc in range(LC)],
                                      ["A0"], extra=[(sdv, latn_v)])
                                last = mmacc(A[1][0:64, :],
                                             [(wukswbf[:, lc, hh * HD:(hh + 1) * HD],
                                               latnT[:, lc, s_ * S:(s_ + 1) * S]) for lc in range(LC)],
                                             ["A1"])
                                tpe_k = p.inc(last, spe, 1)
                                mmacc(A[2][0:64, :],
                                      [(bf_a[:, bufq, c_, par * 64:par * 64 + 64],
                                        xnT[:, c_, s_ * S:(s_ + 1) * S]) for c_ in range(DC)],
                                      ["A2"], extra=[(sgp, max(tca, tcg))])
                                last = mmacc(A[3][0:64, :],
                                             [(bf_g[:, bufq, c_, par * 64:par * 64 + 64],
                                               xnT[:, c_, s_ * S:(s_ + 1) * S]) for c_ in range(DC)],
                                             ["A3"])
                                tpe_q = p.inc(last, spe, 1)
                                p.buf_use("bf_a%d" % bufq, spe, tpe_q)
                                p.buf_use("bf_g%d" % bufq, spe, tpe_q)
                                p.wait(v, "v", spe, tpe_k)
                                p.wait(v, "v", sld, b_ld)
                                p.buf_wait(v, "v", "kroT")
                                v.drain()
                                v.tensor_tensor(ropet[0:64, 0, :], A[0][0:64, :],
                                                cos2t[:, s_ * S:(s_ + 1) * S], op=ALU.mult)
                                v.tensor_tensor(ropet[0:64, 1, :], A[1][0:64, :],
                                                sin2st[:, s_ * S:(s_ + 1) * S], op=ALU.mult)
                                v.drain()
                                tv = p.inc(v.tensor_tensor(
                                    kroT[par * 64:par * 64 + 64, cc, s_ * S:(s_ + 1) * S],
                                    ropet[0:64, 0, :], ropet[0:64, 1, :], op=ALU.add), sdv, 1)
                                p.ps_release(["A0", "A1"], sdv, tv)
                                p.wait(v, "v", spe, tpe_q)
                                p.buf_wait(v, "v", "qroT")
                                v.drain()
                                v.tensor_tensor(ropet[0:64, 0, :], A[2][0:64, :],
                                                cos2t[:, s_ * S:(s_ + 1) * S], op=ALU.mult)
                                v.tensor_tensor(ropet[0:64, 1, :], A[3][0:64, :],
                                                sin2st[:, s_ * S:(s_ + 1) * S], op=ALU.mult)
                                v.drain()
                                tv = p.inc(v.tensor_tensor(
                                    qroT[par * 64:par * 64 + 64, cc, s_ * S:(s_ + 1) * S],
                                    ropet[0:64, 0, :], ropet[0:64, 1, :], op=ALU.add), sdv, 1)
                                p.ps_release(["A2", "A3"], sdv, tv)
                    kq_v = sdv.n

                    for i in range(TC):
                        p.buf_wait(v, "v", "v65")
                        last = mmacc(A[0][:, 0:VW],
                                     [(latnT[:, lc, i * 128:(i + 1) * 128],
                                       wuvbf[:, lc, h0 * HD:h0 * HD + VW]) for lc in range(LC)],
                                     ["A0"])
                        tpe = p.inc(last, spe, 1)
                        p.wait(v, "v", spe, tpe)
                        tv = p.inc(v.tensor_copy(v65[:, i, :, 0:64],
                                                 A[0][:, 0:VW].rearrange("p (h d) -> p h d", d=64)),
                                   sdv, 1)
                        p.ps_release(["A0"], sdv, tv)
                    v65_v = sdv.n

                    first = True
                    units = [(s_, par, cc) for s_ in range(NS) for par in range(2)
                             for cc in range(HC)]
                    for g0 in range(0, len(units), 4):
                        grp_u = units[g0:g0 + 4]
                        toks = []
                        for ui, (s_, par, cc) in enumerate(grp_u):
                            sb2 = ui % 2
                            p.buf_wait(sc, "sc", "attn%d" % sb2)
                            p.wait(sc, "sc", sdv, kq_v)
                            tq = p.inc(sc.activation(attnT[par * 64:par * 64 + 64, sb2, :],
                                                     kroT[par * 64:par * 64 + 64, cc,
                                                          s_ * S:(s_ + 1) * S],
                                                     AF.Square), sac, 1)
                            last = mmacc(Bp[ui][0:1, :],
                                         [(ones128[par * 64:par * 64 + 64, 0:1],
                                           attnT[par * 64:par * 64 + 64, sb2, :])],
                                         ["B%d" % ui], extra=[(sac, tq)])
                            tpe = p.inc(last, spe, 1)
                            p.buf_use("attn%d" % sb2, spe, tpe)
                        p.wait(v, "v", spe, spe.n)
                        v.reduce_max(tiny[0:1, 1:2], psB[0:1, 0:512 * len(grp_u)], axis=AX.X)
                        v.drain()
                        if first:
                            v.tensor_copy(tiny[0:1, 0:1], tiny[0:1, 1:2])
                            first = False
                        else:
                            v.tensor_tensor(tiny[0:1, 0:1], tiny[0:1, 0:1], tiny[0:1, 1:2],
                                            op=ALU.max)
                        tvm = p.inc(v.drain(), sdv, 1)
                        p.ps_release(["B0", "B1", "B2", "B3"], sdv, tvm)
                    mk_v = sdv.n

                    for s_ in range(NS):
                        units_q = [(par, cc) for par in range(2) for cc in range(HC)]
                        for g0 in range(0, len(units_q), 4):
                            grp_u = units_q[g0:g0 + 4]
                            for ui, (par, cc) in enumerate(grp_u):
                                sb2 = ui % 2
                                p.buf_wait(sc, "sc", "attn%d" % sb2)
                                tq = p.inc(sc.activation(attnT[par * 64:par * 64 + 64, sb2, :],
                                                         qroT[par * 64:par * 64 + 64, cc,
                                                              s_ * S:(s_ + 1) * S],
                                                         AF.Square), sac, 1)
                                last = mmacc(Bp[ui][0:1, :],
                                             [(ones128[par * 64:par * 64 + 64, 0:1],
                                               attnT[par * 64:par * 64 + 64, sb2, :])],
                                             ["B%d" % ui], extra=[(sac, tq)])
                                tpe = p.inc(last, spe, 1)
                                p.buf_use("attn%d" % sb2, spe, tpe)
                            p.wait(v, "v", spe, spe.n)
                            v.reduce_max(tiny[0:1, 2:3], psB[0:1, 0:512 * len(grp_u)], axis=AX.X)
                            v.drain()
                            if g0 == 0:
                                v.tensor_copy(tiny[0:1, 3:4], tiny[0:1, 2:3])
                            else:
                                v.tensor_tensor(tiny[0:1, 3:4], tiny[0:1, 3:4], tiny[0:1, 2:3],
                                                op=ALU.max)
                            tvq = p.inc(v.drain(), sdv, 1)
                            p.ps_release(["B0", "B1", "B2", "B3"], sdv, tvq)
                        p.wait(v, "v", sdv, mk_v)
                        tv = p.inc(v.tensor_tensor(tiny[0:1, 4:5], tiny[0:1, 3:4], tiny[0:1, 0:1],
                                                   op=ALU.mult), sdv, 1)
                        p.wait(sc, "sc", sdv, tv)
                        sc.activation(tiny[0:1, 5:6], tiny[0:1, 4:5], AF.Ln)
                        sc.drain()
                        ta = p.inc(sc.activation(tiny[0:1, 6:7], tiny[0:1, 5:6], AF.Exp, scale=0.5),
                                   sac, 1)
                        p.wait(v, "v", sac, ta)
                        v.tensor_scalar(tiny[0:1, 7:8], tiny[0:1, 6:7], -0.125, -2.0,
                                        op0=ALU.mult, op1=ALU.add)
                        v.drain()
                        tv = p.inc(v.tensor_copy(tinyb[0:1, 0:1], tiny[0:1, 7:8]), sdv, 1)
                        p.buf_wait(v, "v", "negc")
                        last = mmacc(psB[:, 1024:1025], [(ones128[0:1, :], tinyb[0:1, 0:1])],
                                     ["B2"], extra=[(sdv, tv)])
                        tpe = p.inc(last, spe, 1)
                        p.wait(v, "v", spe, tpe)
                        tnegc = p.inc(v.tensor_copy(negc[:, 0:1], psB[:, 1024:1025]), sdv, 1)
                        p.ps_release(["B2"], sdv, tnegc)

                        nkt = 4 * s_ + 4
                        for lh in range(HPG):
                            cc = lh // 2; par = lh % 2
                            ob = [0, 1, 3][lh % 3]
                            o65 = Bp[ob][0:65, :]
                            p.ps_acquire(t, ["B%d" % ob])
                            def emit_score(kt):
                                sb_ = kt % 2
                                p.ps_acquire(t, ["A%d" % sb_])
                                p.wait(t, "t", sdv, kq_v)
                                tmm = t.matmul(A[sb_],
                                               kroT[par * 64:par * 64 + 64, cc, kt * 128:(kt + 1) * 128],
                                               qroT[par * 64:par * 64 + 64, cc, s_ * S:(s_ + 1) * S],
                                               start=True, stop=True)
                                return p.inc(tmm, spe, 1)

                            tpe_s = emit_score(0)
                            for kt in range(nkt):
                                sb_ = kt % 2
                                stp = A[sb_]
                                abuf = attnT[:, sb_, :]
                                tpe = tpe_s
                                p.buf_wait(sc, "sc", "attn%d" % sb_)
                                p.wait(sc, "sc", sdv, tnegc)
                                p.wait(sc, "sc", spe, tpe)
                                tex = p.inc(sc.activation(abuf, stp, AF.Exp,
                                                          bias=negc[:, 0:1], scale=0.125), sac, 1)
                                p.buf_use("negc", sac, tex)
                                p.ps_release(["A%d" % sb_], sac, tex)
                                if kt + 1 < nkt:
                                    tpe_s = emit_score(kt + 1)
                                off = kt - 4 * s_
                                if off >= 0:
                                    p.wait(v, "v", sac, tex)
                                    tex2 = p.inc(v.tensor_tensor(abuf, abuf, masksb[:, off, :],
                                                                 op=ALU.mult), sdv, 1)
                                    p.wait(t, "t", sdv, tex2)
                                else:
                                    p.wait(t, "t", sac, tex)
                                p.wait(t, "t", sdv, v65_v)
                                last = t.matmul(o65, v65[:, kt, lh, :], abuf,
                                                start=(kt == 0), stop=(kt == nkt - 1))
                                tpe2 = p.inc(last, spe, 1)
                                p.buf_use("attn%d" % sb_, spe, tpe2)
                                p.buf_use("v65", spe, tpe2)
                            p.wait(sc, "sc", spe, spe.n)
                            p.buf_wait(sc, "sc", "recipb")
                            sc.activation(ropet[64:65, 0, :], o65[64:65, :], AF.Ln)
                            sc.drain()
                            trc = p.inc(sc.activation(recipb[64:65, :], ropet[64:65, 0, :],
                                                      AF.Exp, scale=-1.0), sac, 1)
                            last = mmacc(psB[0:64, 1024:1536],
                                         [(ones128[64:65, 0:64], recipb[64:65, :])],
                                         ["B2"], extra=[(sac, trc)])
                            tpe3 = p.inc(last, spe, 1)
                            p.buf_use("recipb", spe, tpe3)
                            p.wait(v, "v", spe, tpe3)
                            v.tensor_copy(bc_sb[:], psB[0:64, 1024:1536])
                            v.drain()
                            p.buf_wait(v, "v", "catT")
                            tv = p.inc(v.tensor_tensor(catT[par * 64:par * 64 + 64, cc, :],
                                                       o65[0:64, :], bc_sb[:], op=ALU.mult), sdv, 1)
                            p.ps_release(["B2"], sdv, tv)
                            p.ps_release(["B%d" % ob], sdv, tv)
                        cat_v = sdv.n

                        wo_tgt = [("bf_a0", bf_a[:, 0]), ("bf_g0", bf_g[:, 0])]
                        for ccw in range(CPG):
                            r0 = h0 * HD + ccw * 128
                            nm, tgt = wo_tgt[ccw % 2]
                            stage_cast(tgt.rearrange("p c m -> p (c m)"), wo_h[r0:r0 + 128, :],
                                       st_2[:], "st_2", g, "g", sgp, dst_name=nm)
                        wo_g = sgp.n
                        for dch in range(DC):
                            wsl = "A%d" % (2 + dch % 2)
                            wop = A[2 + dch % 2]
                            last = mmacc(wop,
                                         [(wo_tgt[ccw % 2][1][:, dch, :], catT[:, ccw, :])
                                          for ccw in range(CPG)],
                                         [wsl], extra=[(sgp, wo_g), (sdv, cat_v)])
                            tpe = p.inc(last, spe, 1)
                            p.buf_use("bf_a0", spe, tpe)
                            p.buf_use("bf_g0", spe, tpe)
                            p.buf_use("catT", spe, tpe)
                            p.wait(v, "v", spe, tpe)
                            dst = acc[:, dch, s_ * S:(s_ + 1) * S]
                            if grp == 0:
                                tv = p.inc(v.tensor_copy(dst, wop), sdv, 1)
                            else:
                                tv = p.inc(v.tensor_tensor(dst, dst, wop, op=ALU.add), sdv, 1)
                            p.ps_release([wsl], sdv, tv)
            barrier()

            with ExitStack() as cctx:
                def _cb(nm, shape, dty):
                    return cctx.enter_context(nc.sbuf_tensor("c%d_%s" % (b, nm), shape, dty))
                ubuf = _cb("ubuf", [128, UJG, CAPQ * S], BF16)
                w2g = _cb("w2g", [128, UJG, D], BF16)
                smb = _cb("smb", [128, 2, S], BF16)
                cumsb = _cb("cumsb", [8, S], F32)
                sidx = _cb("sidx", [128, 4], U32)
                tokvs = _cb("tokvs", [128, 4, 8], F32)
                tokw = _cb("tokw", [128, CAPQ, 4, 8], F32)
                tidx = _cb("tidx", [128, CAPQ, 4], U32)
                xgs = _cb("xgs", [128, D], F32)
                xgT = _cb("xgT", [128, CAPQ, DC, S], BF16)
                ySBq = _cb("ySBq", [128, DC, CAPQ * S], F32)
                ytok = _cb("ytok", [128, D], F32)
                slotb = ySBq[0:1, 0, 0:S]

                tok_sent = p.inc(sq.dma_start(out=tokf_h, in_=sent8_h), sld, 16)

                for s_ in ([] if DBG["skip_slot"] else range(NS)):
                    p.wait(v, "v", sdv, oh_v)
                    v.drain()
                    if s_ == 0:
                        v.tensor_tensor_scan(cumsb[:], ohT[:, s_ * S:(s_ + 1) * S],
                                             ohT[:, s_ * S:(s_ + 1) * S], 0.0,
                                             op0=ALU.add, op1=ALU.bypass)
                    else:
                        v.tensor_tensor_scan(cumsb[:], ohT[:, s_ * S:(s_ + 1) * S],
                                             ohT[:, s_ * S:(s_ + 1) * S], stat[0:8, 11:12],
                                             op0=ALU.add, op1=ALU.bypass)
                    v.drain()
                    v.tensor_copy(stat[0:8, 11:12], cumsb[:, S - 1:S])
                    v.drain()
                    v.tensor_scalar(cumsb[:], cumsb[:], etm1[0:8, :], None, op0=ALU.add)
                    v.drain()
                    tv = p.inc(v.tensor_tensor(cumsb[:], cumsb[:], ohT[:, s_ * S:(s_ + 1) * S],
                                               op=ALU.mult), sdv, 1)
                    last = mmacc(Bp[s_ % 4][0:1, :], [(onescol_f[0:8, :], cumsb[:])],
                                 ["B%d" % (s_ % 4)], extra=[(sdv, tv)])
                    tpe = p.inc(last, spe, 1)
                    p.wait(v, "v", spe, tpe)
                    p.buf_wait(v, "v", "slotb")
                    p.buf_wait(v, "v", "ySBq")
                    v.drain()
                    tv2 = p.inc(v.tensor_copy(slotb, Bp[s_ % 4][0:1, :]), sdv, 1)
                    p.buf_use("slotb", sdv, tv2)
                    p.ps_release(["B%d" % (s_ % 4)], sdv, tv2)
                    p.buf_wait(sq, "sq", "tokvs")
                    tokld = ld(tokvs[:], tokv_h[s_ * S:(s_ + 1) * S, :]
                               .rearrange("(c p) w -> p c w", p=128), sld2)
                    p.buf_wait(v, "v", "sidx")
                    for cpart in range(4):
                        last = mmacc(psA[:, 0:1], [(ySBq[0:1, 0, cpart * 128:(cpart + 1) * 128],
                                                    onescol_f[0:1, :])],
                                     ["A0"], extra=[(sdv, tv2)])
                        tpe2 = p.inc(last, spe, 1)
                        p.buf_use("slotb", spe, tpe2)
                        p.buf_use("ySBq", spe, tpe2)
                        p.wait(v, "v", spe, tpe2)
                        v.drain()
                        tv3 = p.inc(v.tensor_copy(sidx[:, cpart:cpart + 1], psA[:, 0:1]), sdv, 1)
                        p.ps_release(["A0"], sdv, tv3)
                    p.wait(g, "g", sld, tok_sent)
                    p.wait(g, "g", sld2, tokld)
                    p.wait(g, "g", sdv, sdv.n)
                    p.wait(g, "g", sxn, sxn.n)
                    for cpart in range(4):
                        if DBG["plain_tok"]:
                            g.dma_start(out=tokf_h[(s_ * 4 + cpart) * 128:(s_ * 4 + cpart + 1) * 128, :],
                                        in_=tokvs[:, cpart]).then_inc(sgd.h, 16)
                        else:
                            g.indirect_dma_start(
                                out=tokf_h,
                                out_offset=bass.IndirectOffsetOnAxis(ap=sidx[:, cpart:cpart + 1], axis=0),
                                in_=tokvs[:, cpart], in_offset=None).then_inc(sgd.h, 16)
                    tsc = p.inc(g.drain(), sgi, 1)
                    p.buf_use("tokvs", sgi, tsc)
                    p.buf_use("sidx", sgi, tsc)
                tokf_done = sgi.n

                last = mmacc(Bp[0][0:1, 0:8], [(stat[0:8, 11:12], identf[0:8, 0:8])],
                             ["B0"])
                tpe = p.inc(last, spe, 1)
                p.wait(v, "v", spe, tpe)
                v.drain()
                cnt_v = p.inc(v.tensor_copy(tinyi[:], Bp[0][0:1, 0:8]), sdv, 1)
                p.ps_release(["B0"], sdv, cnt_v)
                cregs = nc.alloc_registers("cnt%d" % b,
                                           engines=[mybir.EngineType.PE,
                                                    mybir.EngineType.DVE,
                                                    mybir.EngineType.Activation,
                                                    mybir.EngineType.Pool])

                for e in range(E):
                    for eng, en in ((t, "t"), (v, "v"), (sc, "sc"), (g, "g")):
                        p.wait(eng, en, sdv, cnt_v)
                    nc.regs_load(cregs, tinyi[0:1, e:e + 1])
                    p.wait(sq, "sq", sgi, tokf_done)
                    p.buf_wait(sq, "sq", "tokw")
                    tokld = ld(tokw[:, 0:CAPQ], tokf_h[e * T:e * T + CAPQ * S, :]
                               .rearrange("(c p) w -> p c w", p=128), sldx[e % 2])
                    p.wait(g, "g", sldx[e % 2], tokld)
                    p.wait(g, "g", sgi, tokf_done)
                    p.buf_wait(g, "g", "tidx")
                    tgi = p.inc(g.tensor_copy(tidx[:, 0:CAPQ], tokw[:, 0:CAPQ, :, 0]), sgp, 1)
                    p.buf_use("tokw", sgp, tgi)
                    def emit_gather(q):
                        for cpart in range(4):
                            p.buf_wait(g, "g", "xgs")
                            p.wait(g, "g", sxn, sxn.n)
                            p.wait(g, "g", sgp, tgi)
                            if DBG["plain_gather"]:
                                g.dma_start(out=xgs[:],
                                            in_=xnscr_h[cpart * 128:(cpart + 1) * 128, :]).then_inc(sgd.h, 16)
                            else:
                                g.indirect_dma_start(
                                    out=xgs[:], out_offset=None,
                                    in_=xnscr_h, in_offset=bass.IndirectOffsetOnAxis(
                                        ap=tidx[:, q, cpart:cpart + 1], axis=0)).then_inc(sgd.h, 16)
                            tgg = p.inc(g.drain(), sgi, 1)
                            p.buf_use("tidx", sgi, tgg)
                            for c0 in range(0, DC, 4):
                                cs = list(range(c0, min(c0 + 4, DC)))
                                p.ps_acquire(t, ["A0", "A1"])
                                p.wait(t, "t", sgi, tgg)
                                ti = None
                                for c in cs:
                                    ti = t.transpose(psA[:, (c % 4) * 256:(c % 4) * 256 + 128],
                                                     xgs[:, c * 128:(c + 1) * 128], identf[:])
                                tpe = p.inc(ti, spe, 1)
                                p.buf_use("xgs", spe, tpe)
                                p.wait(v, "v", spe, tpe)
                                p.buf_wait(v, "v", "xgT")
                                last = None
                                for c2 in cs:
                                    last = v.tensor_copy(
                                        xgT[:, q, c2, cpart * 128:(cpart + 1) * 128],
                                        psA[:, (c2 % 4) * 256:(c2 % 4) * 256 + 128])
                                tvx = p.inc(last, sdv, 1)
                                p.ps_release(["A0", "A1"], sdv, tvx)
                    emit_gather(0)
                    for q in range(1, CAPQ):
                        if DBG["cond_gather"]:
                            cond_region(cregs, q * 512, lambda q=q: emit_gather(q))
                        else:
                            emit_gather(q)
                    xg_v = sdv.n
                    p.buf_use("xgT", sdv, xg_v)

                    for jg in range(NJG):
                        for j8 in range(UJG):
                            j = jg * UJG + j8
                            wbuf = j % 2
                            tca = stage_cast(bf_a[:, wbuf],
                                             w1_h[e, :, j * 128:(j + 1) * 128]
                                             .rearrange("(c p) m -> p c m", p=128),
                                             st_a[:], "st_a",
                                             sc, "sc", sac, dst_name="bf_a%d" % wbuf)
                            xn_ = "xcb%d" % wbuf
                            p.buf_wait(sq, "sq", xn_)
                            tg_ = ld(xcb[:, wbuf].rearrange("p (c m) -> p c m", m=128),
                                     w1_h[e, :, HID + j * 128:HID + (j + 1) * 128]
                                     .rearrange("(c p) m -> p c m", p=128), sldd[wbuf])
                            p.buf_wait(v, "v", "bf_g%d" % wbuf)
                            p.wait(v, "v", sldd[wbuf], tg_)
                            tcg = p.inc(v.tensor_copy(bf_g[:, wbuf].rearrange("p c m -> p (c m)"),
                                                      xcb[:, wbuf]), sdv, 1)
                            p.buf_use(xn_, sdv, tcg)
                            tc2 = stage_cast(w2g[:, j8, :], w2_h[e, j * 128:(j + 1) * 128, :],
                                             st_2[:], "st_2", g, "g", sgp, dst_name="w2g%d" % j8)
                            def emit_u(q, j=j, j8=j8, wbuf=wbuf, tca=tca, tcg=tcg):
                                if DBG["eq_nou"]:
                                    return
                                us = (j * CAPQ + q) % 2
                                apn = psA[:, us * 1024:us * 1024 + 512]
                                gpn = psA[:, us * 1024 + 512:us * 1024 + 1024]
                                sla = "A%d" % (us * 2); slg = "A%d" % (us * 2 + 1)
                                mmacc(apn, [(bf_a[:, wbuf, c, :], xgT[:, q, c, :])
                                            for c in range(DC)], [sla],
                                      extra=[(sac, tca), (sdv, xg_v)])
                                last = mmacc(gpn, [(bf_g[:, wbuf, c, :], xgT[:, q, c, :])
                                                   for c in range(DC)], [slg], extra=[(sdv, tcg)])
                                tpe = p.inc(last, spe, 1)
                                p.buf_use("bf_a%d" % wbuf, spe, tpe)
                                p.buf_use("bf_g%d" % wbuf, spe, tpe)
                                p.buf_use("xgT", spe, tpe)
                                p.buf_wait(sc, "sc", "smb%d" % us)
                                p.wait(sc, "sc", spe, tpe)
                                tsg = p.inc(sc.activation(smb[:, us, :], gpn, AF.Sigmoid,
                                                          bias=b1g[:, e, j:j + 1]), sac, 1)
                                p.wait(v, "v", sac, tsg)
                                p.buf_wait(v, "v", "ubuf")
                                tvu = p.inc(v.scalar_tensor_tensor(
                                    ubuf[:, j8, q * S:(q + 1) * S],
                                    apn, b1a[:, e, j:j + 1], smb[:, us, :],
                                    op0=ALU.add, op1=ALU.mult), sdv, 1)
                                p.buf_use("smb%d" % us, sdv, tvu)
                                p.ps_release([sla, slg], sdv, tvu)
                            emit_u(0)
                            for q in range(1, CAPQ):
                                if DBG["cond_u"]:
                                    cond_region(cregs, q * 512, lambda q=q: emit_u(q))
                                else:
                                    emit_u(q)
                        u_v = sdv.n
                        w2_ready = sgp.n
                        def emit_y(q, jg=jg, dchmax=99, do_mm=True, do_v=True):
                            ndch = 0 if DBG["eq_noy"] else min(DC, dchmax)
                            for dch in range(ndch):
                                ys = dch % 2
                                yp = Bp[ys]
                                tpe = None
                                if do_mm:
                                    p.ps_acquire(t, ["B%d" % ys])
                                    p.wait(t, "t", sdv, u_v)
                                    p.wait(t, "t", sgp, w2_ready)
                                    last = None
                                    njs = min(UJG, DBG["y_njs"]) if q else UJG
                                    for j8 in range(njs):
                                        last = t.matmul(yp, w2g[:, j8, dch * 128:(dch + 1) * 128],
                                                        ubuf[:, j8, q * S:(q + 1) * S],
                                                        start=(j8 == 0), stop=(j8 == njs - 1))
                                    tpe = p.inc(last, spe, 1)
                                    for j8 in range(UJG):
                                        p.buf_use("w2g%d" % j8, spe, tpe)
                                    p.buf_use("ubuf", spe, tpe)
                                if do_v:
                                    if tpe is not None:
                                        p.wait(v, "v", spe, tpe)
                                    p.buf_wait(v, "v", "ySBq")
                                    dst = ySBq[:, dch, q * S:(q + 1) * S]
                                    if jg == 0:
                                        tv = p.inc(v.tensor_scalar(dst, yp,
                                                                   b2pm[:, e, dch:dch + 1], None,
                                                                   op0=ALU.add), sdv, 1)
                                    else:
                                        tv = p.inc(v.tensor_tensor(dst, dst, yp, op=ALU.add), sdv, 1)
                                    p.ps_release(["B%d" % ys], sdv, tv)
                                elif tpe is not None:
                                    p.ps_release(["B%d" % ys], spe, tpe)
                        u_v_local = u_v
                        emit_y(0)
                        for q in range(1, CAPQ):
                            if DBG["cond_y"] and jg < DBG["cond_y_jgmax"] and e < DBG["cond_y_emax"]:
                                thr_ = DBG["cond_thr"] if DBG["cond_thr"] >= 0 else q * 512
                                cond_region(cregs, thr_, lambda q=q: emit_y(
                                    q, dchmax=DBG["y_dchmax"], do_mm=DBG["y_mm"], do_v=DBG["y_v"]))
                            else:
                                emit_y(q)
                    y_v = sdv.n

                    def emit_flush(q):
                        if DBG["eq_noflush"]:
                            return
                        for cpart in range(4):
                            p.buf_wait(v, "v", "ytok")
                            p.ps_acquire(t, ["A0", "A1", "A2", "A3"])
                            p.wait(t, "t", sdv, y_v)
                            ti = None
                            for c in range(DC):
                                ti = t.transpose(psA[:, c * 256:c * 256 + 128],
                                                 ySBq[:, c, q * S + cpart * 128:q * S + (cpart + 1) * 128],
                                                 identf[:])
                            tpe = p.inc(ti, spe, 1)
                            p.buf_use("ySBq", spe, tpe)
                            p.wait(v, "v", spe, tpe)
                            last = None
                            for c in range(DC):
                                last = v.tensor_copy(ytok[:, c * 128:(c + 1) * 128],
                                                     psA[:, c * 256:c * 256 + 128])
                            tvy = p.inc(last, sdv, 1)
                            p.ps_release(["A0", "A1", "A2", "A3"], sdv, tvy)
                            p.wait(g, "g", sdv, tvy)
                            p.wait(g, "g", sgp, tgi)
                            if DBG["plain_yscat"]:
                                g.dma_start(out=yscr_h[cpart * 128:(cpart + 1) * 128, :],
                                            in_=ytok[:]).then_inc(sgd.h, 16)
                            else:
                                g.indirect_dma_start(
                                    out=yscr_h, out_offset=bass.IndirectOffsetOnAxis(
                                        ap=tidx[:, q, cpart:cpart + 1], axis=0),
                                    in_=ytok[:], in_offset=None).then_inc(sgd.h, 16)
                            tsy = p.inc(g.drain(), sgi, 1)
                            p.buf_use("ytok", sgi, tsy)
                            p.buf_use("tidx", sgi, tsy)
                    emit_flush(0)
                    for q in range(1, CAPQ):
                        if DBG["cond_flush"]:
                            cond_region(cregs, q * 512, lambda q=q: emit_flush(q))
                        else:
                            emit_flush(q)
                yscat_done = sgi.n
            acc_done = sdv.n
            barrier()

            with ExitStack() as dctx:
                xtD = dctx.enter_context(nc.sbuf_tensor("d%d_xt" % b, [128, 2, D], F32))
                ytD = dctx.enter_context(nc.sbuf_tensor("d%d_yt" % b, [128, 2, D], F32))
                for i in range(TC):
                    bi = i % 2
                    p.buf_wait(sq, "sq", "xcb%d" % bi)
                    p.buf_wait(sq, "sq", "xtD%d" % bi)
                    p.buf_wait(sq, "sq", "ytD%d" % bi)
                    tokx = ld(xtD[:, bi], x_h[b, i * 128:(i + 1) * 128, :], sldd[bi])
                    toky = ld(ytD[:, bi], yscr_h[i * 128:(i + 1) * 128, :], sldx[bi])
                    p.ps_acquire(t, ["A0", "A1", "A2", "A3"])
                    p.wait(t, "t", sdv, acc_done)
                    ti = None
                    for c in range(DC):
                        ti = t.transpose(psA[:, c * 256:c * 256 + 128],
                                         acc[:, c, i * 128:(i + 1) * 128], identf[:])
                    tpe = p.inc(ti, spe, 1)
                    p.wait(v, "v", spe, tpe)
                    p.wait(v, "v", sldd[bi], tokx)
                    p.wait(v, "v", sldx[bi], toky)
                    for c in range(DC):
                        v.tensor_tensor(xcb[:, bi, c * 128:(c + 1) * 128],
                                        psA[:, c * 256:c * 256 + 128],
                                        xtD[:, bi, c * 128:(c + 1) * 128], op=ALU.add)
                    v.drain()
                    tv = p.inc(v.tensor_tensor(xcb[:, bi], xcb[:, bi], ytD[:, bi],
                                               op=ALU.add), sdv, 1)
                    p.buf_use("xtD%d" % bi, sdv, tv)
                    p.buf_use("ytD%d" % bi, sdv, tv)
                    p.ps_release(["A0", "A1", "A2", "A3"], sdv, tv)
                    p.wait(sc, "sc", sdv, tv)
                    tst = p.inc(sc.dma_start(out=out_h[b, i * 128:(i + 1) * 128, :],
                                             in_=xcb[:, bi]), sstb[bi], 16)
                    p.buf_use("xcb%d" % bi, sstb[bi], tst)
            barrier()

        sc.wait_ge(sstb[0].h, sstb[0].n)
        sc.wait_ge(sstb[1].h, sstb[1].n)
        sq.wait_ge(sstb[0].h, sstb[0].n)
        sq.wait_ge(sstb[1].h, sstb[1].n)
    return nc



def kernel(**inputs):
    cfg = REAL
    shared = host_prep(cfg, inputs)
    x = np.ascontiguousarray(np.asarray(inputs["L0_attn_x"], dtype=np.float32))
    n_cores = 8
    per = x.shape[0] // n_cores
    nc = build_program(cfg)
    in_maps = []
    for c in range(n_cores):
        m = dict(shared)
        m["x"] = x[c * per:(c + 1) * per]
        in_maps.append(m)
    res = run_bass_kernel_spmd(nc, in_maps, core_ids=list(range(n_cores)))
    return np.concatenate([r["out"] for r in res.results], axis=0)


# version 14 (final)
# speedup vs baseline: 1.9010x; 1.0809x over previous
import numpy as np
import ml_dtypes
import concourse.bass as bass
import concourse.mybir as mybir
from concourse.bass_utils import run_bass_kernel_spmd

F32 = mybir.dt.float32
BF16 = mybir.dt.bfloat16
U32 = mybir.dt.uint32
AF = mybir.ActivationFunctionType
ALU = mybir.AluOpType
AX = mybir.AxisListType
BF = ml_dtypes.bfloat16

HD = 64
HALF = 32
S = 512


class CFG:
    def __init__(self, T=2048, D=1024, L=256, HID=4096, E=8, HPG=4, B=2):
        self.T, self.D, self.L, self.HID, self.E, self.HPG, self.B = T, D, L, HID, E, HPG, B
        self.HEADS = D // HD
        self.NG = self.HEADS // HPG
        self.DC = D // 128
        self.TC = T // 128
        self.NS = T // S
        self.LC = L // 128
        self.HJ = HID // 128
        self.UJG = min(8, self.HJ)
        self.NJG = self.HJ // self.UJG
        self.NH = max(1, T // 1024)
        self.HSZ = T // self.NH
        self.CPG = HPG * HD // 128
        self.CAPQ = 2
        assert D % 128 == 0 and T % S == 0 and L % 128 == 0
        assert HID % (128 * self.UJG) == 0
        assert self.HEADS % HPG == 0 and HPG % 2 == 0 and self.CPG >= 1
        assert HPG * HD <= 512


REAL = CFG()

DBG = {"plain_tok": False, "plain_gather": False, "plain_yscat": False, "skip_eq": False, "skip_slot": False, "no_xnscr": False, "eq_noxg": False, "eq_nou": False, "eq_noy": False, "eq_noflush": False, "cond_gather": True, "cond_u": True, "cond_y": True, "cond_flush": True, "cond_y_jgmax": 99, "cond_y_emax": 99, "y_dchmax": 99, "y_mm": True, "y_v": True, "cond_thr": -1, "y_njs": 99, "br_drain": True}


def _swap_cols(w, heads):
    out = np.empty_like(w)
    for h in range(heads):
        b = h * HD
        out[:, b:b + HALF] = w[:, b + HALF:b + HD]
        out[:, b + HALF:b + HD] = w[:, b:b + HALF]
    return out


def host_consts(cfg):
    T = cfg.T
    pos = np.arange(T, dtype=np.float64)[:, None]
    inv = 1.0 / 10000.0 ** (np.arange(HALF, dtype=np.float64) / HALF)
    ang = pos * inv[None, :]
    cosv = np.cos(ang).astype(np.float32).T
    sinv = np.sin(ang).astype(np.float32).T
    cos2 = np.concatenate([cosv, cosv], 0)
    sin2 = np.concatenate([-sinv, sinv], 0)
    masks = np.zeros((4, 128, S), BF)
    qf = np.arange(S)[None, :]
    kp = np.arange(128)[:, None]
    for off in range(4):
        masks[off] = ((kp + off * 128) <= qf).astype(BF)
    iota8 = np.broadcast_to(np.arange(cfg.E, dtype=np.float32), (128, cfg.E)).copy()
    ident = np.eye(128, dtype=np.float32)
    ones128 = np.ones((128, 128), BF)
    epsc = np.full((128, 1), 1e-5, np.float32)
    T, E = cfg.T, cfg.E
    tokv = np.broadcast_to(np.arange(T, dtype=np.float32)[:, None], (T, 8)).copy()
    sent8 = np.broadcast_to((T + np.arange(E * T, dtype=np.float32) % 128)[:, None],
                            (E * T, 8)).copy()
    etm1 = np.zeros((128, 1), np.float32)
    etm1[0:E, 0] = np.arange(E, dtype=np.float32) * T - 1.0
    onescf = np.ones((128, 1), np.float32)
    return dict(cos2t=cos2.astype(BF), sin2st=sin2.astype(BF), masks=masks,
                iota8=iota8, identf=ident, ones128=ones128, epsc=epsc,
                tokv=tokv, sent8=sent8, etm1=etm1, onescf=onescf)


def host_prep(cfg, iv):
    E, L, HID = cfg.E, cfg.L, cfg.HID
    d = dict(host_consts(cfg))
    d["wq"] = np.ascontiguousarray(iv["L0_attn_W_q"], dtype=np.float32)
    d["wqsw"] = np.ascontiguousarray(_swap_cols(d["wq"], cfg.HEADS))
    d["wdkv"] = np.ascontiguousarray(iv["L0_attn_W_dkv"], dtype=np.float32)
    d["wuk"] = np.ascontiguousarray(iv["L0_attn_W_uk"], dtype=np.float32)
    d["wuksw"] = np.ascontiguousarray(_swap_cols(d["wuk"], cfg.HEADS))
    d["wuv"] = np.ascontiguousarray(iv["L0_attn_W_uv"], dtype=np.float32)
    d["wo"] = np.ascontiguousarray(iv["L0_attn_W_o"], dtype=np.float32)
    lng = np.broadcast_to(np.asarray(iv["L0_attn_ln_gamma"], np.float32), (128, L))
    lnb = np.broadcast_to(np.asarray(iv["L0_attn_ln_beta"], np.float32), (128, L))
    d["lngb"] = np.ascontiguousarray(np.stack([lng, lnb]))
    d["rw"] = np.ascontiguousarray(iv["L0_ffn_router_w"], dtype=np.float32)
    d["rbt"] = np.ascontiguousarray(
        np.broadcast_to(np.asarray(iv["L0_ffn_router_b"], np.float32), (128, E)))
    d["w1"] = np.ascontiguousarray(iv["L0_ffn_w1"], dtype=np.float32)
    d["w2"] = np.ascontiguousarray(iv["L0_ffn_w2"], dtype=np.float32)
    b1 = np.asarray(iv["L0_ffn_b1"], np.float32)
    d["b1a"] = np.ascontiguousarray(b1[:, :HID].reshape(E, cfg.HJ, 128).transpose(0, 2, 1))
    d["b1g"] = np.ascontiguousarray(b1[:, HID:].reshape(E, cfg.HJ, 128).transpose(0, 2, 1))
    b2 = np.asarray(iv["L0_ffn_b2"], np.float32)
    d["b2pm"] = np.ascontiguousarray(
        b2.reshape(E, cfg.DC, 128).transpose(2, 0, 1))
    return d


class Sem:
    __slots__ = ("h", "n", "name")
    def __init__(self, h, name):
        self.h, self.n, self.name = h, 0, name


class P:
    def __init__(self):
        self.seen = {}
        self.slot = {}
        self.bufc = {}

    def inc(self, instr, sem, n=1):
        instr.then_inc(sem.h, n)
        sem.n += n
        return sem.n

    def wait(self, eng, ename, sem, val):
        if val <= 0:
            return
        key = (ename, sem.name)
        if self.seen.get(key, 0) >= val:
            return
        eng.wait_ge(sem.h, val)
        self.seen[key] = val

    def _w(self, eng, ename, d):
        for sem, tok in d.values():
            self.wait(eng, ename, sem, tok)

    def ps_acquire(self, teng, slots):
        for s_ in slots:
            self._w(teng, "t", self.slot.get(s_, {}))

    def ps_release(self, slots, sem, tok):
        for s_ in slots:
            self.slot.setdefault(s_, {})[sem.name] = (sem, tok)

    def buf_wait(self, eng, ename, name):
        self._w(eng, ename, self.bufc.get(name, {}))

    def buf_use(self, name, sem, tok):
        self.bufc.setdefault(name, {})[sem.name] = (sem, tok)


def build_program(cfg):
    from contextlib import ExitStack
    nc = bass.Bass("TRN2", target_bir_lowering=False, detect_race_conditions=False)
    T, D, L, HID, E, B = cfg.T, cfg.D, cfg.L, cfg.HID, cfg.E, cfg.B
    DC, TC, NS, LC, HJ = cfg.DC, cfg.TC, cfg.NS, cfg.LC, cfg.HJ
    NG, HPG, CPG, UJG, NJG = cfg.NG, cfg.HPG, cfg.CPG, cfg.UJG, cfg.NJG
    CAPQ = cfg.CAPQ
    NH, HSZ = cfg.NH, cfg.HSZ
    HC = HPG // 2
    VW = HPG * HD

    def din(name, shape, dty=F32):
        return nc.dram_tensor(name, list(shape), dty, kind="ExternalInput").ap()

    x_h = din("x", (B, T, D))
    wq_h = din("wq", (D, D)); wqsw_h = din("wqsw", (D, D))
    wdkv_h = din("wdkv", (D, L))
    wuk_h = din("wuk", (L, D)); wuksw_h = din("wuksw", (L, D))
    wuv_h = din("wuv", (L, D)); wo_h = din("wo", (D, D))
    lngb_h = din("lngb", (2, 128, L))
    rw_h = din("rw", (D, E)); rbt_h = din("rbt", (128, E))
    w1_h = din("w1", (E, D, 2 * HID)); w2_h = din("w2", (E, HID, D))
    b1a_h = din("b1a", (E, 128, HJ)); b1g_h = din("b1g", (E, 128, HJ))
    b2pm_h = din("b2pm", (128, E, DC))
    tokv_h = din("tokv", (T, 8))
    sent8_h = din("sent8", (E * T, 8))
    etm1_h = din("etm1", (128, 1))
    onescf_h = din("onescf", (128, 1))
    xnscr_h = nc.dram_tensor("xnscr", [T + 128, D], F32, kind="Internal").ap()
    yscr_h = nc.dram_tensor("yscr", [T + 128, D], F32, kind="Internal").ap()
    tokf_h = nc.dram_tensor("tokf", [E * T, 8], F32, kind="Internal").ap()
    cos_h = din("cos2t", (64, T), BF16); sin_h = din("sin2st", (64, T), BF16)
    masks_h = din("masks", (4, 128, S), BF16)
    iota_h = din("iota8", (128, E)); identf_h = din("identf", (128, 128))
    ones_h = din("ones128", (128, 128), BF16)
    epsc_h = din("epsc", (128, 1))
    out_h = nc.dram_tensor("out", [B, T, D], F32, kind="ExternalOutput").ap()

    with ExitStack() as _ctx:
        def _sb(*a):
            return _ctx.enter_context(nc.sbuf_tensor(*a))
        def _ps(*a):
            return _ctx.enter_context(nc.psum_tensor(*a))
        def _sem(n):
            return _ctx.enter_context(nc.semaphore(n))
        acc = _sb("g_acc", [128, DC, T], F32)
        ohT = _sb("g_ohT", [8, T], BF16)
        iota8 = _sb("g_iota8", [128, E], F32)
        identf = _sb("g_identf", [128, 128], F32)
        ones128 = _sb("g_ones128", [128, 128], BF16)
        rbt = _sb("g_rbt", [128, E], F32)
        lngb = _sb("g_lngb", [128, 2, L], F32)
        b1a = _sb("g_b1a", [128, E, HJ], F32)
        b1g = _sb("g_b1g", [128, E, HJ], F32)
        rwbf = _sb("g_rwbf", [128, DC, E], BF16)
        etm1 = _sb("g_etm1", [128, 1], F32)
        onescol_f = _sb("g_onescf", [128, 1], F32)
        b2pm = _sb("g_b2pm", [128, E, DC], F32)
        st_a = _sb("g_st_a", [128, DC, 128], F32)
        st_2 = _sb("g_st_2", [128, D], F32)
        bf_a = _sb("g_bf_a", [128, 2, DC, 128], BF16)
        bf_g = _sb("g_bf_g", [128, 2, DC, 128], BF16)
        xcb = _sb("g_xcb", [128, 2, D], F32)
        stat = _sb("g_stat", [128, 12], F32)
        lgb = _sb("g_lgb", [128, E], F32)
        mx8 = _sb("g_mx8", [128, 8], F32)
        idx8 = _sb("g_idx8", [128, 8], U32)
        ohf = _sb("g_ohf", [128, E], F32)
        tiny = _sb("g_tiny", [1, 16], F32)
        tinyi = _sb("g_tinyi", [1, 8], U32)
        tinyb = _sb("g_tinyb", [1, 2], BF16)
        negc = _sb("g_negc", [128, 1], F32)
        epsc = _sb("g_epsc", [128, 1], F32)
        psA = _ps("ps_A", [128, 2048], F32)
        psB = _ps("ps_B", [128, 2048], F32)
        h_ld = _sem("sldc"); h_pe = _sem("spe")
        h_dv = _sem("sdv"); h_ac = _sem("sac"); h_gp = _sem("sgp")
        h_la = _sem("slda"); h_l2 = _sem("sld2")
        h_lx0 = _sem("sldx0"); h_lx1 = _sem("sldx1")
        h_ld0 = _sem("sldd0"); h_ld1 = _sem("sldd1")
        h_st0 = _sem("sst0"); h_st1 = _sem("sst1")
        h_xn = _sem("sxn")
        h_gi = _sem("sgi")
        h_gd = _sem("sgd")
        block = _ctx.enter_context(nc.Block())

        p = P()
        sld = Sem(h_ld, "ldc"); spe = Sem(h_pe, "pe")
        sdv = Sem(h_dv, "dv"); sac = Sem(h_ac, "ac"); sgp = Sem(h_gp, "gp")
        slda = Sem(h_la, "lda"); sld2 = Sem(h_l2, "ld2")
        sldx = [Sem(h_lx0, "ldx0"), Sem(h_lx1, "ldx1")]
        sldd = [Sem(h_ld0, "ldd0"), Sem(h_ld1, "ldd1")]
        sstb = [Sem(h_st0, "st0"), Sem(h_st1, "st1")]
        sxn = Sem(h_xn, "xn")
        sgi = Sem(h_gi, "gi")
        sgd = Sem(h_gd, "gd")
        sq, t, v, sc, g = nc.sync, nc.tensor, nc.vector, nc.scalar, nc.gpsimd
        A = [psA[:, i * 512:(i + 1) * 512] for i in range(4)]
        Bp = [psB[:, i * 512:(i + 1) * 512] for i in range(4)]
        ALL_SEMS = (sld, spe, sdv, sac, sgp, slda, sld2, sldx[0], sldx[1],
                    sldd[0], sldd[1], sstb[0], sstb[1], sxn, sgi)
        ENGS = ((sq, "sq"), (t, "t"), (v, "v"), (sc, "sc"), (g, "g"))
        _COMP = {"pe": t, "dv": v, "ac": sc, "gp": g, "gi": g, "gd": g}

        def ld(dst, src, sem=None):
            sem = sem or sld
            return p.inc(sq.dma_start(out=dst, in_=src), sem, 16)

        def mmacc(out, pairs, slots, extra=()):
            p.ps_acquire(t, slots)
            for (sem, tok) in extra:
                p.wait(t, "t", sem, tok)
            n = len(pairs)
            last = None
            for i_, (l_, r_) in enumerate(pairs):
                last = t.matmul(out, l_, r_, start=(i_ == 0), stop=(i_ == n - 1))
            return last

        def barrier():
            for eng, en in ENGS:
                for sem in ALL_SEMS:
                    p.wait(eng, en, sem, sem.n)

        COMP_ENG = _COMP

        def cond_region(cregs, thr, emit_body):
            if DBG["br_drain"]:
                t.wait_ge(spe.h, spe.n)
            before = [(s, s.n) for s in ALL_SEMS]
            seen_save = dict(p.seen)
            with nc.If_cmp(cregs, thr, "IS_GT"):
                emit_body()
            nc.end_ifs()
            p.seen = seen_save
            with nc.If_cmp(cregs, thr, "IS_LE"):
                for s, n0 in before:
                    d = s.n - n0
                    if d:
                        COMP_ENG[s.name].sem_inc(s.h, d)
            nc.end_ifs()

        def stage_cast(dst, src_h, stage, stage_name, ceng, cename, csem, dst_name=None):
            lsem = {"st_a": slda, "st_2": sld2}[stage_name]
            p.buf_wait(sq, "sq", stage_name)
            tok = ld(stage, src_h, lsem)
            p.wait(ceng, cename, lsem, tok)
            if dst_name:
                p.buf_wait(ceng, cename, dst_name)
            if ceng is sc:
                tc = p.inc(sc.copy(dst, stage), csem, 1)
            elif ceng is v:
                tc = p.inc(v.tensor_copy(dst, stage), csem, 1)
            else:
                tc = p.inc(g.tensor_copy(dst, stage), csem, 1)
            p.buf_use(stage_name, csem, tc)
            return tc

        ld(iota8[:], iota_h); ld(identf[:], identf_h)
        ld(ones128[:], ones_h)
        ld(etm1[:], etm1_h); ld(onescol_f[:], onescf_h)
        ld(b2pm[:], b2pm_h)
        ld(rbt[:], rbt_h)
        ld(epsc[:], epsc_h)
        ld(lngb[:], lngb_h.rearrange("o p q -> p o q"))
        ld(b1a[:], b1a_h.rearrange("e p j -> p e j"))
        ld(b1g[:], b1g_h.rearrange("e p j -> p e j"))
        stage_cast(rwbf[:], rw_h.rearrange("(c p) e -> p c e", p=128),
                   st_a[:, :, 0:E], "st_a", g, "g", sgp)
        v.memset(xcb[:, 0], 0.0)
        tvz = p.inc(v.drain(), sdv, 1)
        p.wait(sq, "sq", sdv, tvz)
        p.inc(sq.dma_start(out=xnscr_h[T:T + 128, :], in_=xcb[:, 0]), sld, 16)
        p.inc(sq.dma_start(out=yscr_h[T:T + 128, :], in_=xcb[:, 0]), sld, 16)
        barrier()

        for b in range(B):
            with ExitStack() as bctx:
                def _bb(nm, shape, dty):
                    return bctx.enter_context(nc.sbuf_tensor("b%d_%s" % (b, nm), shape, dty))
                dumpA = _bb("dumpA", [128, D], F32)
                xnT = _bb("xnT", [128, DC, T], BF16)
                wdkvbf = _bb("wdkv", [128, DC, L], BF16)
                wukbf = _bb("wuk", [128, LC, D], BF16)
                wukswbf = _bb("wuksw", [128, LC, D], BF16)
                wuvbf = _bb("wuv", [128, LC, D], BF16)
                latnT = _bb("latnT", [128, LC, T], BF16)
                kroT = _bb("kroT", [128, HC, T], BF16)
                qroT = _bb("qroT", [128, HC, T], BF16)
                v65 = _bb("v65", [128, TC, HPG, 65], BF16)
                catT = _bb("catT", [128, CPG, S], BF16)
                ropet = _bb("ropet", [128, 2, S], F32)
                attnT = _bb("attnT", [128, 2, S], BF16)
                recipb = _bb("recipb", [128, S], BF16)
                bc_sb = _bb("bc_sb", [64, S], BF16)
                masksb = _bb("masksb", [128, 4, S], BF16)
                cos2t = _bb("cos2t", [64, T], BF16)
                sin2st = _bb("sin2st", [64, T], BF16)
                _hold = {}

                prelude = []
                def _step_consts():
                    ld(masksb[:], masks_h.rearrange("o p q -> p o q"))
                    ld(cos2t[:], cos_h)
                    _hold["b_ld"] = ld(sin2st[:], sin_h)
                prelude.append(_step_consts)
                for lc2 in range(L // 128):
                    prelude.append(lambda lc2=lc2: stage_cast(
                        wdkvbf[:, :, lc2 * 128:(lc2 + 1) * 128],
                        wdkv_h[:, lc2 * 128:(lc2 + 1) * 128].rearrange("(c p) l -> p c l", p=128),
                        st_a[:], "st_a", g, "g", sgp))
                for (hsrc, dstw) in ((wuk_h, wukbf), (wuksw_h, wukswbf), (wuv_h, wuvbf)):
                    for lc in range(LC):
                        prelude.append(lambda hsrc=hsrc, dstw=dstw, lc=lc: stage_cast(
                            dstw[:, lc, :], hsrc[lc * 128:(lc + 1) * 128, :],
                            st_2[:], "st_2", g, "g", sgp))
                def _step_ones():
                    v.memset(v65[:, :, :, 64:65], 1.0)
                    p.inc(v.drain(), sdv, 1)
                prelude.append(_step_ones)

                for i in range(TC):
                    bi = i % 2
                    xn_ = "xcb%d" % bi
                    p.buf_wait(sq, "sq", xn_)
                    tokx = ld(xcb[:, bi], x_h[b, i * 128:(i + 1) * 128, :], sldx[bi])
                    if prelude:
                        prelude.pop(0)()
                    p.wait(v, "v", sldx[bi], tokx)
                    v.reduce_sum(stat[:, 0:1], xcb[:, bi], axis=AX.X)
                    v.drain()
                    v.tensor_scalar(stat[:, 1:2], stat[:, 0:1], 1.0 / D, None, op0=ALU.mult)
                    v.drain()
                    tv = p.inc(v.tensor_scalar(xcb[:, bi], xcb[:, bi], stat[:, 1:2], None,
                                               op0=ALU.subtract), sdv, 1)
                    p.wait(sc, "sc", sdv, tv)
                    p.buf_wait(sc, "sc", "dumpA")
                    sc.activation(dumpA[:], xcb[:, bi], AF.Square, accum_out=stat[:, 2:3])
                    sc.drain()
                    sc.activation(stat[:, 3:4], stat[:, 2:3], AF.Ln, bias=epsc[:], scale=1.0 / D)
                    sc.drain()
                    ta = p.inc(sc.activation(stat[:, 4:5], stat[:, 3:4], AF.Exp, scale=-0.5), sac, 1)
                    p.buf_use("dumpA", sac, ta)
                    p.wait(v, "v", sac, ta)
                    v.drain()
                    tv2 = p.inc(v.tensor_scalar(xcb[:, bi], xcb[:, bi], stat[:, 4:5], None,
                                                op0=ALU.mult), sdv, 1)
                    if not DBG["no_xnscr"]:
                        p.wait(sc, "sc", sdv, tv2)
                        p.wait(sc, "sc", sxn, sxn.n)
                        txs = p.inc(sc.dma_start(out=xnscr_h[i * 128:(i + 1) * 128, :],
                                                 in_=xcb[:, bi]), sxn, 16)
                        p.buf_use("xcb%d" % bi, sxn, txs)
                    p.ps_acquire(t, ["A0", "A1", "A2", "A3"])
                    p.wait(t, "t", sdv, tv2)
                    ti = None
                    for c in range(DC):
                        ti = t.transpose(psA[:, c * 256:c * 256 + 128],
                                         xcb[:, bi, c * 128:(c + 1) * 128], identf[:])
                    tpe = p.inc(ti, spe, 1)
                    p.wait(v, "v", spe, tpe)
                    last = None
                    for c in range(DC):
                        last = v.tensor_copy(xnT[:, c, i * 128:(i + 1) * 128],
                                             psA[:, c * 256:c * 256 + 128])
                    tv3 = p.inc(last, sdv, 1)
                    p.buf_use(xn_, sdv, tv3)
                    p.ps_release(["A0", "A1", "A2", "A3"], sdv, tv3)
                    last = mmacc(psB[:, 0:E],
                                 [(xnT[:, c, i * 128:(i + 1) * 128], rwbf[:, c, :]) for c in range(DC)],
                                 ["B0"], extra=[(sdv, tv3)])
                    tpe2 = p.inc(last, spe, 1)
                    p.wait(v, "v", spe, tpe2)
                    v.tensor_tensor(lgb[:], psB[:, 0:E], rbt[:], op=ALU.add)
                    v.drain()
                    v.max(mx8[:], lgb[:])
                    v.drain()
                    v.max_index(idx8[:], mx8[:], lgb[:])
                    v.drain()
                    v.tensor_copy(stat[:, 5:6], idx8[:, 0:1])
                    v.drain()
                    tv4 = p.inc(v.tensor_scalar(ohf[:], iota8[:], stat[:, 5:6], None,
                                                op0=ALU.is_equal), sdv, 1)
                    p.ps_release(["B0"], sdv, tv4)
                    p.ps_acquire(t, ["B1"])
                    p.wait(t, "t", sdv, tv4)
                    ti = t.transpose(psB[0:E, 512:640], ohf[:], identf[:])
                    tpe3 = p.inc(ti, spe, 1)
                    p.wait(v, "v", spe, tpe3)
                    tv5 = p.inc(v.tensor_copy(ohT[:, i * 128:(i + 1) * 128], psB[0:E, 512:640]), sdv, 1)
                    p.ps_release(["B1"], sdv, tv5)
                oh_v = sdv.n
                for st in prelude:
                    st()
                prelude = []
                b_ld = _hold["b_ld"]
                barrier()

                for i in range(TC):
                    last = mmacc(psB[:, 1024:1280],
                                 [(xnT[:, c, i * 128:(i + 1) * 128], wdkvbf[:, c, :]) for c in range(DC)],
                                 ["B2"], extra=[(sgp, sgp.n)])
                    tpe = p.inc(last, spe, 1)
                    p.wait(v, "v", spe, tpe)
                    p.buf_wait(v, "v", "xcb0")
                    v.reduce_sum(stat[:, 6:7], psB[:, 1024:1280], axis=AX.X)
                    v.drain()
                    v.tensor_scalar(stat[:, 7:8], stat[:, 6:7], 1.0 / L, None, op0=ALU.mult)
                    v.drain()
                    tv = p.inc(v.tensor_scalar(xcb[:, 0, 0:L], psB[:, 1024:1280], stat[:, 7:8], None,
                                               op0=ALU.subtract), sdv, 1)
                    p.ps_release(["B2"], sdv, tv)
                    p.wait(sc, "sc", sdv, tv)
                    sc.activation(ropet[:, 0, 0:L], xcb[:, 0, 0:L], AF.Square, accum_out=stat[:, 8:9])
                    sc.drain()
                    sc.activation(stat[:, 9:10], stat[:, 8:9], AF.Ln, bias=epsc[:], scale=1.0 / L)
                    sc.drain()
                    ta = p.inc(sc.activation(stat[:, 10:11], stat[:, 9:10], AF.Exp, scale=-0.5), sac, 1)
                    p.wait(v, "v", sac, ta)
                    v.drain()
                    v.tensor_scalar(xcb[:, 1, 0:L], xcb[:, 0, 0:L], stat[:, 10:11], None, op0=ALU.mult)
                    v.drain()
                    v.tensor_tensor(xcb[:, 0, 0:L], xcb[:, 1, 0:L], lngb[:, 0, :], op=ALU.mult)
                    v.drain()
                    tv2 = p.inc(v.tensor_tensor(xcb[:, 1, 0:L], xcb[:, 0, 0:L], lngb[:, 1, :],
                                                op=ALU.add), sdv, 1)
                    p.buf_use("xcb0", sdv, tv2)
                    p.buf_use("xcb1", sdv, tv2)
                    p.ps_acquire(t, ["B3"])
                    p.wait(t, "t", sdv, tv2)
                    ti = None
                    for lc in range(LC):
                        ti = t.transpose(psB[:, 1536 + lc * 128:1536 + (lc + 1) * 128],
                                         xcb[:, 1, lc * 128:(lc + 1) * 128], identf[:])
                    tpe2 = p.inc(ti, spe, 1)
                    p.wait(v, "v", spe, tpe2)
                    last = None
                    for lc in range(LC):
                        last = v.tensor_copy(latnT[:, lc, i * 128:(i + 1) * 128],
                                             psB[:, 1536 + lc * 128:1536 + (lc + 1) * 128])
                    tv3 = p.inc(last, sdv, 1)
                    p.ps_release(["B3"], sdv, tv3)
                latn_v = sdv.n

                for grp in range(NG):
                    h0 = grp * HPG
                    for cc in range(HC):
                        c0 = (h0 + 2 * cc) * HD
                        bufq = cc % 2
                        tca = stage_cast(bf_a[:, bufq],
                                         wq_h[:, c0:c0 + 128].rearrange("(c p) m -> p c m", p=128),
                                         st_a[:], "st_a",
                                         g, "g", sgp, dst_name="bf_a%d" % bufq)
                        tcg = stage_cast(bf_g[:, bufq],
                                         wqsw_h[:, c0:c0 + 128].rearrange("(c p) m -> p c m", p=128),
                                         st_2[:].rearrange("p (c m) -> p c m", m=128), "st_2",
                                         g, "g", sgp, dst_name="bf_g%d" % bufq)
                        for par in range(2):
                            hh = h0 + 2 * cc + par
                            for s_ in range(NS):
                                mmacc(A[0][0:64, :],
                                      [(wukbf[:, lc, hh * HD:(hh + 1) * HD],
                                        latnT[:, lc, s_ * S:(s_ + 1) * S]) for lc in range(LC)],
                                      ["A0"], extra=[(sdv, latn_v)])
                                last = mmacc(A[1][0:64, :],
                                             [(wukswbf[:, lc, hh * HD:(hh + 1) * HD],
                                               latnT[:, lc, s_ * S:(s_ + 1) * S]) for lc in range(LC)],
                                             ["A1"])
                                tpe_k = p.inc(last, spe, 1)
                                mmacc(A[2][0:64, :],
                                      [(bf_a[:, bufq, c_, par * 64:par * 64 + 64],
                                        xnT[:, c_, s_ * S:(s_ + 1) * S]) for c_ in range(DC)],
                                      ["A2"], extra=[(sgp, max(tca, tcg))])
                                last = mmacc(A[3][0:64, :],
                                             [(bf_g[:, bufq, c_, par * 64:par * 64 + 64],
                                               xnT[:, c_, s_ * S:(s_ + 1) * S]) for c_ in range(DC)],
                                             ["A3"])
                                tpe_q = p.inc(last, spe, 1)
                                p.buf_use("bf_a%d" % bufq, spe, tpe_q)
                                p.buf_use("bf_g%d" % bufq, spe, tpe_q)
                                p.wait(v, "v", spe, tpe_k)
                                p.wait(v, "v", sld, b_ld)
                                p.buf_wait(v, "v", "kroT")
                                v.drain()
                                v.tensor_tensor(ropet[0:64, 0, :], A[0][0:64, :],
                                                cos2t[:, s_ * S:(s_ + 1) * S], op=ALU.mult)
                                v.tensor_tensor(ropet[0:64, 1, :], A[1][0:64, :],
                                                sin2st[:, s_ * S:(s_ + 1) * S], op=ALU.mult)
                                v.drain()
                                tv = p.inc(v.tensor_tensor(
                                    kroT[par * 64:par * 64 + 64, cc, s_ * S:(s_ + 1) * S],
                                    ropet[0:64, 0, :], ropet[0:64, 1, :], op=ALU.add), sdv, 1)
                                p.ps_release(["A0", "A1"], sdv, tv)
                                p.wait(v, "v", spe, tpe_q)
                                p.buf_wait(v, "v", "qroT")
                                v.drain()
                                v.tensor_tensor(ropet[0:64, 0, :], A[2][0:64, :],
                                                cos2t[:, s_ * S:(s_ + 1) * S], op=ALU.mult)
                                v.tensor_tensor(ropet[0:64, 1, :], A[3][0:64, :],
                                                sin2st[:, s_ * S:(s_ + 1) * S], op=ALU.mult)
                                v.drain()
                                tv = p.inc(v.tensor_tensor(
                                    qroT[par * 64:par * 64 + 64, cc, s_ * S:(s_ + 1) * S],
                                    ropet[0:64, 0, :], ropet[0:64, 1, :], op=ALU.add), sdv, 1)
                                p.ps_release(["A2", "A3"], sdv, tv)
                    kq_v = sdv.n

                    for i in range(TC):
                        p.buf_wait(v, "v", "v65")
                        last = mmacc(A[0][:, 0:VW],
                                     [(latnT[:, lc, i * 128:(i + 1) * 128],
                                       wuvbf[:, lc, h0 * HD:h0 * HD + VW]) for lc in range(LC)],
                                     ["A0"])
                        tpe = p.inc(last, spe, 1)
                        p.wait(v, "v", spe, tpe)
                        tv = p.inc(v.tensor_copy(v65[:, i, :, 0:64],
                                                 A[0][:, 0:VW].rearrange("p (h d) -> p h d", d=64)),
                                   sdv, 1)
                        p.ps_release(["A0"], sdv, tv)
                    v65_v = sdv.n

                    first = True
                    units = [(s_, par, cc) for s_ in range(NS) for par in range(2)
                             for cc in range(HC)]
                    for g0 in range(0, len(units), 4):
                        grp_u = units[g0:g0 + 4]
                        toks = []
                        for ui, (s_, par, cc) in enumerate(grp_u):
                            sb2 = ui % 2
                            p.buf_wait(sc, "sc", "attn%d" % sb2)
                            p.wait(sc, "sc", sdv, kq_v)
                            tq = p.inc(sc.activation(attnT[par * 64:par * 64 + 64, sb2, :],
                                                     kroT[par * 64:par * 64 + 64, cc,
                                                          s_ * S:(s_ + 1) * S],
                                                     AF.Square), sac, 1)
                            last = mmacc(Bp[ui][0:1, :],
                                         [(ones128[par * 64:par * 64 + 64, 0:1],
                                           attnT[par * 64:par * 64 + 64, sb2, :])],
                                         ["B%d" % ui], extra=[(sac, tq)])
                            tpe = p.inc(last, spe, 1)
                            p.buf_use("attn%d" % sb2, spe, tpe)
                        p.wait(v, "v", spe, spe.n)
                        v.reduce_max(tiny[0:1, 1:2], psB[0:1, 0:512 * len(grp_u)], axis=AX.X)
                        v.drain()
                        if first:
                            v.tensor_copy(tiny[0:1, 0:1], tiny[0:1, 1:2])
                            first = False
                        else:
                            v.tensor_tensor(tiny[0:1, 0:1], tiny[0:1, 0:1], tiny[0:1, 1:2],
                                            op=ALU.max)
                        tvm = p.inc(v.drain(), sdv, 1)
                        p.ps_release(["B0", "B1", "B2", "B3"], sdv, tvm)
                    mk_v = sdv.n

                    for s_ in range(NS):
                        units_q = [(par, cc) for par in range(2) for cc in range(HC)]
                        for g0 in range(0, len(units_q), 4):
                            grp_u = units_q[g0:g0 + 4]
                            for ui, (par, cc) in enumerate(grp_u):
                                sb2 = ui % 2
                                p.buf_wait(sc, "sc", "attn%d" % sb2)
                                tq = p.inc(sc.activation(attnT[par * 64:par * 64 + 64, sb2, :],
                                                         qroT[par * 64:par * 64 + 64, cc,
                                                              s_ * S:(s_ + 1) * S],
                                                         AF.Square), sac, 1)
                                last = mmacc(Bp[ui][0:1, :],
                                             [(ones128[par * 64:par * 64 + 64, 0:1],
                                               attnT[par * 64:par * 64 + 64, sb2, :])],
                                             ["B%d" % ui], extra=[(sac, tq)])
                                tpe = p.inc(last, spe, 1)
                                p.buf_use("attn%d" % sb2, spe, tpe)
                            p.wait(v, "v", spe, spe.n)
                            v.reduce_max(tiny[0:1, 2:3], psB[0:1, 0:512 * len(grp_u)], axis=AX.X)
                            v.drain()
                            if g0 == 0:
                                v.tensor_copy(tiny[0:1, 3:4], tiny[0:1, 2:3])
                            else:
                                v.tensor_tensor(tiny[0:1, 3:4], tiny[0:1, 3:4], tiny[0:1, 2:3],
                                                op=ALU.max)
                            tvq = p.inc(v.drain(), sdv, 1)
                            p.ps_release(["B0", "B1", "B2", "B3"], sdv, tvq)
                        p.wait(v, "v", sdv, mk_v)
                        tv = p.inc(v.tensor_tensor(tiny[0:1, 4:5], tiny[0:1, 3:4], tiny[0:1, 0:1],
                                                   op=ALU.mult), sdv, 1)
                        p.wait(sc, "sc", sdv, tv)
                        sc.activation(tiny[0:1, 5:6], tiny[0:1, 4:5], AF.Ln)
                        sc.drain()
                        ta = p.inc(sc.activation(tiny[0:1, 6:7], tiny[0:1, 5:6], AF.Exp, scale=0.5),
                                   sac, 1)
                        p.wait(v, "v", sac, ta)
                        v.tensor_scalar(tiny[0:1, 7:8], tiny[0:1, 6:7], -0.125, -2.0,
                                        op0=ALU.mult, op1=ALU.add)
                        v.drain()
                        tv = p.inc(v.tensor_copy(tinyb[0:1, 0:1], tiny[0:1, 7:8]), sdv, 1)
                        p.buf_wait(v, "v", "negc")
                        last = mmacc(psB[:, 1024:1025], [(ones128[0:1, :], tinyb[0:1, 0:1])],
                                     ["B2"], extra=[(sdv, tv)])
                        tpe = p.inc(last, spe, 1)
                        p.wait(v, "v", spe, tpe)
                        tnegc = p.inc(v.tensor_copy(negc[:, 0:1], psB[:, 1024:1025]), sdv, 1)
                        p.ps_release(["B2"], sdv, tnegc)

                        nkt = 4 * s_ + 4
                        for lh in range(HPG):
                            cc = lh // 2; par = lh % 2
                            ob = [0, 1, 3][lh % 3]
                            o65 = Bp[ob][0:65, :]
                            p.ps_acquire(t, ["B%d" % ob])
                            def emit_score(kt):
                                sb_ = kt % 2
                                p.ps_acquire(t, ["A%d" % sb_])
                                p.wait(t, "t", sdv, kq_v)
                                tmm = t.matmul(A[sb_],
                                               kroT[par * 64:par * 64 + 64, cc, kt * 128:(kt + 1) * 128],
                                               qroT[par * 64:par * 64 + 64, cc, s_ * S:(s_ + 1) * S],
                                               start=True, stop=True)
                                return p.inc(tmm, spe, 1)

                            tpe_s = emit_score(0)
                            for kt in range(nkt):
                                sb_ = kt % 2
                                stp = A[sb_]
                                abuf = attnT[:, sb_, :]
                                tpe = tpe_s
                                p.buf_wait(sc, "sc", "attn%d" % sb_)
                                p.wait(sc, "sc", sdv, tnegc)
                                p.wait(sc, "sc", spe, tpe)
                                tex = p.inc(sc.activation(abuf, stp, AF.Exp,
                                                          bias=negc[:, 0:1], scale=0.125), sac, 1)
                                p.buf_use("negc", sac, tex)
                                p.ps_release(["A%d" % sb_], sac, tex)
                                if kt + 1 < nkt:
                                    tpe_s = emit_score(kt + 1)
                                off = kt - 4 * s_
                                if off >= 0:
                                    p.wait(v, "v", sac, tex)
                                    tex2 = p.inc(v.tensor_tensor(abuf, abuf, masksb[:, off, :],
                                                                 op=ALU.mult), sdv, 1)
                                    p.wait(t, "t", sdv, tex2)
                                else:
                                    p.wait(t, "t", sac, tex)
                                p.wait(t, "t", sdv, v65_v)
                                last = t.matmul(o65, v65[:, kt, lh, :], abuf,
                                                start=(kt == 0), stop=(kt == nkt - 1))
                                tpe2 = p.inc(last, spe, 1)
                                p.buf_use("attn%d" % sb_, spe, tpe2)
                                p.buf_use("v65", spe, tpe2)
                            p.wait(sc, "sc", spe, spe.n)
                            p.buf_wait(sc, "sc", "recipb")
                            sc.activation(ropet[64:65, 0, :], o65[64:65, :], AF.Ln)
                            sc.drain()
                            trc = p.inc(sc.activation(recipb[64:65, :], ropet[64:65, 0, :],
                                                      AF.Exp, scale=-1.0), sac, 1)
                            last = mmacc(psB[0:64, 1024:1536],
                                         [(ones128[64:65, 0:64], recipb[64:65, :])],
                                         ["B2"], extra=[(sac, trc)])
                            tpe3 = p.inc(last, spe, 1)
                            p.buf_use("recipb", spe, tpe3)
                            p.wait(v, "v", spe, tpe3)
                            v.tensor_copy(bc_sb[:], psB[0:64, 1024:1536])
                            v.drain()
                            p.buf_wait(v, "v", "catT")
                            tv = p.inc(v.tensor_tensor(catT[par * 64:par * 64 + 64, cc, :],
                                                       o65[0:64, :], bc_sb[:], op=ALU.mult), sdv, 1)
                            p.ps_release(["B2"], sdv, tv)
                            p.ps_release(["B%d" % ob], sdv, tv)
                        cat_v = sdv.n

                        wo_tgt = [("bf_a0", bf_a[:, 0]), ("bf_g0", bf_g[:, 0])]
                        for ccw in range(CPG):
                            r0 = h0 * HD + ccw * 128
                            nm, tgt = wo_tgt[ccw % 2]
                            stage_cast(tgt.rearrange("p c m -> p (c m)"), wo_h[r0:r0 + 128, :],
                                       st_2[:], "st_2", g, "g", sgp, dst_name=nm)
                        wo_g = sgp.n
                        for dch in range(DC):
                            wsl = "A%d" % (2 + dch % 2)
                            wop = A[2 + dch % 2]
                            last = mmacc(wop,
                                         [(wo_tgt[ccw % 2][1][:, dch, :], catT[:, ccw, :])
                                          for ccw in range(CPG)],
                                         [wsl], extra=[(sgp, wo_g), (sdv, cat_v)])
                            tpe = p.inc(last, spe, 1)
                            p.buf_use("bf_a0", spe, tpe)
                            p.buf_use("bf_g0", spe, tpe)
                            p.buf_use("catT", spe, tpe)
                            p.wait(v, "v", spe, tpe)
                            dst = acc[:, dch, s_ * S:(s_ + 1) * S]
                            if grp == 0:
                                tv = p.inc(v.tensor_copy(dst, wop), sdv, 1)
                            else:
                                tv = p.inc(v.tensor_tensor(dst, dst, wop, op=ALU.add), sdv, 1)
                            p.ps_release([wsl], sdv, tv)
            barrier()

            with ExitStack() as cctx:
                def _cb(nm, shape, dty):
                    return cctx.enter_context(nc.sbuf_tensor("c%d_%s" % (b, nm), shape, dty))
                ubuf = _cb("ubuf", [128, UJG, CAPQ * S], BF16)
                w2g = _cb("w2g", [128, UJG, D], BF16)
                smb = _cb("smb", [128, 2, S], BF16)
                cumsb = _cb("cumsb", [8, S], F32)
                sidx = _cb("sidx", [128, 4], U32)
                tokvs = _cb("tokvs", [128, 4, 8], F32)
                tokw = _cb("tokw", [128, CAPQ, 4, 8], F32)
                tidx = _cb("tidx", [128, CAPQ, 4], U32)
                xgs = _cb("xgs", [128, D], F32)
                xgT = _cb("xgT", [128, CAPQ, DC, S], BF16)
                ySBq = _cb("ySBq", [128, DC, CAPQ * S], F32)
                ytok = _cb("ytok", [128, D], F32)
                slotb = ySBq[0:1, 0, 0:S]

                tok_sent = p.inc(sq.dma_start(out=tokf_h, in_=sent8_h), sld, 16)

                for s_ in ([] if DBG["skip_slot"] else range(NS)):
                    p.wait(v, "v", sdv, oh_v)
                    v.drain()
                    if s_ == 0:
                        v.tensor_tensor_scan(cumsb[:], ohT[:, s_ * S:(s_ + 1) * S],
                                             ohT[:, s_ * S:(s_ + 1) * S], 0.0,
                                             op0=ALU.add, op1=ALU.bypass)
                    else:
                        v.tensor_tensor_scan(cumsb[:], ohT[:, s_ * S:(s_ + 1) * S],
                                             ohT[:, s_ * S:(s_ + 1) * S], stat[0:8, 11:12],
                                             op0=ALU.add, op1=ALU.bypass)
                    v.drain()
                    v.tensor_copy(stat[0:8, 11:12], cumsb[:, S - 1:S])
                    v.drain()
                    v.tensor_scalar(cumsb[:], cumsb[:], etm1[0:8, :], None, op0=ALU.add)
                    v.drain()
                    tv = p.inc(v.tensor_tensor(cumsb[:], cumsb[:], ohT[:, s_ * S:(s_ + 1) * S],
                                               op=ALU.mult), sdv, 1)
                    last = mmacc(Bp[s_ % 4][0:1, :], [(onescol_f[0:8, :], cumsb[:])],
                                 ["B%d" % (s_ % 4)], extra=[(sdv, tv)])
                    tpe = p.inc(last, spe, 1)
                    p.wait(v, "v", spe, tpe)
                    p.buf_wait(v, "v", "slotb")
                    p.buf_wait(v, "v", "ySBq")
                    v.drain()
                    tv2 = p.inc(v.tensor_copy(slotb, Bp[s_ % 4][0:1, :]), sdv, 1)
                    p.buf_use("slotb", sdv, tv2)
                    p.ps_release(["B%d" % (s_ % 4)], sdv, tv2)
                    p.buf_wait(sq, "sq", "tokvs")
                    tokld = ld(tokvs[:], tokv_h[s_ * S:(s_ + 1) * S, :]
                               .rearrange("(c p) w -> p c w", p=128), sld2)
                    p.buf_wait(v, "v", "sidx")
                    for cpart in range(4):
                        last = mmacc(psA[:, 0:1], [(ySBq[0:1, 0, cpart * 128:(cpart + 1) * 128],
                                                    onescol_f[0:1, :])],
                                     ["A0"], extra=[(sdv, tv2)])
                        tpe2 = p.inc(last, spe, 1)
                        p.buf_use("slotb", spe, tpe2)
                        p.buf_use("ySBq", spe, tpe2)
                        p.wait(v, "v", spe, tpe2)
                        v.drain()
                        tv3 = p.inc(v.tensor_copy(sidx[:, cpart:cpart + 1], psA[:, 0:1]), sdv, 1)
                        p.ps_release(["A0"], sdv, tv3)
                    p.wait(g, "g", sld, tok_sent)
                    p.wait(g, "g", sld2, tokld)
                    p.wait(g, "g", sdv, sdv.n)
                    p.wait(g, "g", sxn, sxn.n)
                    for cpart in range(4):
                        if DBG["plain_tok"]:
                            g.dma_start(out=tokf_h[(s_ * 4 + cpart) * 128:(s_ * 4 + cpart + 1) * 128, :],
                                        in_=tokvs[:, cpart]).then_inc(sgd.h, 16)
                        else:
                            g.indirect_dma_start(
                                out=tokf_h,
                                out_offset=bass.IndirectOffsetOnAxis(ap=sidx[:, cpart:cpart + 1], axis=0),
                                in_=tokvs[:, cpart], in_offset=None).then_inc(sgd.h, 16)
                    tsc = p.inc(g.drain(), sgi, 1)
                    p.buf_use("tokvs", sgi, tsc)
                    p.buf_use("sidx", sgi, tsc)
                tokf_done = sgi.n

                last = mmacc(Bp[0][0:1, 0:8], [(stat[0:8, 11:12], identf[0:8, 0:8])],
                             ["B0"])
                tpe = p.inc(last, spe, 1)
                p.wait(v, "v", spe, tpe)
                v.drain()
                cnt_v = p.inc(v.tensor_copy(tinyi[:], Bp[0][0:1, 0:8]), sdv, 1)
                p.ps_release(["B0"], sdv, cnt_v)
                cregs = nc.alloc_registers("cnt%d" % b,
                                           engines=[mybir.EngineType.PE,
                                                    mybir.EngineType.DVE,
                                                    mybir.EngineType.Activation,
                                                    mybir.EngineType.Pool])

                for e in range(E):
                    for eng, en in ((t, "t"), (v, "v"), (sc, "sc"), (g, "g")):
                        p.wait(eng, en, sdv, cnt_v)
                    nc.regs_load(cregs, tinyi[0:1, e:e + 1])
                    p.wait(sq, "sq", sgi, tokf_done)
                    p.buf_wait(sq, "sq", "tokw")
                    tokld = ld(tokw[:, 0:CAPQ], tokf_h[e * T:e * T + CAPQ * S, :]
                               .rearrange("(c p) w -> p c w", p=128), sldx[e % 2])
                    p.wait(g, "g", sldx[e % 2], tokld)
                    p.wait(g, "g", sgi, tokf_done)
                    p.buf_wait(g, "g", "tidx")
                    tgi = p.inc(g.tensor_copy(tidx[:, 0:CAPQ], tokw[:, 0:CAPQ, :, 0]), sgp, 1)
                    p.buf_use("tokw", sgp, tgi)
                    def emit_gather(q):
                        for cpart in range(4):
                            p.buf_wait(g, "g", "xgs")
                            p.wait(g, "g", sxn, sxn.n)
                            p.wait(g, "g", sgp, tgi)
                            if DBG["plain_gather"]:
                                g.dma_start(out=xgs[:],
                                            in_=xnscr_h[cpart * 128:(cpart + 1) * 128, :]).then_inc(sgd.h, 16)
                            else:
                                g.indirect_dma_start(
                                    out=xgs[:], out_offset=None,
                                    in_=xnscr_h, in_offset=bass.IndirectOffsetOnAxis(
                                        ap=tidx[:, q, cpart:cpart + 1], axis=0)).then_inc(sgd.h, 16)
                            tgg = p.inc(g.drain(), sgi, 1)
                            p.buf_use("tidx", sgi, tgg)
                            for c0 in range(0, DC, 4):
                                cs = list(range(c0, min(c0 + 4, DC)))
                                p.ps_acquire(t, ["A0", "A1"])
                                p.wait(t, "t", sgi, tgg)
                                ti = None
                                for c in cs:
                                    ti = t.transpose(psA[:, (c % 4) * 256:(c % 4) * 256 + 128],
                                                     xgs[:, c * 128:(c + 1) * 128], identf[:])
                                tpe = p.inc(ti, spe, 1)
                                p.buf_use("xgs", spe, tpe)
                                p.wait(v, "v", spe, tpe)
                                p.buf_wait(v, "v", "xgT")
                                last = None
                                for c2 in cs:
                                    last = v.tensor_copy(
                                        xgT[:, q, c2, cpart * 128:(cpart + 1) * 128],
                                        psA[:, (c2 % 4) * 256:(c2 % 4) * 256 + 128])
                                tvx = p.inc(last, sdv, 1)
                                p.ps_release(["A0", "A1"], sdv, tvx)
                    emit_gather(0)
                    for q in range(1, CAPQ):
                        if DBG["cond_gather"]:
                            cond_region(cregs, q * 512, lambda q=q: emit_gather(q))
                        else:
                            emit_gather(q)
                    xg_v = sdv.n
                    p.buf_use("xgT", sdv, xg_v)

                    for jg in range(NJG):
                        pend = []
                        for j8 in range(UJG):
                            j = jg * UJG + j8
                            wbuf = j % 2
                            tca = stage_cast(bf_a[:, wbuf],
                                             w1_h[e, :, j * 128:(j + 1) * 128]
                                             .rearrange("(c p) m -> p c m", p=128),
                                             st_a[:], "st_a",
                                             sc, "sc", sac, dst_name="bf_a%d" % wbuf)
                            xn_ = "xcb%d" % wbuf
                            p.buf_wait(sq, "sq", xn_)
                            tg_ = ld(xcb[:, wbuf].rearrange("p (c m) -> p c m", m=128),
                                     w1_h[e, :, HID + j * 128:HID + (j + 1) * 128]
                                     .rearrange("(c p) m -> p c m", p=128), sldd[wbuf])
                            p.buf_wait(v, "v", "bf_g%d" % wbuf)
                            p.wait(v, "v", sldd[wbuf], tg_)
                            tcg = p.inc(v.tensor_copy(bf_g[:, wbuf].rearrange("p c m -> p (c m)"),
                                                      xcb[:, wbuf]), sdv, 1)
                            p.buf_use(xn_, sdv, tcg)
                            tc2 = stage_cast(w2g[:, j8, :], w2_h[e, j * 128:(j + 1) * 128, :],
                                             st_2[:], "st_2", g, "g", sgp, dst_name="w2g%d" % j8)
                            def emit_u(q, j=j, j8=j8, wbuf=wbuf, tca=tca, tcg=tcg):
                                if DBG["eq_nou"]:
                                    return
                                us = (j * CAPQ + q) % 2
                                apn = psA[:, us * 1024:us * 1024 + 512]
                                gpn = psA[:, us * 1024 + 512:us * 1024 + 1024]
                                sla = "A%d" % (us * 2); slg = "A%d" % (us * 2 + 1)
                                mmacc(apn, [(bf_a[:, wbuf, c, :], xgT[:, q, c, :])
                                            for c in range(DC)], [sla],
                                      extra=[(sac, tca), (sdv, xg_v)])
                                last = mmacc(gpn, [(bf_g[:, wbuf, c, :], xgT[:, q, c, :])
                                                   for c in range(DC)], [slg], extra=[(sdv, tcg)])
                                tpe = p.inc(last, spe, 1)
                                p.buf_use("bf_a%d" % wbuf, spe, tpe)
                                p.buf_use("bf_g%d" % wbuf, spe, tpe)
                                p.buf_use("xgT", spe, tpe)
                                p.buf_wait(sc, "sc", "smb%d" % us)
                                p.wait(sc, "sc", spe, tpe)
                                tsg = p.inc(sc.activation(smb[:, us, :], gpn, AF.Sigmoid,
                                                          bias=b1g[:, e, j:j + 1]), sac, 1)
                                p.wait(v, "v", sac, tsg)
                                p.buf_wait(v, "v", "ubuf")
                                tvu = p.inc(v.scalar_tensor_tensor(
                                    ubuf[:, j8, q * S:(q + 1) * S],
                                    apn, b1a[:, e, j:j + 1], smb[:, us, :],
                                    op0=ALU.add, op1=ALU.mult), sdv, 1)
                                p.buf_use("smb%d" % us, sdv, tvu)
                                p.ps_release([sla, slg], sdv, tvu)
                            emit_u(0)
                            for q in range(1, CAPQ):
                                if DBG["cond_u"]:
                                    pend.append((q, emit_u))
                                else:
                                    emit_u(q)
                            if DBG["cond_u"] and j8 % 2 == 1 and j8 != UJG - 1:
                                plist = pend
                                pend = []
                                for q in range(1, CAPQ):
                                    thr_ = DBG["cond_thr"] if DBG["cond_thr"] >= 0 else q * 512
                                    qs = [f for (qq, f) in plist if qq == q]
                                    if qs:
                                        cond_region(cregs, thr_,
                                                    lambda q=q, qs=qs: [f(q) for f in qs])
                        u_v = sdv.n
                        w2_ready = sgp.n
                        def emit_y(q, jg=jg, dchmax=99, do_mm=True, do_v=True, uv=None):
                            ndch = 0 if DBG["eq_noy"] else min(DC, dchmax)
                            for dch in range(ndch):
                                ys = dch % 2
                                yp = Bp[ys]
                                tpe = None
                                if do_mm:
                                    p.ps_acquire(t, ["B%d" % ys])
                                    p.wait(t, "t", sdv, uv if uv is not None else u_v)
                                    p.wait(t, "t", sgp, w2_ready)
                                    last = None
                                    njs = min(UJG, DBG["y_njs"]) if q else UJG
                                    for j8 in range(njs):
                                        last = t.matmul(yp, w2g[:, j8, dch * 128:(dch + 1) * 128],
                                                        ubuf[:, j8, q * S:(q + 1) * S],
                                                        start=(j8 == 0), stop=(j8 == njs - 1))
                                    tpe = p.inc(last, spe, 1)
                                    for j8 in range(UJG):
                                        p.buf_use("w2g%d" % j8, spe, tpe)
                                    p.buf_use("ubuf", spe, tpe)
                                if do_v:
                                    if tpe is not None:
                                        p.wait(v, "v", spe, tpe)
                                    p.buf_wait(v, "v", "ySBq")
                                    dst = ySBq[:, dch, q * S:(q + 1) * S]
                                    if jg == 0:
                                        tv = p.inc(v.tensor_scalar(dst, yp,
                                                                   b2pm[:, e, dch:dch + 1], None,
                                                                   op0=ALU.add), sdv, 1)
                                    else:
                                        tv = p.inc(v.tensor_tensor(dst, dst, yp, op=ALU.add), sdv, 1)
                                    p.ps_release(["B%d" % ys], sdv, tv)
                                elif tpe is not None:
                                    p.ps_release(["B%d" % ys], spe, tpe)
                        emit_y(0)
                        for q in range(1, CAPQ):
                            qs = [f for (qq, f) in pend if qq == q]
                            do_cy = (DBG["cond_y"] and jg < DBG["cond_y_jgmax"]
                                     and e < DBG["cond_y_emax"])
                            thr_ = DBG["cond_thr"] if DBG["cond_thr"] >= 0 else q * 512
                            if do_cy or qs:
                                def _body(q=q, qs=qs, do_cy=do_cy):
                                    for f in qs:
                                        f(q)
                                    if do_cy:
                                        emit_y(q, dchmax=DBG["y_dchmax"],
                                               do_mm=DBG["y_mm"], do_v=DBG["y_v"],
                                               uv=sdv.n)
                                cond_region(cregs, thr_, _body)
                                if not do_cy:
                                    emit_y(q)
                            else:
                                emit_y(q)
                        pend = []
                    y_v = sdv.n

                    def emit_flush(q):
                        if DBG["eq_noflush"]:
                            return
                        for cpart in range(4):
                            p.buf_wait(v, "v", "ytok")
                            p.ps_acquire(t, ["A0", "A1", "A2", "A3"])
                            p.wait(t, "t", sdv, y_v)
                            ti = None
                            for c in range(DC):
                                ti = t.transpose(psA[:, c * 256:c * 256 + 128],
                                                 ySBq[:, c, q * S + cpart * 128:q * S + (cpart + 1) * 128],
                                                 identf[:])
                            tpe = p.inc(ti, spe, 1)
                            p.buf_use("ySBq", spe, tpe)
                            p.wait(v, "v", spe, tpe)
                            last = None
                            for c in range(DC):
                                last = v.tensor_copy(ytok[:, c * 128:(c + 1) * 128],
                                                     psA[:, c * 256:c * 256 + 128])
                            tvy = p.inc(last, sdv, 1)
                            p.ps_release(["A0", "A1", "A2", "A3"], sdv, tvy)
                            p.wait(g, "g", sdv, tvy)
                            p.wait(g, "g", sgp, tgi)
                            if DBG["plain_yscat"]:
                                g.dma_start(out=yscr_h[cpart * 128:(cpart + 1) * 128, :],
                                            in_=ytok[:]).then_inc(sgd.h, 16)
                            else:
                                g.indirect_dma_start(
                                    out=yscr_h, out_offset=bass.IndirectOffsetOnAxis(
                                        ap=tidx[:, q, cpart:cpart + 1], axis=0),
                                    in_=ytok[:], in_offset=None).then_inc(sgd.h, 16)
                            tsy = p.inc(g.drain(), sgi, 1)
                            p.buf_use("ytok", sgi, tsy)
                            p.buf_use("tidx", sgi, tsy)
                    emit_flush(0)
                    for q in range(1, CAPQ):
                        if DBG["cond_flush"]:
                            cond_region(cregs, q * 512, lambda q=q: emit_flush(q))
                        else:
                            emit_flush(q)
                yscat_done = sgi.n
            acc_done = sdv.n
            barrier()

            with ExitStack() as dctx:
                xtD = dctx.enter_context(nc.sbuf_tensor("d%d_xt" % b, [128, 2, D], F32))
                ytD = dctx.enter_context(nc.sbuf_tensor("d%d_yt" % b, [128, 2, D], F32))
                for i in range(TC):
                    bi = i % 2
                    p.buf_wait(sq, "sq", "xcb%d" % bi)
                    p.buf_wait(sq, "sq", "xtD%d" % bi)
                    p.buf_wait(sq, "sq", "ytD%d" % bi)
                    tokx = ld(xtD[:, bi], x_h[b, i * 128:(i + 1) * 128, :], sldd[bi])
                    toky = ld(ytD[:, bi], yscr_h[i * 128:(i + 1) * 128, :], sldx[bi])
                    p.ps_acquire(t, ["A0", "A1", "A2", "A3"])
                    p.wait(t, "t", sdv, acc_done)
                    ti = None
                    for c in range(DC):
                        ti = t.transpose(psA[:, c * 256:c * 256 + 128],
                                         acc[:, c, i * 128:(i + 1) * 128], identf[:])
                    tpe = p.inc(ti, spe, 1)
                    p.wait(v, "v", spe, tpe)
                    p.wait(v, "v", sldd[bi], tokx)
                    p.wait(v, "v", sldx[bi], toky)
                    for c in range(DC):
                        v.tensor_tensor(xcb[:, bi, c * 128:(c + 1) * 128],
                                        psA[:, c * 256:c * 256 + 128],
                                        xtD[:, bi, c * 128:(c + 1) * 128], op=ALU.add)
                    v.drain()
                    tv = p.inc(v.tensor_tensor(xcb[:, bi], xcb[:, bi], ytD[:, bi],
                                               op=ALU.add), sdv, 1)
                    p.buf_use("xtD%d" % bi, sdv, tv)
                    p.buf_use("ytD%d" % bi, sdv, tv)
                    p.ps_release(["A0", "A1", "A2", "A3"], sdv, tv)
                    p.wait(sc, "sc", sdv, tv)
                    tst = p.inc(sc.dma_start(out=out_h[b, i * 128:(i + 1) * 128, :],
                                             in_=xcb[:, bi]), sstb[bi], 16)
                    p.buf_use("xcb%d" % bi, sstb[bi], tst)
            barrier()

        sc.wait_ge(sstb[0].h, sstb[0].n)
        sc.wait_ge(sstb[1].h, sstb[1].n)
        sq.wait_ge(sstb[0].h, sstb[0].n)
        sq.wait_ge(sstb[1].h, sstb[1].n)
    return nc



def kernel(**inputs):
    cfg = REAL
    shared = host_prep(cfg, inputs)
    x = np.ascontiguousarray(np.asarray(inputs["L0_attn_x"], dtype=np.float32))
    n_cores = 8
    per = x.shape[0] // n_cores
    nc = build_program(cfg)
    in_maps = []
    for c in range(n_cores):
        m = dict(shared)
        m["x"] = x[c * per:(c + 1) * per]
        in_maps.append(m)
    res = run_bass_kernel_spmd(nc, in_maps, core_ids=list(range(n_cores)))
    return np.concatenate([r["out"] for r in res.results], axis=0)
